# version 04
# baseline (speedup 1.0000x reference)
import numpy as np
import ml_dtypes
import concourse.bass as bass
import concourse.mybir as mybir
from concourse.bass_utils import run_bass_kernel_spmd

F32 = mybir.dt.float32
BF16 = mybir.dt.bfloat16
AF = mybir.ActivationFunctionType
ALU = mybir.AluOpType
AX = mybir.AxisListType
NBF = ml_dtypes.bfloat16

ENGS = ("pe", "dve", "act", "pool", "sp")
FCONST_VALS = (1e-5, 1e-6, -1.0, 1.0, 1.0 / 256, 0.125, 2.0, 0.5)
ENG_ATTR = {"pe": "tensor", "dve": "vector", "act": "scalar",
            "pool": "gpsimd", "sp": "sync"}
NQSEM = 8


class Op:
    __slots__ = ("fn", "after", "chan", "inc", "seq", "val")

    def __init__(self, fn, after, chan, inc):
        self.fn = fn
        self.after = after
        self.chan = chan
        self.inc = inc
        self.val = None


class Sched:

    def __init__(self):
        self.ops = {e: [] for e in ENGS}
        self.qcount = {"sp": 0, "pool": 0}
        self.prologue = []
        self.preprologue = set()

    def add(self, eng, fn, after=()):
        op = Op(fn, [t for t in after if t is not None], eng, 1)
        self.ops[eng].append(op)
        return op

    def dma(self, eng, fn, after=()):
        i = self.qcount[eng]
        self.qcount[eng] = i + 1
        chan = f"{eng}q{i % NQSEM}"
        op = Op(fn, [t for t in after if t is not None], chan, 16)
        self.ops[eng].append(op)
        return op

    def barrier(self, engines=ENGS, extra=()):
        marks = list(extra)
        for e in engines:
            if self.ops[e]:
                marks.append(self.ops[e][-1])
        out = []
        for e in engines:
            if self.ops[e]:
                out.append(self.add(e, lambda h: h.nop(nofuse=True), after=marks))
        return out

    def emit(self, nc, block, ctx):
        referenced = set()
        for t in self.prologue:
            referenced.add(id(t))
        for e in ENGS:
            for op in self.ops[e]:
                for t in op.after:
                    referenced.add(id(t))
        counters = {}
        for e in ENGS:
            for op in self.ops[e]:
                if id(op) in referenced or op.inc == 16:
                    counters[op.chan] = counters.get(op.chan, 0) + op.inc
                    op.val = counters[op.chan]
        sems = {c: ctx.enter_context(nc.semaphore(f"s_{c}")) for c in counters}

        def make_body(e):
            def body(h):
                watermark = {}
                for op in self.ops[e]:
                    pro = [] if id(op) in self.preprologue else list(self.prologue)
                    for t in pro + op.after:
                        assert t.val is not None, "dep on unreferenced op?"
                        if t.val > watermark.get(t.chan, 0):
                            h.wait_ge(sems[t.chan], t.val)
                            watermark[t.chan] = t.val
                    if op.inc == 16 and op.val is not None and op.val > 16 and \
                            op.val - 16 > watermark.get(op.chan, 0):
                        h.wait_ge(sems[op.chan], op.val - 16)
                        watermark[op.chan] = op.val - 16
                    try:
                        instr = op.fn(h)
                    except Exception:
                        import traceback as _tb
                        print(f"EMIT FAIL eng={e} idx={self.ops[e].index(op)}")
                        _tb.print_exc()
                        raise
                    if op.val is not None:
                        ii = instr[-1] if isinstance(instr, (list, tuple)) else instr
                        ii.then_inc(sems[op.chan], op.inc)
            return body

        for e in ENGS:
            if self.ops[e]:
                getattr(block, ENG_ATTR[e])(make_body(e))


from contextlib import ExitStack

T = 512
NST = 8
NCH = 8
NC4 = 4
NCORE = 8


def reorder(S):
    import heapq
    ECOST = {"pe": 0.25, "dve": 0.40, "act": 0.35, "pool": 1.0, "sp": 0.30}
    DMA_LAT = 8.0
    DMA_ISSUE = 0.15

    order = []
    engof = {}
    for eng in ENGS:
        for op in S.ops[eng]:
            order.append((eng, op))
            engof[id(op)] = eng
    pro = list(S.prologue)

    def deps_of(op):
        d = op.after if id(op) in S.preprologue else op.after + pro
        return d

    cost = {}
    for eng, op in order:
        cost[id(op)] = DMA_LAT if op.inc == 16 else ECOST[eng]

    height = {id(op): cost[id(op)] for _, op in order}
    for eng, op in reversed(order):
        h = height[id(op)]
        for d in deps_of(op):
            if height[id(d)] < h + cost[id(d)]:
                height[id(d)] = h + cost[id(d)]

    consumers = {id(op): [] for _, op in order}
    indeg = {}
    for seq, (eng, op) in enumerate(order):
        ds = {id(d): d for d in deps_of(op)}
        indeg[id(op)] = len(ds)
        for did, d in ds.items():
            consumers[did].append(op)

    seqno = {id(op): i for i, (_, op) in enumerate(order)}
    rt = {id(op): 0.0 for _, op in order}
    future = {e: [] for e in ENGS}
    avail = {e: [] for e in ENGS}
    free = {e: 0.0 for e in ENGS}
    for _, op in order:
        if indeg[id(op)] == 0:
            heapq.heappush(future[engof[id(op)]], (0.0, seqno[id(op)], op))
    newops = {e: [] for e in ENGS}
    n = len(order)
    fin = {}
    while n:
        best = None
        for eng in ENGS:
            fq, aq = future[eng], avail[eng]
            while fq and fq[0][0] <= free[eng]:
                rt_, sq_, op_ = heapq.heappop(fq)
                heapq.heappush(aq, (-height[id(op_)], sq_, op_))
            if aq:
                cand = free[eng]
            elif fq:
                cand = max(free[eng], fq[0][0])
            else:
                continue
            if best is None or cand < best[0]:
                best = (cand, eng)
        start, eng = best
        if not avail[eng]:
            free[eng] = future[eng][0][0]
            continue
        _, _, op = heapq.heappop(avail[eng])
        f = start + cost[id(op)]
        free[eng] = start + (DMA_ISSUE if op.inc == 16 else cost[id(op)])
        fin[id(op)] = f
        newops[eng].append(op)
        for c_ in consumers[id(op)]:
            cid = id(c_)
            if f > rt[cid]:
                rt[cid] = f
            indeg[cid] -= 1
            if indeg[cid] == 0:
                heapq.heappush(future[engof[cid]], (rt[cid], seqno[cid], c_))
        n -= 1
    for eng in ENGS:
        assert len(newops[eng]) == len(S.ops[eng])
        S.ops[eng] = newops[eng]
    return max(fin.values()) if fin else 0.0


class Bank:

    def __init__(self, ap):
        self.ap = ap
        self.readers = []
        self.writer = None

    def war(self, extra=()):
        d = list(self.readers) + list(extra)
        self.readers = []
        return d

    def read_dep(self):
        return [self.writer] if self.writer is not None else []


def pe_acc(S, bank, pairs, extra_after=()):
    n = len(pairs)
    war = bank.war(extra_after)
    t = None
    for i, (lh, rh, deps) in enumerate(pairs):
        aft = list(deps) + (war if i == 0 else [t])
        st = (i == 0)
        sp = (i == n - 1)
        if i == 0 and bank.writer is not None:
            aft.append(bank.writer)
        t = S.add("pe", (lambda lh=lh, rh=rh, st=st, sp=sp, out=bank.ap[:, :]:
                         (lambda h: h.matmul(out, lh, rh, start=st, stop=sp)))(),
                  after=aft)
    bank.writer = t
    return t


def pe_acc_sl(S, bank, out_ap, pairs, extra_after=()):
    n = len(pairs)
    war = bank.war(extra_after)
    t = None
    for i, (lh, rh, deps) in enumerate(pairs):
        aft = list(deps) + (war if i == 0 else [t])
        if i == 0 and bank.writer is not None:
            aft.append(bank.writer)
        st = (i == 0)
        sp = (i == n - 1)
        t = S.add("pe", (lambda lh=lh, rh=rh, st=st, sp=sp, out=out_ap:
                         (lambda h: h.matmul(out, lh, rh, start=st, stop=sp)))(),
                  after=aft)
    bank.writer = t
    return t


def rd(S, eng, bank, fn, extra=()):
    t = S.add(eng, fn, after=bank.read_dep() + list(extra))
    bank.readers.append(t)
    return t


def build(dbg=None):
    nc = bass.Bass("TRN2", target_bir_lowering=False)
    S = Sched()
    ctx = ExitStack()

    dram = lambda n, sh, dt, kind: nc.dram_tensor(n, sh, dt, kind=kind).ap()
    x_fm = dram("x_fm", [1024, 4096], F32, "ExternalInput")
    wT = {n: dram(n, [1024, 1024], F32, "ExternalInput")
          for n in ("wqT", "wkT", "wvT", "woT")}
    w1T = dram("w1T", [1088, 1024], F32, "ExternalInput")
    wbT = dram("wbT", [1024, 4], F32, "ExternalInput")
    w2gT = dram("w2gT", [1024, 16], F32, "ExternalInput")
    convw = {n: dram(n, [1024, 4], F32, "ExternalInput")
             for n in ("convq", "convk", "convv")}
    firS = dram("firS", [1024, 5], F32, "ExternalInput")
    firL = dram("firL", [1024, 63], F32, "ExternalInput")
    b1d = dram("b1", [128, 8], F32, "ExternalInput")
    b2g_rep = dram("b2g_rep", [128, 16], F32, "ExternalInput")
    lt_row = dram("lt_row", [1, 16], F32, "ExternalInput")
    fp_row = dram("fp_row", [1, 16], F32, "ExternalInput")
    onw = dram("onw", [128, 1024], F32, "ExternalInput")
    identD = dram("identc", [128, 128], BF16, "ExternalInput")
    ones1D = dram("ones1c", [1, 128], BF16, "ExternalInput")
    onescolD = dram("onescolc", [128, 1], BF16, "ExternalInput")
    ind1D = dram("ind1c", [1024, 4], BF16, "ExternalInput")
    mtriu1nD = dram("mtriu1n", [128, 128], BF16, "ExternalInput")
    mtril1nD = dram("mtril1n", [128, 128], BF16, "ExternalInput")
    mtriu0D = dram("mtriu0", [128, 128], BF16, "ExternalInput")
    fconstD = dram("fconsts", [128, 8], F32, "ExternalInput")
    colbD = dram("colbase", [1, 1], mybir.dt.int32, "ExternalInput")

    agin = dram("agin", [1024, 4096], F32, "Internal")
    agout = dram("agout", [8192, 4096], F32, "Internal")
    inpT = dram("inpT", [1024, 3072], F32, "ExternalInput")
    inpB = dram("inpB", [128, 24], F32, "ExternalInput")
    outwT = dram("outwT", [1024, 1024], F32, "ExternalInput")
    outB = dram("outB", [128, 8], F32, "ExternalInput")
    ln1g = dram("ln1g", [128, 8], F32, "ExternalInput")
    ln1b = dram("ln1b", [128, 8], F32, "ExternalInput")
    ln2g = dram("ln2g", [128, 8], F32, "ExternalInput")
    ln2b = dram("ln2b", [128, 8], F32, "ExternalInput")
    rwT = dram("rwT", [1024, 8], F32, "ExternalInput")
    rbrep = dram("rbrep", [128, 8], F32, "ExternalInput")
    w1pT = dram("w1pT", [8, 1024, 1024], F32, "ExternalInput")
    w2pT = dram("w2pT", [8, 1024, 1024], F32, "ExternalInput")
    b1p2 = dram("b1p2", [128, 8, 8], F32, "ExternalInput")
    b2p2 = dram("b2p2", [8, 1024], F32, "ExternalInput")
    blockmD = dram("blockm", [128, 128], BF16, "ExternalInput")
    repmD = dram("repm", [8, 128], BF16, "ExternalInput")
    onehotsD = dram("onehots", [8, 8, 128], BF16, "ExternalInput")
    idxcD = dram("idxc", [128, 8], F32, "ExternalInput")
    scr1 = dram("scr1", [8, 128, 8, 1024], BF16, "Internal")
    scr2 = dram("scr2", [8, 8, 128, 8, 128], BF16, "Internal")
    scrI = dram("scrI", [24, 128, 8, 128], BF16, "Internal")
    scrO = dram("scrO", [8, 128, 8, 128], BF16, "Internal")
    xloc = dram("xloc", [8192, 512], F32, "Internal")
    out_fm = dram("out_fm", [1024, 4096], F32, "ExternalOutput")

    dbg_out = {}
    if dbg == "qkv":
        for n in ("dq", "dk", "dv", "dxh"):
            dbg_out[n] = dram(n, [1024, 4096], BF16, "ExternalOutput")
    if dbg in ("x1", "x1s"):
        dbg_out["dx1"] = dram("dx1", [1024, 4096], F32, "ExternalOutput")
    if dbg == "x1s":
        for n in ("dps", "dpl"):
            dbg_out[n] = dram(n, [1024, 4096], BF16, "ExternalOutput")
        dbg_out["ddel"] = dram("ddel", [4096, 1024], BF16, "ExternalOutput")
        dbg_out["dsr"] = dram("dsr", [64, 4096], BF16, "ExternalOutput")
        dbg_out["dS0"] = dram("dS0", [128, 2048], F32, "ExternalOutput")
        dbg_out["dua"] = dram("dua", [128, 256], BF16, "ExternalOutput")
        dbg_out["dkh"] = dram("dkh", [128, 256], BF16, "ExternalOutput")
        dbg_out["dvb"] = dram("dvb", [128, 256], BF16, "ExternalOutput")
        dbg_out["dkraw"] = dram("dkraw", [128, 256], BF16, "ExternalOutput")
        dbg_out["dkh2"] = dram("dkh2", [128, 128], BF16, "ExternalOutput")
        dbg_out["dwt"] = dram("dwt", [4096, 16], F32, "ExternalOutput")

    ctx1 = ExitStack()
    sbg = lambda n, sh, dt: ctx.enter_context(nc.sbuf_tensor(n, sh, dt))
    sb = lambda n, sh, dt: ctx1.enter_context(nc.sbuf_tensor(n, sh, dt))
    ps = lambda n, sh, dt: ctx.enter_context(nc.psum_tensor(n, sh, dt))

    pb = [Bank(ps(f"pb{i}", [128, 512], F32)) for i in range(6)]
    pbb = [Bank(ps(f"pbb{i}", [128, 1024], BF16)) for i in range(2)]

    ident = sbg("ident", [128, 128], BF16)
    ones1 = sbg("ones1", [1, 128], BF16)
    onescol = sbg("onescol", [128, 1], BF16)
    mtriu1n = sbg("mtriu1n_s", [128, 128], BF16)
    mtril1n = sbg("mtril1n_s", [128, 128], BF16)
    mtriu0 = sbg("mtriu0_s", [128, 128], BF16)
    fconst = sbg("fconst", [128, 8], F32)
    colb = sbg("colb", [1, 1], mybir.dt.int32)
    for _i, _v in enumerate(FCONST_VALS):
        nc.const_aps.register(F32, _v, fconst[:, _i:_i + 1])
    ind1 = sb("ind1", [128, 8, 4], BF16)
    ctaps = {n: sb(f"ct_{n}", [128, 8, 4], F32) for n in ("convq", "convk", "convv")}
    ftapS = sb("ftapS", [128, 8, 5], F32)
    ftapL = sb("ftapL", [128, 8, 63], F32)
    b1sb = sb("b1sb", [128, 8], F32)
    b2gr = sb("b2gr", [128, 16], F32)
    onw16 = sb("onw16", [128, 1024], BF16)
    trow = sb("trow", [1, 16], F32)
    trow2 = sb("trow2", [1, 16], F32)
    trow3 = sb("trow3", [1, 16], F32)
    trow4 = sb("trow4", [1, 4], F32)
    trow4b = sb("trow4b", [1, 16], F32)
    trowb = sb("trowb", [1, 16], BF16)
    invtau_rep = sb("invtau_rep", [128, 16], F32)
    nf_rep = sb("nf_rep", [128, 16], F32)
    w1tail = [sb(f"w1tail{g}", [16, 1024], BF16) for g in range(4)]

    wslot = [sb(f"wslot{i}", [128, 8, 1024], BF16) for i in range(2)]
    wslot_cons = [[], []]

    xres = sb("xres", [128, 512], F32)
    xo16 = sb("xo16", [128, 8, 512], BF16)
    xhat = sb("xhat", [128, 8, 512], BF16)
    raw = sb("raw", [128, 8, 516], BF16)
    q16 = sb("q16", [128, 8, 512], BF16)
    k16 = sb("k16", [128, 8, 512], BF16)
    v16 = sb("v16", [128, 8, 576], BF16)
    v16o = sb("v16o", [128, 8, 576], BF16)
    ps16 = sb("ps16_", [128, 8, 512], BF16)
    pl16 = sb("pl16_", [128, 8, 512], BF16)
    gacc = sb("gacc", [128, 8, 512], BF16)
    delta16 = sb("delta16", [128, 1024], BF16)
    rawtail = {n: sb(f"rtail_{n}", [128, 8, 4], BF16) for n in ("q", "k", "v")}
    attnout = sb("attnout", [128, 8, 512], BF16)
    gp = sb("gp", [128, 7, 128], BF16)
    ap_ = sb("ap_", [128, 6, 128], BF16)
    msb = sb("msb", [128, 128], BF16)
    attnT = sb("attnT", [128, 128], BF16)
    khat = sb("khat", [128, 256], BF16)
    vb16 = sb("vb16", [128, 256], BF16)
    uadj = sb("uadj", [128, 256], BF16)
    S32 = sb("S32", [128, 8, 256], F32)
    S16 = sb("S16", [128, 8, 256], BF16)
    stm = sb("stm", [128, 16], F32)
    stm16 = sb("stm16", [128, 16], BF16)
    statrow = [sb(f"strow{p}", [16, 128], BF16) for p in range(4)]
    sqs = sb("sqs", [128, 256], F32)
    pt16 = sb("pt16", [128, 3, 1024], BF16)
    beta_tm = sb("beta_tm", [128, 4, 4], F32)
    rq_tm = sb("rq_tm", [128, 4, 4], F32)
    rk_tm = sb("rk_tm", [128, 4, 4], F32)
    nbrk_tm = sb("nbrk_tm", [128, 4, 4], F32)
    brk_tm = sb("brk_tm", [128, 4, 4], F32)
    row4a = sb("row4a", [1, 2048], F32)
    row4b = sb("row4b", [1, 2048], F32)
    rkrow16 = sb("rkrow16", [1, 2048], BF16)
    brkrow16 = sb("brkrow16", [1, 2048], BF16)
    wts = sb("wts", [128, 4, 16], F32)
    o16 = sb("o16", [128, 1024], BF16)
    ss4 = sb("ss4", [128, 4], F32)
    rstd4 = sb("rstd4", [128, 4], F32)
    lgt = sb("lgt", [128, 16], F32)
    r_s1 = row4a[0:1, 0:512]
    r_s2 = row4a[0:1, 512:1024]
    r_t1 = row4a[0:1, 1024:1536]
    r_t2 = row4a[0:1, 1536:2048]
    r_b1 = sb("r_b1", [1, 512], BF16)
    r_b2 = sb("r_b2", [1, 512], BF16)
    rstd0 = sb("rstd0", [128, 512], BF16)
    mrs0 = sb("mrs0", [128, 512], BF16)
    rstdA = sb("rstdA", [128, 512], BF16)
    mrsA = sb("mrsA", [128, 512], BF16)
    x1c = sb("x1c", [128, 512], F32)
    stg = [sb(f"stg{i}", [128, 1024], BF16) for i in range(1)]
    cacc = sb("cacc", [128, 512], BF16)
    env = dict(locals())
    return _build_rest(env)


def bc(ap3, shape):
    return ap3.to_broadcast(shape)


def _build_rest(E):
    from types import SimpleNamespace
    e = SimpleNamespace(**E)
    nc, S, ctx = e.nc, e.S, e.ctx
    dbg = e.dbg

    cl = []

    def ld(dst, src, eng="sp"):
        t = S.dma(eng, lambda h, d=dst, s=src: h.dma_start(out=d, in_=s))
        cl.append(t)
        return t

    ld(e.ident[:], e.identD)
    ld(e.ones1[:], e.ones1D)
    ld(e.onescol[:], e.onescolD)
    ld(e.ind1[:], e.ind1D.rearrange("(c p) f -> p c f", p=128))
    ld(e.mtriu1n[:], e.mtriu1nD)
    ld(e.mtril1n[:], e.mtril1nD)
    ld(e.mtriu0[:], e.mtriu0D)
    for n in ("convq", "convk", "convv"):
        ld(e.ctaps[n][:], e.convw[n].rearrange("(c p) f -> p c f", p=128))
    ld(e.ftapS[:], e.firS.rearrange("(c p) f -> p c f", p=128))
    ld(e.ftapL[:], e.firL.rearrange("(c p) f -> p c f", p=128))
    ld(e.b1sb[:], e.b1d)
    ld(e.b2gr[:], e.b2g_rep)
    ld(e.onw16[:], e.onw, eng="pool")
    wb16 = e.ctx1.enter_context(nc.sbuf_tensor("wb16", [128, 8, 4], BF16))
    w2g16 = e.ctx1.enter_context(nc.sbuf_tensor("w2g16", [128, 8, 16], BF16))
    ld(wb16[:], e.wbT.rearrange("(c p) f -> p c f", p=128), eng="pool")
    ld(w2g16[:], e.w2gT.rearrange("(c p) f -> p c f", p=128), eng="pool")
    for g in range(4):
        ld(e.w1tail[g][:], e.w1T[1024 + 16 * g:1024 + 16 * (g + 1), :], eng="pool")
    ld(e.fconst[:], e.fconstD)
    ld(e.colb[:], e.colbD)
    S.prologue = cl[:]
    for _e in ENGS:
        for _o in S.ops[_e]:
            S.preprologue.add(id(_o))
    tl = ld(e.trow[:], e.lt_row)
    tf = ld(e.trow2[:], e.fp_row)

    a1 = S.add("act", lambda h: h.activation(out=e.trow3[:], in_=e.trow[:], func=AF.Abs), after=[tl])
    a2 = S.add("act", lambda h: h.activation(out=e.trow4b[:], in_=e.trow3[:], func=AF.Exp, scale=-1.0), after=[a1])
    a3 = S.add("act", lambda h: h.activation(out=e.trow3[:], in_=e.trow4b[:], func=AF.Ln, bias=1.0), after=[a2])
    a4 = S.add("dve", lambda h: h.tensor_scalar_max(out=e.trow[:], in0=e.trow[:], scalar1=0.0), after=[tl, a1])
    a5 = S.add("dve", lambda h: h.tensor_add(out=e.trow[:], in0=e.trow[:], in1=e.trow3[:]), after=[a3, a4])
    a6 = S.add("dve", lambda h: h.tensor_scalar_add(out=e.trow[:], in0=e.trow[:], scalar1=0.25), after=[a5])
    a7 = S.add("dve", lambda h: h.reciprocal(out=e.trow[:], in_=e.trow[:]), after=[a6])
    a8 = S.add("dve", lambda h: h.tensor_copy(out=e.trowb[:], in_=e.trow[:]), after=[a7])
    b1_ = S.add("act", lambda h: h.activation(out=e.trow2[:], in_=e.trow2[:], func=AF.Sigmoid), after=[tf])
    b2_ = S.add("dve", lambda h: h.tensor_scalar(out=e.trow2[:], in0=e.trow2[:], scalar1=0.07, scalar2=0.005, op0=ALU.mult, op1=ALU.add), after=[b1_])
    b3_ = S.add("dve", lambda h: h.tensor_scalar_min(out=e.trow2[:], in0=e.trow2[:], scalar1=0.25), after=[b2_])
    b4_ = S.add("dve", lambda h: h.tensor_reduce(out=e.trow4[:], in_=e.trow2[:].rearrange("o (g f) -> o g f", f=4), axis=AX.X, op=ALU.add), after=[b3_])
    b5_ = S.add("dve", lambda h: h.tensor_scalar_min(out=e.trow4[:], in0=e.trow4[:], scalar1=0.99), after=[b4_])
    b6_ = S.add("dve", lambda h: h.reciprocal(out=e.trow4[:], in_=e.trow4[:]), after=[b5_])
    b7_ = S.add("dve", lambda h: h.tensor_tensor(out=e.trow2[:].rearrange("o (g f) -> o g f", f=4), in0=e.trow2[:].rearrange("o (g f) -> o g f", f=4), in1=bc(e.trow4[:].unsqueeze(2), (1, 4, 4)), op=ALU.mult), after=[b6_])
    b8_ = S.add("dve", lambda h: h.tensor_scalar(out=e.trow2[:], in0=e.trow2[:], scalar1=0.99, scalar2=1e-9, op0=ALU.mult, op1=ALU.add), after=[b7_])
    b9_ = S.add("dve", lambda h: h.tensor_copy(out=e.trow3[:], in_=e.trow2[:]), after=[b8_])
    nfb = e.ctx1.enter_context(nc.sbuf_tensor("nfb16", [1, 16], BF16))
    b10 = S.add("dve", lambda h: h.tensor_copy(out=nfb[:], in_=e.trow3[:]), after=[b9_])
    t1 = pe_acc_sl(S, e.pb[0], e.pb[0].ap[:, 0:16], [(e.ones1[:], e.trowb[:], [a8] + cl[:1])])
    r1 = rd(S, "act", e.pb[0], lambda h: h.copy(out=e.invtau_rep[:], in_=e.pb[0].ap[:, 0:16]))
    t2 = pe_acc_sl(S, e.pb[1], e.pb[1].ap[:, 0:16], [(e.ones1[:], nfb[:], [b10])])
    r2 = rd(S, "act", e.pb[1], lambda h: h.copy(out=e.nf_rep[:], in_=e.pb[1].ap[:, 0:16]))
    boot = [r1, r2] + cl

    bootq = []

    def cast_pair(src_f32, dst_bf16, stage_cols=1024):
        def mk(slot, src_f32=src_f32, dst_bf16=dst_bf16, stage_cols=stage_cols):
            lf = lambda h: h.dma_start(out=e.stg[slot][:, 0:stage_cols], in_=src_f32)
            shp = dst_bf16.shape
            src2 = e.stg[slot][:, 0:stage_cols]
            dst2 = dst_bf16
            if len(shp) == 3:
                src2 = src2.rearrange("p (d t) -> p d t", d=shp[0])
                dst2 = dst_bf16.rearrange("d p t -> p d t")
            sf = lambda h: h.dma_start(out=dst2, in_=src2)
            return lf, sf
        bootq.append(mk)

    for ee in range(8):
        for kcc in range(8):
            cast_pair(e.w1pT[ee, kcc * 128:(kcc + 1) * 128, :], e.scr1[ee, :, kcc, :])
    for ee in range(8):
        for kcc in range(8):
            cast_pair(e.w2pT[ee, kcc * 128:(kcc + 1) * 128, :],
                      e.scr2[ee, :, :, kcc, :].rearrange("d p t -> p d t") if False else e.scr2[ee, :, :, kcc, :])
    for kcc in range(8):
        for gg in range(3):
            cast_pair(e.inpT[kcc * 128:(kcc + 1) * 128, gg * 1024:(gg + 1) * 1024],
                      e.scrI[gg * 8:(gg + 1) * 8, :, kcc, :])
    for kcc in range(8):
        cast_pair(e.outwT[kcc * 128:(kcc + 1) * 128, :], e.scrO[:, :, kcc, :])

    boot_state = {"i": 0, "slot_prev": [None], "done": []}

    def drain_boot(n):
        for _ in range(n):
            i = boot_state["i"]
            if i >= len(bootq):
                return
            boot_state["i"] = i + 1
            slot = 0
            lf, sf = bootq[i](slot)
            prev = boot_state["slot_prev"][slot]
            lt = S.dma("pool", lf, after=([prev] if prev else []))
            st_ = S.dma("pool", sf, after=[lt])
            boot_state["slot_prev"][slot] = st_
            boot_state["done"].append(st_)

    x1_all = []

    pb, pbb = e.pb, e.pbb
    wslot_cons = [[], []]
    wload = [None, None]
    S16_last = {}

    def load_w(slot, srcT, nchunks=8):
        after = wslot_cons[slot][:]
        wslot_cons[slot] = []
        t = S.dma("pool", lambda h, d=e.wslot[slot][:, 0:nchunks, :],
                  s=srcT[0:128 * nchunks, :].rearrange("(c p) o -> p c o", p=128):
                  h.dma_start(out=d, in_=s), after=after)
        wload[slot] = t
        return t

    def use_w(slot):
        def dep(t):
            wslot_cons[slot].append(t)
        return wload[slot], dep

    NT = 512

    def ones_mm_rows(x16ap_chunks, bank, deps):
        pairs = [(e.onescol[:], x16ap_chunks[dc], deps) for dc in range(8)]
        return pe_acc_sl(S, bank, bank.ap[0:1, :], pairs)

    rows_done = [[]]

    def ln_rows(bank1, bank2, eps, outb1, outb2, extra=()):
        t1 = rd(S, "dve", bank1, lambda h: h.tensor_scalar_mul(out=e.r_t1, in0=bank1.ap[0:1, :], scalar1=1.0 / 1024), list(extra) + rows_done[0])
        t2 = S.add("act", lambda h: h.activation(out=e.r_t2, in_=e.r_t1, func=AF.Square), after=[t1])
        t3 = rd(S, "dve", bank2, lambda h: h.scalar_tensor_tensor(out=e.r_t2, in0=bank2.ap[0:1, :], scalar=1.0 / 1024, in1=e.r_t2, op0=ALU.mult, op1=ALU.subtract), [t2])
        t4 = S.add("act", lambda h, ep=eps: h.activation(out=e.r_t2, in_=e.r_t2, func=AF.Sqrt, bias=ep), after=[t3])
        t5 = S.add("dve", lambda h: h.reciprocal(out=e.r_t2, in_=e.r_t2), after=[t4])
        t6 = S.add("dve", lambda h: h.tensor_mul(out=e.r_t1, in0=e.r_t1, in1=e.r_t2), after=[t5])
        t7 = S.add("dve", lambda h: h.tensor_scalar_mul(out=e.r_t1, in0=e.r_t1, scalar1=-1.0), after=[t6])
        t8 = S.add("dve", lambda h: h.tensor_copy(out=e.r_b1[:], in_=e.r_t2), after=[t5])
        t9 = S.add("dve", lambda h: h.tensor_copy(out=e.r_b2[:], in_=e.r_t1), after=[t7])
        tb1 = pe_acc(S, pb[4], [(e.ones1[:], e.r_b1[:], [t8])])
        o1 = rd(S, "act", pb[4], lambda h: h.copy(out=outb1, in_=pb[4].ap[:]))
        tb2 = pe_acc(S, pb[5], [(e.ones1[:], e.r_b2[:], [t9])])
        o2 = rd(S, "act", pb[5], lambda h: h.copy(out=outb2, in_=pb[5].ap[:]))
        rows_done[0] = [o1, o2]
        return o1, o2

    x16v = e.xo16
    dbg_ts = []

    for s in range(NST):
        t0 = s * NT
        first_st = (s == 0)
        stdeps = [] if first_st else [bar_t]

        tx16 = S.dma("pool", lambda h, t0=t0: h.dma_start(
            out=x16v[:], in_=e.x_fm.rearrange("(c p) t -> p c t", p=128)[:, :, t0:t0 + NT]),
            after=boot + stdeps)
        if first_st:
            vh = S.add("dve", lambda h: h.memset(e.v16[:, :, 0:64], 0.0), after=stdeps)
        else:
            vh = S.add("dve", lambda h: h.tensor_copy(out=e.v16[:, :, 0:64], in_=e.v16[:, :, 512:576]), after=stdeps)

        sqt = [S.add("act", lambda h, dc=dc: h.activation(out=e.gacc[:, dc, :], in_=x16v[:, dc, :], func=AF.Square), after=[tx16] + stdeps) for dc in range(8)]
        s1t = ones_mm_rows([x16v[:, dc, :] for dc in range(8)], pb[0], [tx16])
        s2t = ones_mm_rows([e.gacc[:, dc, :] for dc in range(8)], pb[1], sqt)
        lr1, lr2 = ln_rows(pb[0], pb[1], 1e-5, e.rstd0[:], e.mrs0[:])
        xh_t = []
        for dc in range(8):
            u1 = S.add("dve", lambda h, dc=dc: h.tensor_mul(out=e.xhat[:, dc, :], in0=x16v[:, dc, :], in1=e.rstd0[:]), after=[lr1, tx16] + stdeps)
            u2 = S.add("dve", lambda h, dc=dc: h.tensor_add(out=e.xhat[:, dc, :], in0=e.xhat[:, dc, :], in1=e.mrs0[:]), after=[lr2, u1])
            xh_t.append(u2)

        if dbg == "qkv":
            dbg_ts.append(S.dma("sp", lambda h, t0=t0: h.dma_start(
                out=e.dbg_out["dxh"].rearrange("(c p) t -> p c t", p=128)[:, :, t0:t0 + NT],
                in_=e.xhat[:]), after=xh_t))

        def proj(name, wname, slot, outbuf, out_off, rawtail, ctap, prev_conv_ts):
            lw = load_w(slot, e.wT[wname])
            if first_st:
                hh = S.add("dve", lambda h: h.memset(e.raw[:, :, 0:4], 0.0), after=prev_conv_ts)
            else:
                hh = S.add("dve", lambda h, rt=rawtail: h.tensor_copy(out=e.raw[:, :, 0:4], in_=rt[:]), after=prev_conv_ts + stdeps)
            ev = []
            for dc in range(8):
                bank = pb[dc % 4]
                _, dep = use_w(slot)
                pairs = [(e.wslot[slot][:, kc, dc * 128:(dc + 1) * 128], e.xhat[:, kc, :], [lw, xh_t[kc]]) for kc in range(8)]
                mt = pe_acc(S, bank, pairs, extra_after=prev_conv_ts)
                dep(mt)
                ev.append(rd(S, "act", bank, lambda h, dc=dc, bap=bank.ap: h.copy(out=e.raw[:, dc, 4:516], in_=bap[:, 0:512])))
            st_ = S.add("dve", lambda h, rt=rawtail: h.tensor_copy(out=rt[:], in_=e.raw[:, :, 512:516]), after=ev + [hh])
            conv_ts = [st_]
            for dc in range(8):
                tt = S.add("dve", lambda h, dc=dc: h.tensor_scalar_mul(
                    out=e.cacc[:, 0:512], in0=e.raw[:, dc, 4:516], scalar1=ctap[:, dc, 3:4]),
                    after=[ev[dc], hh] + ([] if not conv_ts else [conv_ts[-1]]))
                for j in (1, 2, 3):
                    tt = S.add("dve", lambda h, dc=dc, j=j: h.scalar_tensor_tensor(
                        out=e.cacc[:, 0:512], in0=e.raw[:, dc, 4 - j:516 - j], scalar=ctap[:, dc, 3 - j:4 - j],
                        in1=e.cacc[:, 0:512], op0=ALU.mult, op1=ALU.add), after=[tt])
                si = S.add("act", lambda h, dc=dc, oo=out_off: h.activation(
                    out=outbuf[:, dc, oo:oo + 512], in_=e.cacc[:, 0:512], func=AF.Silu), after=[tt])
                conv_ts.append(si)
            return conv_ts[1:]

        qc_ts = proj("q", "wqT", 0, e.q16, 0, e.rawtail["q"], e.ctaps["convq"], [])
        kc_ts = proj("k", "wkT", 1, e.k16, 0, e.rawtail["k"], e.ctaps["convk"], qc_ts)
        vc_ts = proj("v", "wvT", 0, e.v16, 64, e.rawtail["v"], e.ctaps["convv"], kc_ts)
        bt_ts = []
        for tc in range(4):
            bank = pb[tc % 2]
            pairs = [(e.xhat[:, kc, tc * 128:(tc + 1) * 128], wb16[:, kc, :], [xh_t[kc]] + boot) for kc in range(8)]
            mt = pe_acc_sl(S, bank, bank.ap[:, 0:4], pairs)
            bt_ts.append(rd(S, "act", bank, lambda h, tc=tc, bap=bank.ap: h.activation(
                out=e.beta_tm[:, tc, :], in_=bap[:, 0:4], func=AF.Sigmoid)))

        def l2pass(src16, out_tm, conv_ts):
            sq_ts = [S.add("act", lambda h, dc=dc: h.activation(out=e.gacc[:, dc, :], in_=src16[:, dc, :], func=AF.Square), after=conv_ts) for dc in range(8)]
            outs = []
            for tc in range(4):
                bank = pb[2 + tc % 2]
                pairs = [(e.gacc[:, kc, tc * 128:(tc + 1) * 128], e.ind1[:, kc, :], [sq_ts[kc]]) for kc in range(8)]
                mt = pe_acc_sl(S, bank, bank.ap[:, 0:4], pairs)
                gacc_rd.append(mt)
                q1 = rd(S, "act", bank, lambda h, tc=tc, bap=bank.ap: h.activation(
                    out=out_tm[:, tc, :], in_=bap[:, 0:4], func=AF.Sqrt, bias=1e-6))
                q2 = S.add("dve", lambda h, tc=tc: h.reciprocal(out=out_tm[:, tc, :], in_=out_tm[:, tc, :]), after=[q1])
                outs.append(q2)
            return sq_ts, outs

        gacc_rd = [s2t]
        _, rq_ts = l2pass(e.q16, e.rq_tm, qc_ts)
        ksq_ts, rk_ts = l2pass(e.k16, e.rk_tm, kc_ts)
        ev_k, ev_b = [], []
        for hh_ in range(4):
            pairs = [(e.ind1[:, kc, hh_:hh_ + 1], e.gacc[:, kc, :], [ksq_ts[kc]]) for kc in range(8)]
            mt = pe_acc_sl(S, pb[4], pb[4].ap[0:1, :], pairs)
            gacc_rd.append(mt)
            ev_k.append(rd(S, "act", pb[4], lambda h, hh_=hh_: h.copy(
                out=e.row4a[0:1, hh_ * 512:(hh_ + 1) * 512], in_=pb[4].ap[0:1, 0:512]), rows_done[0]))
            pairs = [(wb16[:, kc, hh_:hh_ + 1], e.xhat[:, kc, :], [xh_t[kc]]) for kc in range(8)]
            mt = pe_acc_sl(S, pb[5], pb[5].ap[0:1, :], pairs)
            ev_b.append(rd(S, "act", pb[5], lambda h, hh_=hh_: h.activation(
                out=e.row4b[0:1, hh_ * 512:(hh_ + 1) * 512], in_=pb[5].ap[0:1, 0:512], func=AF.Sigmoid)))
        kr1 = S.add("act", lambda h: h.activation(out=e.row4a[:], in_=e.row4a[:], func=AF.Sqrt, bias=1e-6), after=ev_k)
        kr2 = S.add("dve", lambda h: h.reciprocal(out=e.row4a[:], in_=e.row4a[:]), after=[kr1])
        kr3 = S.add("dve", lambda h: h.tensor_copy(out=e.rkrow16[:], in_=e.row4a[:]), after=[kr2])
        br2 = S.add("dve", lambda h: h.tensor_mul(out=e.row4b[:], in0=e.row4b[:], in1=e.row4a[:]), after=ev_b + [kr2])
        br3 = S.add("dve", lambda h: h.tensor_copy(out=e.brkrow16[:], in_=e.row4b[:]), after=[br2])
        sc_ts = []
        for tc in range(4):
            c1 = S.add("dve", lambda h, tc=tc: h.tensor_mul(out=e.brk_tm[:, tc, :], in0=e.beta_tm[:, tc, :], in1=e.rk_tm[:, tc, :]), after=[bt_ts[tc], rk_ts[tc]])
            c2 = S.add("dve", lambda h, tc=tc: h.tensor_scalar_mul(out=e.nbrk_tm[:, tc, :], in0=e.brk_tm[:, tc, :], scalar1=-1.0), after=[c1])
            sc_ts.append(c2)

        vo_t = S.add("dve", lambda h: h.tensor_copy(out=e.v16o[:, :, 1:576], in_=e.v16[:, :, 0:575]), after=vc_ts + [vh])
        fir_ts = {"s": [], "l": []}
        for dc in range(8):
            tt = S.add("dve", lambda h, dc=dc: h.tensor_scalar_mul(
                out=e.ps16[:, dc, :], in0=e.v16[:, dc, 64:576], scalar1=e.ftapS[:, dc, 4:5]),
                after=vc_ts + [vh])
            for j in range(1, 5):
                src = e.v16 if j % 2 == 0 else e.v16o
                off = 64 - j if j % 2 == 0 else 65 - j
                tt = S.add("dve", lambda h, dc=dc, j=j, src=src, off=off: h.scalar_tensor_tensor(
                    out=e.ps16[:, dc, :], in0=src[:, dc, off:off + 512], scalar=e.ftapS[:, dc, 4 - j:5 - j],
                    in1=e.ps16[:, dc, :], op0=ALU.mult, op1=ALU.add), after=[tt, vo_t])
            fir_ts["s"].append(tt)
        lacc_prev = []
        for dc in range(8):
            gts = []
            for g in range(8):
                ntap = 8 if g < 7 else 7
                tt = None
                for jj in range(ntap):
                    j = g * 8 + jj
                    src = e.v16 if j % 2 == 0 else e.v16o
                    off = 64 - j if j % 2 == 0 else 65 - j
                    if jj == 0:
                        tt = S.add("dve", lambda h, dc=dc, g=g, j=j, src=src, off=off: h.tensor_scalar_mul(
                            out=e.gacc[:, g, :], in0=src[:, dc, off:off + 512], scalar1=e.ftapL[:, dc, 62 - j:63 - j]),
                            after=vc_ts + [vo_t] + lacc_prev + gacc_rd)
                    else:
                        tt = S.add("dve", lambda h, dc=dc, g=g, j=j, src=src, off=off: h.scalar_tensor_tensor(
                            out=e.gacc[:, g, :], in0=src[:, dc, off:off + 512], scalar=e.ftapL[:, dc, 62 - j:63 - j],
                            in1=e.gacc[:, g, :], op0=ALU.mult, op1=ALU.add), after=[tt])
                gts.append(tt)
            c1 = S.add("dve", lambda h: h.tensor_add(out=e.gacc[:, 0:4, :], in0=e.gacc[:, 0:4, :], in1=e.gacc[:, 4:8, :]), after=gts)
            c2 = S.add("dve", lambda h: h.tensor_add(out=e.gacc[:, 0:2, :], in0=e.gacc[:, 0:2, :], in1=e.gacc[:, 2:4, :]), after=[c1])
            c3 = S.add("dve", lambda h, dc=dc: h.tensor_add(out=e.pl16[:, dc, :], in0=e.gacc[:, 0, :], in1=e.gacc[:, 1, :]), after=[c2])
            lacc_prev = [c3]
            fir_ts["l"].append(c3)

        wstat_rd = []
        x16_rd = sqt + xh_t + [s1t]
        oevac_ts = []
        wstat_ts = []
        l1w = load_w(1, e.w1T)
        h1_ts = []
        scan_guard = []
        sqs_guard = []
        stm_guard = []
        prev_tail = []
        prev_ott = []
        prev_on = []
        prev_rm = []
        prev_lg = []
        stat_tt_prev = []
        for c in range(4):
            ccols = slice(c * 128, (c + 1) * 128)
            od_list = []
            for hh_ in range(4):
                FIRST = first_st and c == 0
                kq_dep = [qc_ts[2 * hh_], qc_ts[2 * hh_ + 1], kc_ts[2 * hh_], kc_ts[2 * hh_ + 1]]
                mM = pe_acc_sl(S, pb[0], pb[0].ap[:, 0:128],
                               [(e.k16[:, 2 * hh_ + kc, ccols], e.k16[:, 2 * hh_ + kc, ccols], kq_dep) for kc in (0, 1)])
                eM = rd(S, "act", pb[0], lambda h, bap=pb[0].ap: h.copy(out=e.msb[:], in_=bap[:, 0:128]), scan_guard)
                bb1 = pe_acc_sl(S, pb[2], pb[2].ap[:, 0:128],
                                [(e.ones1[:], e.brkrow16[0:1, hh_ * 512 + c * 128: hh_ * 512 + (c + 1) * 128], [br3])])
                g1 = rd(S, "dve", pb[2], lambda h, bap=pb[2].ap: h.tensor_mul(out=e.gp[:, 0, :], in0=e.msb[:], in1=bap[:, 0:128]), [eM] + scan_guard)
                g2 = S.add("dve", lambda h: h.tensor_mul(out=e.gp[:, 0, :], in0=e.gp[:, 0, :], in1=e.mtriu1n[:]), after=[g1])
                g3 = S.add("dve", lambda h, c=c, hh_=hh_: h.tensor_scalar_mul(out=e.gp[:, 0, :], in0=e.gp[:, 0, :], scalar1=e.rk_tm[:, c, hh_:hh_ + 1]), after=[g2, rk_ts[c]])
                bb2 = pe_acc_sl(S, pb[2], pb[2].ap[:, 0:128],
                                [(e.ones1[:], e.rkrow16[0:1, hh_ * 512 + c * 128: hh_ * 512 + (c + 1) * 128], [kr3])])
                a1_ = rd(S, "dve", pb[2], lambda h, bap=pb[2].ap: h.tensor_mul(out=e.ap_[:, 0, :], in0=e.msb[:], in1=bap[:, 0:128]), [eM] + scan_guard)
                a2_ = S.add("dve", lambda h: h.tensor_mul(out=e.ap_[:, 0, :], in0=e.ap_[:, 0, :], in1=e.mtril1n[:]), after=[a1_])
                a3_ = S.add("dve", lambda h, c=c, hh_=hh_: h.tensor_scalar_mul(out=e.ap_[:, 0, :], in0=e.ap_[:, 0, :], scalar1=e.brk_tm[:, c, hh_:hh_ + 1]), after=[a2_, sc_ts[c]])
                mN = pe_acc_sl(S, pb[1], pb[1].ap[:, 0:128],
                               [(e.k16[:, 2 * hh_ + kc, ccols], e.q16[:, 2 * hh_ + kc, ccols], kq_dep) for kc in (0, 1)])
                eN = rd(S, "act", pb[1], lambda h, bap=pb[1].ap: h.copy(out=e.msb[:], in_=bap[:, 0:128]), [g1, a1_])
                at1 = S.add("dve", lambda h: h.tensor_mul(out=e.attnT[:], in0=e.msb[:], in1=e.mtriu0[:]), after=[eN] + scan_guard)
                at2 = S.add("dve", lambda h, c=c, hh_=hh_: h.tensor_scalar_mul(out=e.attnT[:], in0=e.attnT[:], scalar1=e.rk_tm[:, c, hh_:hh_ + 1]), after=[at1, rk_ts[c]])
                gprev, aprev = g3, a3_
                for lv in range(1, 7):
                    mg = pe_acc_sl(S, pb[2], pb[2].ap[:, 0:128],
                                   [(e.ap_[:, lv - 1, :], e.gp[:, lv - 1, :], [gprev, aprev])])
                    eg = rd(S, "act", pb[2], lambda h, lv=lv, bap=pb[2].ap: h.copy(out=e.gp[:, lv, :], in_=bap[:, 0:128]))
                    if lv < 6:
                        ma = pe_acc_sl(S, pb[3], pb[3].ap[:, 0:128],
                                       [(e.gp[:, lv - 1, :], e.ap_[:, lv - 1, :], [gprev, aprev])])
                        ea = rd(S, "act", pb[3], lambda h, lv=lv, bap=pb[3].ap: h.copy(out=e.ap_[:, lv, :], in_=bap[:, 0:128]))
                        aprev = ea
                    gprev = eg
                gchain = gprev
                tr_ts = []
                for kc in (0, 1):
                    bankb = pbb[kc]
                    tv = S.add("pe", lambda h, kc=kc, hh_=hh_, c=c, bap=bankb.ap: h.transpose(
                        bap[:, 128:256], e.v16[:, 2 * hh_ + kc, 64 + c * 128:64 + (c + 1) * 128], e.ident[:]),
                        after=vc_ts + bankb.war() + cl[:1] + scan_guard)
                    bankb.writer = tv
                    ev_ = rd(S, "act", bankb, lambda h, kc=kc, c=c, hh_=hh_, bap=bankb.ap: h.activation(
                        out=e.vb16[:, kc * 128:(kc + 1) * 128], in_=bap[:, 128:256], func=AF.Copy,
                        scale=e.beta_tm[:, c, hh_:hh_ + 1]), [bt_ts[c]])
                    tk = S.add("pe", lambda h, kc=kc, hh_=hh_, c=c, bap=bankb.ap: h.transpose(
                        bap[:, 0:128], e.k16[:, 2 * hh_ + kc, c * 128:(c + 1) * 128], e.ident[:]),
                        after=kc_ts + bankb.war() + cl[:1] + scan_guard)
                    bankb.writer = tk
                    ek_ = rd(S, "act", bankb, lambda h, kc=kc, c=c, hh_=hh_, bap=bankb.ap: h.activation(
                        out=e.khat[:, kc * 128:(kc + 1) * 128], in_=bap[:, 0:128], func=AF.Copy,
                        scale=e.rk_tm[:, c, hh_:hh_ + 1]), [rk_ts[c]])
                    if dbg == "x1s" and first_st and c == 0 and hh_ == 0 and kc == 0:
                        ex2 = rd(S, "act", bankb, lambda h, bap=bankb.ap: h.copy(
                            out=e.o16[:, 0:128], in_=bap[:, 0:128]))
                        dbg_ts.append(S.dma("sp", lambda h: h.dma_start(
                            out=e.dbg_out["dkh2"][:], in_=e.o16[:, 0:128]), after=[ex2]))
                        tr_ts += [dbg_ts[-1]]
                    tr_ts += [ev_, ek_]
                if FIRST:
                    yt = S.add("dve", lambda h: h.tensor_copy(out=e.uadj[:], in_=e.vb16[:]), after=tr_ts + scan_guard)
                else:
                    sdep = [S16_last[(hh_, 0)], S16_last[(hh_, 1)]]
                    mz = pe_acc_sl(S, pb[4], pb[4].ap[:, 0:256],
                                   [(e.k16[:, 2 * hh_ + kc, ccols], e.S16[:, 2 * hh_ + kc, :], kq_dep + sdep) for kc in (0, 1)])
                    yt = rd(S, "dve", pb[4], lambda h, c=c, hh_=hh_, bap=pb[4].ap: h.scalar_tensor_tensor(
                        out=e.uadj[:], in0=bap[:, 0:256], scalar=e.nbrk_tm[:, c, hh_:hh_ + 1],
                        in1=e.vb16[:], op0=ALU.mult, op1=ALU.add), tr_ts + [sc_ts[c]] + scan_guard)
                owar = pb[5].war()
                if pb[5].writer is not None:
                    owar.append(pb[5].writer)
                sreaders = []
                oprev = None
                if not FIRST:
                    sdep = [S16_last[(hh_, 0)], S16_last[(hh_, 1)]]
                    m1 = S.add("pe", lambda h, hh_=hh_, c=c: h.matmul(
                        pb[5].ap[:, 0:256], e.q16[:, 2 * hh_ + 0, c * 128:(c + 1) * 128],
                        e.S16[:, 2 * hh_ + 0, :], start=True, stop=False),
                        after=kq_dep + sdep + owar)
                    m2 = S.add("pe", lambda h, hh_=hh_, c=c: h.matmul(
                        pb[5].ap[:, 0:256], e.q16[:, 2 * hh_ + 1, c * 128:(c + 1) * 128],
                        e.S16[:, 2 * hh_ + 1, :], start=False, stop=False), after=[m1])
                    sreaders += [m1, m2]
                    oprev = m2
                for lv in range(7):
                    mapp = pe_acc_sl(S, pb[4], pb[4].ap[:, 0:256],
                                     [(e.gp[:, lv, :], e.uadj[:], [yt, gchain])])
                    yt = rd(S, "dve", pb[4], lambda h, bap=pb[4].ap: h.tensor_add(
                        out=e.uadj[:], in0=bap[:, 0:256], in1=e.uadj[:]))
                mo = S.add("pe", lambda h, FIRST=FIRST: h.matmul(
                    pb[5].ap[:, 0:256], e.attnT[:], e.uadj[:], start=FIRST, stop=True),
                    after=[at2, yt] + ([oprev] if oprev is not None else owar))
                pb[5].writer = mo
                od = rd(S, "act", pb[5], lambda h, c=c, hh_=hh_, bap=pb[5].ap: h.activation(
                    out=e.delta16[:, hh_ * 256:(hh_ + 1) * 256], in_=bap[:, 0:256],
                    func=AF.Copy, scale=e.rq_tm[:, c, hh_:hh_ + 1]), [rq_ts[c]] + prev_tail)
                od_list.append(od)
                scan_guard = [od, at2]
                if not FIRST:
                    sreaders.append(mz)
                s16ts = []
                for kc in (0, 1):
                    bank = pb[2 + kc]
                    mds = pe_acc_sl(S, bank, bank.ap[:, 0:256],
                                    [(e.khat[:, kc * 128:(kc + 1) * 128], e.uadj[:], [yt] + tr_ts)])
                    if FIRST:
                        evs = rd(S, "dve", bank, lambda h, hh_=hh_, kc=kc, bap=bank.ap: h.tensor_copy(
                            out=e.S32[:, 2 * hh_ + kc, :], in_=bap[:, 0:256]))
                    else:
                        evs = rd(S, "dve", bank, lambda h, hh_=hh_, kc=kc, bap=bank.ap: h.tensor_add(
                            out=e.S32[:, 2 * hh_ + kc, :], in0=bap[:, 0:256], in1=e.S32[:, 2 * hh_ + kc, :]),
                            [S16_last.get((hh_, kc))] if (hh_, kc) in S16_last else [])
                    s16t = S.add("dve", lambda h, hh_=hh_, kc=kc: h.tensor_copy(
                        out=e.S16[:, 2 * hh_ + kc, :], in_=e.S32[:, 2 * hh_ + kc, :]),
                        after=[evs] + sreaders)
                    S16_last[(hh_, kc)] = s16t
                    s16ts.append(s16t)
                scan_guard = [od] + s16ts
                if dbg == "x1s" and first_st and c == 0 and hh_ == 0:
                    dbg_ts.append(S.dma("sp", lambda h: h.dma_start(
                        out=e.dbg_out["dua"][:], in_=e.uadj[:]), after=[yt]))
                    dbg_ts.append(S.dma("sp", lambda h: h.dma_start(
                        out=e.dbg_out["dkh"][:], in_=e.khat[:]), after=tr_ts))
                    dbg_ts.append(S.dma("sp", lambda h: h.dma_start(
                        out=e.dbg_out["dvb"][:], in_=e.vb16[:]), after=tr_ts))
                    dbg_ts.append(S.dma("sp", lambda h: h.dma_start(
                        out=e.dbg_out["dkraw"][:].rearrange("p (a t) -> p a t", a=2), in_=e.k16[:, 0:2, 0:128]), after=kc_ts))
                    scan_guard = scan_guard + dbg_ts[-4:]

            pt_ts = []
            for pth, (srcbuf, soff, deps) in enumerate(
                    ((e.ps16, 0, fir_ts["s"]), (e.pl16, 0, fir_ts["l"]), (e.v16, 64, vc_ts))):
                for cc in range(8):
                    bankb = pbb[cc % 2]
                    tt = S.add("pe", lambda h, srcbuf=srcbuf, soff=soff, cc=cc, c=c, bap=bankb.ap: h.transpose(
                        bap[:, 0:128], srcbuf[:, cc, soff + c * 128:soff + (c + 1) * 128], e.ident[:]),
                        after=list(deps) + bankb.war() + cl[:1])
                    bankb.writer = tt
                    pt_ts.append(rd(S, "act", bankb, lambda h, pth=pth, cc=cc, bap=bankb.ap: h.copy(
                        out=e.pt16[:, pth, cc * 128:(cc + 1) * 128], in_=bap[:, 0:128]), prev_tail))

            for pth in range(4):
                if pth == 3:
                    src = e.pt16[:, 2, :]
                    sdeps = pt_ts[16:24]
                elif pth == 2:
                    src = e.delta16[:]
                    sdeps = [od]
                else:
                    src = e.pt16[:, pth, :]
                    sdeps = pt_ts[pth * 8:(pth + 1) * 8]
                st_ts = []
                for hh_ in range(4):
                    sl_ = src[:, hh_ * 256:(hh_ + 1) * 256]
                    r1_ = S.add("dve", lambda h, sl_=sl_, hh_=hh_: h.tensor_reduce(
                        out=e.stm[:, 4 * hh_:4 * hh_ + 1], in_=sl_, axis=AX.X, op=ALU.add), after=sdeps + stm_guard)
                    r2_ = S.add("act", lambda h, sl_=sl_, hh_=hh_: h.activation(
                        out=e.sqs[:], in_=sl_, func=AF.Square, accum_out=e.stm[:, 4 * hh_ + 1:4 * hh_ + 2]), after=sdeps + sqs_guard + stm_guard)
                    sqs_guard = [r2_]
                    r3_ = S.add("act", lambda h, sl_=sl_, hh_=hh_: h.activation(
                        out=e.sqs[:], in_=sl_, func=AF.Abs, accum_out=e.stm[:, 4 * hh_ + 2:4 * hh_ + 3]), after=sdeps + [r2_])
                    sqs_guard = [r3_]
                    st_ts += [r1_, r2_, r3_]
                f1 = S.add("dve", lambda h: h.tensor_scalar_mul(
                    out=e.stm[:].rearrange("p (g f) -> p f g", f=4)[:, 0, :],
                    in0=e.stm[:].rearrange("p (g f) -> p f g", f=4)[:, 0, :], scalar1=1.0 / 256), after=st_ts)
                f2 = S.add("act", lambda h: h.activation(
                    out=e.stm[:].rearrange("p (g f) -> p f g", f=4)[:, 3, :],
                    in_=e.stm[:].rearrange("p (g f) -> p f g", f=4)[:, 1, :], func=AF.Sqrt), after=st_ts)
                f3 = S.add("act", lambda h: h.activation(
                    out=e.sqs[:, 0:4], in_=e.stm[:].rearrange("p (g f) -> p f g", f=4)[:, 0, :],
                    func=AF.Square), after=[f1] + sqs_guard)
                sqs_guard = [f3]
                f4 = S.add("dve", lambda h: h.scalar_tensor_tensor(
                    out=e.stm[:].rearrange("p (g f) -> p f g", f=4)[:, 1, :],
                    in0=e.stm[:].rearrange("p (g f) -> p f g", f=4)[:, 1, :], scalar=1.0 / 256,
                    in1=e.sqs[:, 0:4], op0=ALU.mult, op1=ALU.subtract), after=[f2, f3])
                f5 = S.add("dve", lambda h: h.tensor_scalar_mul(
                    out=e.stm[:].rearrange("p (g f) -> p f g", f=4)[:, 2, :],
                    in0=e.stm[:].rearrange("p (g f) -> p f g", f=4)[:, 2, :], scalar1=1.0 / 256), after=st_ts)
                f6 = S.add("dve", lambda h: h.tensor_copy(out=e.stm16[:], in_=e.stm[:]), after=[f1, f2, f4, f5] + stat_tt_prev)
                sqs_guard = [f3, f4]
                stm_guard = [f6]
                bankb = pbb[pth % 2]
                tt = S.add("pe", lambda h, bap=bankb.ap: h.transpose(
                    bap[0:16, 0:128], e.stm16[:], e.ident[:]), after=[f6] + bankb.war() + cl[:1])
                bankb.writer = tt
                stat_tt_prev = [tt]
                wstat_ts.append(rd(S, "act", bankb, lambda h, pth=pth, bap=bankb.ap: h.copy(
                    out=e.statrow[pth][:], in_=bap[0:16, 0:128]), wstat_rd))

            wstat_rd = wstat_rd[-8:]
            g_ts = []
            for dc in range(8):
                bank = pb[dc % 2]
                _, dep = use_w(1)
                pairs = [(e.wslot[1][:, kc, dc * 128:(dc + 1) * 128], e.xhat[:, kc, c * 128:(c + 1) * 128], [l1w, xh_t[kc]]) for kc in range(8)]
                pairs += [(e.w1tail[g][:, dc * 128:(dc + 1) * 128], e.statrow[g][:], [wstat_ts[-4 + g]] + boot) for g in range(4)]
                mt = pe_acc_sl(S, bank, bank.ap[:, 0:128], pairs)
                dep(mt)
                wstat_rd.append(mt)
                g_ts.append(rd(S, "act", bank, lambda h, dc=dc, c=c, bap=bank.ap: h.activation(
                    out=e.raw[:, dc, c * 128:(c + 1) * 128], in_=bap[:, 0:128], func=AF.Gelu,
                    bias=e.b1sb[:, dc:dc + 1])))
            h1_ts += g_ts
            pairs = [(e.raw[:, kc, c * 128:(c + 1) * 128], w2g16[:, kc, :], [g_ts[kc]]) for kc in range(8)]
            mt = pe_acc_sl(S, pb[2], pb[2].ap[:, 0:16], pairs)
            lg1 = rd(S, "dve", pb[2], lambda h, bap=pb[2].ap: h.tensor_add(out=e.lgt[:], in0=bap[:, 0:16], in1=e.b2gr[:]), boot + prev_lg)
            lg2 = S.add("dve", lambda h: h.tensor_mul(out=e.lgt[:], in0=e.lgt[:], in1=e.invtau_rep[:]), after=[lg1])
            lg3 = S.add("dve", lambda h: h.tensor_reduce(out=e.ss4[:], in_=e.lgt[:].rearrange("p (g f) -> p g f", f=4), axis=AX.X, op=ALU.max), after=[lg2] + prev_rm)
            lg4 = S.add("dve", lambda h: h.tensor_tensor(
                out=e.lgt[:].rearrange("p (g f) -> p g f", f=4), in0=e.lgt[:].rearrange("p (g f) -> p g f", f=4),
                in1=bc(e.ss4[:].unsqueeze(2), (128, 4, 4)), op=ALU.subtract), after=[lg3])
            lg5 = S.add("act", lambda h: h.activation(out=e.lgt[:], in_=e.lgt[:], func=AF.Exp), after=[lg4])
            lg6 = S.add("dve", lambda h: h.tensor_reduce(out=e.ss4[:], in_=e.lgt[:].rearrange("p (g f) -> p g f", f=4), axis=AX.X, op=ALU.add), after=[lg5])
            lg7 = S.add("dve", lambda h: h.reciprocal(out=e.ss4[:], in_=e.ss4[:]), after=[lg6])
            lg8 = S.add("dve", lambda h: h.tensor_tensor(
                out=e.lgt[:].rearrange("p (g f) -> p g f", f=4), in0=e.lgt[:].rearrange("p (g f) -> p g f", f=4),
                in1=bc(e.ss4[:].unsqueeze(2), (128, 4, 4)), op=ALU.mult), after=[lg7])
            lg9 = S.add("dve", lambda h: h.tensor_max(out=e.lgt[:], in0=e.lgt[:], in1=e.nf_rep[:]), after=[lg8])
            lg10 = S.add("dve", lambda h: h.tensor_reduce(out=e.ss4[:], in_=e.lgt[:].rearrange("p (g f) -> p g f", f=4), axis=AX.X, op=ALU.add), after=[lg9])
            lg11 = S.add("dve", lambda h: h.tensor_scalar_add(out=e.ss4[:], in0=e.ss4[:], scalar1=1e-8), after=[lg10])
            lg12 = S.add("dve", lambda h: h.reciprocal(out=e.ss4[:], in_=e.ss4[:]), after=[lg11])
            lg13 = S.add("dve", lambda h, c=c: h.tensor_tensor(
                out=e.wts[:, c, :].rearrange("p (g f) -> p g f", f=4), in0=e.lgt[:].rearrange("p (g f) -> p g f", f=4),
                in1=bc(e.ss4[:].unsqueeze(2), (128, 4, 4)), op=ALU.mult), after=[lg12])

            bl_ts = []
            for hh_ in range(4):
                hsl = slice(hh_ * 256, (hh_ + 1) * 256)
                t1_ = S.add("dve", lambda h, hh_=hh_, c=c: h.tensor_scalar_mul(
                    out=e.o16[:, hh_ * 256:(hh_ + 1) * 256], in0=e.pt16[:, 0, hh_ * 256:(hh_ + 1) * 256],
                    scalar1=e.wts[:, c, 4 * hh_:4 * hh_ + 1]), after=[lg13] + pt_ts[0:8] + prev_ott)
                for pi, (srcap,) in enumerate(((e.pt16[:, 1, hsl],), (e.delta16[:, hsl],), (e.pt16[:, 2, hsl],))):
                    deps = [t1_] + (pt_ts[8:16] if pi == 0 else ([od] if pi == 1 else pt_ts[16:24]))
                    t1_ = S.add("dve", lambda h, srcap=srcap, hh_=hh_, c=c, pi=pi: h.scalar_tensor_tensor(
                        out=e.o16[:, hh_ * 256:(hh_ + 1) * 256], in0=srcap,
                        scalar=e.wts[:, c, 4 * hh_ + 1 + pi:4 * hh_ + 2 + pi],
                        in1=e.o16[:, hh_ * 256:(hh_ + 1) * 256], op0=ALU.mult, op1=ALU.add), after=deps)
                r_ = S.add("act", lambda h, hh_=hh_: h.activation(
                    out=e.sqs[:], in_=e.o16[:, hh_ * 256:(hh_ + 1) * 256], func=AF.Square,
                    accum_out=e.ss4[:, hh_:hh_ + 1]), after=[t1_] + sqs_guard)
                sqs_guard = [r_]
                bl_ts.append((t1_, r_))
            rm1 = S.add("act", lambda h: h.activation(out=e.rstd4[:], in_=e.ss4[:], func=AF.Sqrt, scale=1.0 / 256, bias=1e-5), after=[r for _, r in bl_ts] + [lg13] + prev_on)
            rm2 = S.add("dve", lambda h: h.reciprocal(out=e.rstd4[:], in_=e.rstd4[:]), after=[rm1])
            on_ts = []
            for hh_ in range(4):
                n1 = S.add("dve", lambda h, hh_=hh_: h.tensor_scalar_mul(
                    out=e.o16[:, hh_ * 256:(hh_ + 1) * 256], in0=e.o16[:, hh_ * 256:(hh_ + 1) * 256],
                    scalar1=e.rstd4[:, hh_:hh_ + 1]), after=[rm2, bl_ts[hh_][0]])
                on_ts.append(n1)
            onm = S.add("dve", lambda h: h.tensor_mul(out=e.o16[:], in0=e.o16[:], in1=e.onw16[:]), after=on_ts + boot)
            for cc in range(8):
                bankb = pbb[cc % 2]
                tt = S.add("pe", lambda h, cc=cc, bap=bankb.ap: h.transpose(
                    bap[:, 0:128], e.o16[:, cc * 128:(cc + 1) * 128], e.ident[:]),
                    after=[onm] + bankb.war() + cl[:1])
                bankb.writer = tt
                oevac_ts.append(rd(S, "act", bankb, lambda h, cc=cc, c=c, bap=bankb.ap: h.copy(
                    out=e.xo16[:, cc, c * 128:(c + 1) * 128], in_=bap[:, 0:128]), x16_rd))

            prev_tail = stm_guard + [onm]
            prev_ott = oevac_ts[-2:]
            prev_on = list(on_ts)
            prev_rm = [rm1]
            prev_lg = [lg13]

            if dbg == "x1s":
                bar_extra = [S.dma("sp", lambda h, t0=t0, c=c: h.dma_start(
                    out=e.dbg_out["ddel"][t0 + c * 128:t0 + (c + 1) * 128, :],
                    in_=e.delta16[:]), after=od_list)]
                if s == 0 and c == 0:
                    bar_extra.append(S.dma("sp", lambda h: h.dma_start(
                        out=e.dbg_out["dS0"][:],
                        in_=e.S32[:].rearrange("p a b -> p (a b)")),
                        after=[S16_last[(hx, kx)] for hx in range(4) for kx in (0, 1)]))
                scan_guard = scan_guard + bar_extra

        lwo = load_w(0, e.wT["woT"])
        ao_ts = []
        for dc in range(8):
            bank = pb[dc % 4]
            _, dep = use_w(0)
            pairs = [(e.wslot[0][:, kc, dc * 128:(dc + 1) * 128], e.xo16[:, kc, :], [lwo] + oevac_ts) for kc in range(8)]
            mt = pe_acc(S, bank, pairs)
            dep(mt)
            ao_ts.append(rd(S, "act", bank, lambda h, dc=dc, bap=bank.ap: h.copy(
                out=e.attnout[:, dc, :], in_=bap[:, 0:512])))
        sqa = [S.add("act", lambda h, dc=dc: h.activation(out=e.gacc[:, dc, :], in_=e.attnout[:, dc, :], func=AF.Square), after=[ao_ts[dc]] + fir_ts["l"]) for dc in range(8)]
        sA1 = ones_mm_rows([e.attnout[:, dc, :] for dc in range(8)], pb[0], ao_ts)
        sA2 = ones_mm_rows([e.gacc[:, dc, :] for dc in range(8)], pb[1], sqa)
        la1, la2 = ln_rows(pb[0], pb[1], 1e-5, e.rstdA[:], e.mrsA[:])
        x1_ts = []
        for dc in range(8):
            xr = S.dma("sp", lambda h, dc=dc, t0=t0: h.dma_start(
                out=e.xres[:], in_=e.x_fm.rearrange("(c p) t -> p c t", p=128)[:, dc, t0:t0 + NT]),
                after=x1_ts[-2:] + stdeps)
            w1_ = S.add("dve", lambda h, dc=dc: h.tensor_mul(out=e.x1c[:], in0=e.attnout[:, dc, :], in1=e.rstdA[:]), after=[la1, ao_ts[dc]] + x1_ts[-2:])
            w2_ = S.add("dve", lambda h: h.tensor_add(out=e.x1c[:], in0=e.x1c[:], in1=e.mrsA[:]), after=[la2, w1_])
            w3_ = S.add("dve", lambda h: h.tensor_add(out=e.x1c[:], in0=e.x1c[:], in1=e.xres[:]), after=[w2_, xr])
            st_ = S.dma("sp", lambda h, dc=dc, t0=t0: h.dma_start(
                out=e.agin[dc * 128:(dc + 1) * 128, t0:t0 + NT], in_=e.x1c[:]), after=[w3_])
            if dbg in ("x1", "x1s"):
                st2_ = S.dma("sp", lambda h, dc=dc, t0=t0: h.dma_start(
                    out=e.dbg_out["dx1"][dc * 128:(dc + 1) * 128, t0:t0 + NT], in_=e.x1c[:]), after=[w3_])
                x1_ts.append(st2_)
            x1_ts.append(st_)

        if dbg == "qkv":
            for nm, buf, off in (("dq", e.q16, 0), ("dk", e.k16, 0), ("dv", e.v16, 64)):
                dbg_ts.append(S.dma("sp", lambda h, nm=nm, buf=buf, off=off, t0=t0: h.dma_start(
                    out=e.dbg_out[nm].rearrange("(c p) t -> p c t", p=128)[:, :, t0:t0 + NT],
                    in_=buf[:, :, off:off + 512]), after=qc_ts + kc_ts + vc_ts))
        if dbg == "x1s":
            dbg_ts.append(S.dma("sp", lambda h, t0=t0: h.dma_start(
                out=e.dbg_out["dps"].rearrange("(c p) t -> p c t", p=128)[:, :, t0:t0 + NT],
                in_=e.ps16[:]), after=fir_ts["s"]))
            dbg_ts.append(S.dma("sp", lambda h, t0=t0: h.dma_start(
                out=e.dbg_out["dpl"].rearrange("(c p) t -> p c t", p=128)[:, :, t0:t0 + NT],
                in_=e.pl16[:]), after=fir_ts["l"]))
            dbg_ts.append(S.dma("sp", lambda h, t0=t0: h.dma_start(
                out=e.dbg_out["dwt"][t0:t0 + NT, :].rearrange("(n p) f -> p n f", p=128),
                in_=e.wts[:]), after=[x1_ts[0]]))

        x1_all += [t for t in x1_ts]
        drain_boot(24)
        bar = S.barrier(extra=x1_ts[-1:])
        bar_t = bar[0]
    drain_boot(len(bootq))
    pbar = S.barrier()
    ag = S.add("pool", lambda h: h.collective_compute(
        "AllGather", ALU.bypass, replica_groups=[list(range(NCORE))],
        ins=[e.agin.opt()], outs=[e.agout.opt()]),
        after=x1_all + boot_state["done"] + pbar)
    e.ctx1.close()
    if dbg is None or dbg == "p2":
        _phase2(e, S, ctx, ag, boot_state["done"], dbg)
    return nc, S, ctx, e, dbg_ts


def finish(res):
    nc, S, ctx, e, dbg_ts = res
    import os
    if os.environ.get("K1_NOREORDER") != "1":
        est = reorder(S)
        print(f"[k1] reorder: est makespan {est:.0f}us")
    with nc.Block() as block:
        S.emit(nc, block, ctx)
    return nc


def host_maps(inputs):
    f32 = np.float32
    x = np.asarray(inputs["L0_attn_hidden_states"], f32)
    g = lambda n: np.asarray(inputs[n], f32)
    shared = {
        "wqT": np.ascontiguousarray(g("L0_attn_Wq").T),
        "wkT": np.ascontiguousarray(g("L0_attn_Wk").T),
        "wvT": np.ascontiguousarray(g("L0_attn_Wv").T),
        "woT": np.ascontiguousarray(g("L0_attn_Wo").T),
        "w1T": np.ascontiguousarray(g("L0_attn_gate_W1").T),
        "wbT": np.ascontiguousarray(g("L0_attn_Wb").T),
        "w2gT": np.ascontiguousarray(g("L0_attn_gate_W2").T),
        "convq": np.ascontiguousarray(g("L0_attn_conv_q_w")),
        "convk": np.ascontiguousarray(g("L0_attn_conv_k_w")),
        "convv": np.ascontiguousarray(g("L0_attn_conv_v_w")),
        "firS": np.ascontiguousarray(g("L0_attn_fir_short_w").reshape(1024, 5)),
        "firL": np.ascontiguousarray(g("L0_attn_fir_long_w").reshape(1024, 63)),
        "b1": np.ascontiguousarray(g("L0_attn_gate_b1").reshape(8, 128).T),
        "b2g_rep": np.ascontiguousarray(np.broadcast_to(g("L0_attn_gate_b2"), (128, 16))),
        "lt_row": np.ascontiguousarray(g("L0_attn_log_temp").reshape(1, 16)),
        "fp_row": np.ascontiguousarray(g("L0_attn_floor_param").reshape(1, 16)),
        "onw": np.ascontiguousarray(np.broadcast_to(np.tile(g("L0_attn_o_norm_w"), 4), (128, 1024))),
        "identc": np.eye(128, dtype=NBF),
        "ones1c": np.ones((1, 128), NBF),
        "onescolc": np.ones((128, 1), NBF),
        "ind1c": np.repeat(np.eye(4, dtype=NBF), 256, axis=0),
        "mtriu1n": (-np.triu(np.ones((128, 128), f32), 1)).astype(NBF),
        "mtril1n": (-np.tril(np.ones((128, 128), f32), -1)).astype(NBF),
        "mtriu0": np.triu(np.ones((128, 128), f32), 0).astype(NBF),
        "fconsts": np.ascontiguousarray(np.broadcast_to(np.array(FCONST_VALS, f32), (128, 8))),
    }
    f = lambda n: np.asarray(inputs[n], f32)
    oh = np.zeros((8, 8, 128), NBF)
    for ee in range(8):
        oh[ee, ee, :] = 1
    gidx = np.arange(128) // 8
    shared.update({
        "inpT": np.ascontiguousarray(f("L0_ffn_in_proj_w").T),
        "inpB": np.ascontiguousarray(f("L0_ffn_in_proj_b").reshape(24, 128).T),
        "outwT": np.ascontiguousarray(f("L0_ffn_out_w").T),
        "outB": np.ascontiguousarray(f("L0_ffn_out_b").reshape(8, 128).T),
        "ln1g": np.ascontiguousarray(f("L0_ffn_ln1_g").reshape(8, 128).T),
        "ln1b": np.ascontiguousarray(f("L0_ffn_ln1_b").reshape(8, 128).T),
        "ln2g": np.ascontiguousarray(f("L0_ffn_ln2_g").reshape(8, 128).T),
        "ln2b": np.ascontiguousarray(f("L0_ffn_ln2_b").reshape(8, 128).T),
        "rwT": np.ascontiguousarray(f("L0_ffn_router_w").T),
        "rbrep": np.ascontiguousarray(np.broadcast_to(f("L0_ffn_router_b"), (128, 8))),
        "w1pT": np.ascontiguousarray(f("L0_ffn_w1").transpose(0, 2, 1)),
        "w2pT": np.ascontiguousarray(f("L0_ffn_w2").transpose(0, 2, 1)),
        "b1p2": np.ascontiguousarray(f("L0_ffn_b1").reshape(8, 8, 128).transpose(2, 0, 1)),
        "b2p2": np.ascontiguousarray(f("L0_ffn_b2")),
        "blockm": (gidx[:, None] == gidx[None, :]).astype(NBF),
        "repm": np.ascontiguousarray(np.eye(8, dtype=NBF)[:, np.tile(np.arange(8), 16)].reshape(8, 128)) if False else np.ascontiguousarray((np.arange(8)[:, None] == (np.arange(128) % 8)[None, :]).astype(NBF)),
        "onehots": oh,
        "idxc": np.ascontiguousarray(np.broadcast_to(np.arange(8, dtype=f32), (128, 8))),
    })
    maps = []
    for b in range(8):
        m = dict(shared)
        m["x_fm"] = np.ascontiguousarray(x[b].T)
        m["colbase"] = np.array([[b]], np.int32)
        maps.append(m)
    return maps


def kernel(**inputs):
    maps = host_maps(inputs)
    nc = finish(build(dbg=None))
    res = run_bass_kernel_spmd(nc, maps, core_ids=list(range(NCORE)))
    out = np.empty((8, 4096, 1024), np.float32)
    for c in range(NCORE):
        a = res.results[c]["out_fm"].reshape(1024, 512, 8)
        out[:, 512 * c:512 * (c + 1), :] = a.transpose(2, 1, 0)
    return out


def _phase2(e, S, ctx, ag, scr_done, dbg):
    nc = e.nc
    pb, pbb = e.pb, e.pbb
    sb = lambda n, sh, dt: ctx.enter_context(nc.sbuf_tensor(n, sh, dt))

    blockm = sb("blockm_s", [128, 128], BF16)
    repm = sb("repm_s", [8, 128], BF16)
    onehots = sb("onehots_s", [8, 8, 128], BF16)
    idxc = sb("idxc_s", [128, 8], F32)
    inpb_s = sb("inpb_s", [128, 24], F32)
    outb_s = sb("outb_s", [128, 8], F32)
    g1_s = sb("g1_s", [128, 8], F32)
    bb1_s = sb("bb1_s", [128, 8], F32)
    g2_s = sb("g2_s", [128, 8], F32)
    bb2_s = sb("bb2_s", [128, 8], F32)
    rw16 = sb("rw16", [128, 8, 8], BF16)
    rb_s = sb("rb_s", [128, 8], F32)
    b1p_s = sb("b1p_s", [128, 8, 8], F32)
    b2_16 = sb("b2_16", [8, 1024], BF16)

    ld2 = []

    def ld(dst, src, eng="sp"):
        t = S.dma(eng, lambda h, d=dst, s=src: h.dma_start(out=d, in_=s), after=[ag])
        ld2.append(t)
        return t
    ld(blockm[:], e.blockmD)
    ld(repm[:], e.repmD)
    ld(onehots[:], e.onehotsD)
    ld(idxc[:], e.idxcD)
    ld(inpb_s[:], e.inpB)
    ld(outb_s[:], e.outB)
    ld(g1_s[:], e.ln1g)
    ld(bb1_s[:], e.ln1b)
    ld(g2_s[:], e.ln2g)
    ld(bb2_s[:], e.ln2b)
    ld(rw16[:], e.rwT.rearrange("(c p) f -> p c f", p=128), eng="pool")
    ld(rb_s[:], e.rbrep)
    ld(b1p_s[:], e.b1p2)
    ld(b2_16[:], e.b2p2, eng="pool")
    colb_sb = e.colb
    xsl = S.dma("sp", lambda h: h.dma_start(
        out=e.xloc[:, :],
        in_=e.agout[:, bass.ts(h.partition_id(), 512)]), after=[ag])
    ld2.append(xsl)

    xg = sb("xg", [128, 8, 64], F32)
    x1t16 = sb("x1t16", [128, 8, 512], BF16)
    xf16 = sb("xf16", [128, 8, 512], BF16)
    qk16 = sb("qk16", [128, 16, 512], BF16)
    vtmp = sb("vtmp", [128, 512], BF16)
    big8 = sb("big8", [128, 4096], BF16)
    vtm16 = big8[:].rearrange("p (a b) -> p a b", a=4)
    ctx16 = sb("ctx16", [128, 8, 512], BF16)
    t16s = sb("t16s", [128, 512], BF16)

    hsum16 = x1t16
    hh16 = sb("hh16", [128, 8, 512], BF16)
    he16 = sb("he16", [128, 8, 4, 512], BF16)
    w1buf = [sb(f"w1b{i}", [128, 8, 512], BF16) for i in range(2)]
    w2buf = [sb(f"w2b{i}", [128, 8, 4, 128], BF16) for i in range(2)]
    r2acc = sb("r2acc", [128, 8, 512], F32)
    iobuf = [sb(f"iob{i}", [128, 8, 128], BF16) for i in range(2)]
    mrep16 = sb("mrep16", [128, 8, 512], BF16)
    maskrows = sb("maskrows", [8, 512], BF16)
    sqc = t16s
    scoret = sb("scoret", [128, 16, 8], F32)
    attn16 = sb("attn16", [128, 16, 8], BF16)
    msb2 = sb("msb2", [128, 128], BF16)
    atT = sb("atT", [8, 128], BF16)
    asb = sb("asb", [128, 128], BF16)
    logit2 = sb("logit2", [128, 8], F32)
    m2s = sb("m2s", [128, 1], F32)
    i1s = sb("i1s", [128, 1], F32)
    msk2 = sb("msk2", [128, 8], F32)
    mask216 = sb("mask216", [128, 8], BF16)
    r2_16 = big8[:].rearrange("p (c t) -> p c t", c=8)
    fo16 = qk16[:, 0:8, :]
    x2c = sb("x2c", [128, 512], F32)
    rows2 = sb("rows2", [1, 1024], F32)
    rb1_2 = sb("rb1_2", [1, 512], BF16)
    rb2_2 = sb("rb2_2", [1, 512], BF16)
    rstd_sb = sb("rstd_sb2", [128, 512], BF16)
    mrs_sb = sb("mrs_sb2", [128, 512], BF16)
    mx16 = sb("mx16", [128, 16], F32)
    tmp8 = sb("tmp8", [128, 8], F32)
    tf32 = sb("tf32", [128, 512], F32)

    r_t1 = rows2[0:1, 0:512]
    r_t2 = rows2[0:1, 512:1024]

    rows2_done = [[]]

    def ln_rows2(bank1, bank2, outb1, outb2, extra=()):
        t1 = rd(S, "dve", bank1, lambda h: h.tensor_scalar_mul(out=r_t1, in0=bank1.ap[0:1, :], scalar1=1.0 / 1024), list(extra) + rows2_done[0])
        t2 = S.add("act", lambda h: h.activation(out=r_t2, in_=r_t1, func=AF.Square), after=[t1])
        t3 = rd(S, "dve", bank2, lambda h: h.scalar_tensor_tensor(out=r_t2, in0=bank2.ap[0:1, :], scalar=1.0 / 1024, in1=r_t2, op0=ALU.mult, op1=ALU.subtract), [t2])
        t4 = S.add("act", lambda h: h.activation(out=r_t2, in_=r_t2, func=AF.Sqrt, bias=1e-5), after=[t3])
        t5 = S.add("dve", lambda h: h.reciprocal(out=r_t2, in_=r_t2), after=[t4])
        t6 = S.add("dve", lambda h: h.tensor_mul(out=r_t1, in0=r_t1, in1=r_t2), after=[t5])
        t7 = S.add("dve", lambda h: h.tensor_scalar_mul(out=r_t1, in0=r_t1, scalar1=-1.0), after=[t6])
        t8 = S.add("dve", lambda h: h.tensor_copy(out=rb1_2[:], in_=r_t2), after=[t5])
        t9 = S.add("dve", lambda h: h.tensor_copy(out=rb2_2[:], in_=r_t1), after=[t7])
        tb1 = pe_acc(S, pb[4], [(e.ones1[:], rb1_2[:], [t8])])
        o1 = rd(S, "act", pb[4], lambda h, outb1=outb1: h.copy(out=outb1, in_=pb[4].ap[:]))
        tb2 = pe_acc(S, pb[5], [(e.ones1[:], rb2_2[:], [t9])])
        o2 = rd(S, "act", pb[5], lambda h, outb2=outb2: h.copy(out=outb2, in_=pb[5].ap[:]))
        rows2_done[0] = [o1, o2]
        return o1, o2

    sq_state = {"last": None}

    def ln_sums(chunks_ap, chunk_deps):
        s1t = pe_acc_sl(S, pb[0], pb[0].ap[0:1, :],
                        [(e.onescol[:], chunks_ap[dc], chunk_deps[dc]) for dc in range(8)])
        prevsq = sq_state["last"]
        war = pb[1].war()
        if pb[1].writer is not None:
            war.append(pb[1].writer)
        last = None
        for dc in range(8):
            dep = list(chunk_deps[dc]) + ([prevsq] if prevsq else [])
            sq = S.add("act", lambda h, src=chunks_ap[dc]: h.activation(out=sqc[:], in_=src, func=AF.Square), after=dep)
            mm_ = S.add("pe", (lambda dc=dc, st=(dc == 0), sp=(dc == 7): (lambda h: h.matmul(
                pb[1].ap[0:1, :], e.onescol[:], sqc[:], start=st, stop=sp)))(),
                after=[sq] + (war if dc == 0 else []))
            prevsq = mm_
            last = mm_
        pb[1].writer = last
        sq_state["last"] = last
        return s1t, last

    out_all = []
    io_cons = [[], []]
    w1_cons = [[], []]
    w2_cons = [[], []]
    he_rd = {}
    prev_ro = []
    hsum_guard = []
    fo_guard = []
    prev_out = []
    prev_big8 = []
    prev_moe = []
    att_guard = []
    asg_guard = []
    NPOS = 64

    for j in range(8):
        tiledeps = [ag] + ld2
        x1t_ts = []
        for dc in range(8):
            lx = S.dma("sp", lambda h, dc=dc, j=j: h.dma_start(
                out=xg[:],
                in_=e.xloc.rearrange("(u c p) t -> u c p t", c=8, p=128)[:, dc, :, j * 64:(j + 1) * 64]
                    .rearrange("u p t -> p u t")),
                after=tiledeps + prev_ro + hsum_guard)
            ro = S.add("dve", lambda h, dc=dc: h.tensor_copy(
                out=x1t16[:, dc, :].rearrange("p (q u) -> p q u", u=8),
                in_=xg[:].rearrange("p u q -> p q u")), after=[lx])
            prev_ro = [ro]
            x1t_ts.append(ro)
        s1t, s2t = ln_sums([x1t16[:, dc, :] for dc in range(8)], [[x1t_ts[dc]] for dc in range(8)])
        la1, la2 = ln_rows2(pb[0], pb[1], rstd_sb[:], mrs_sb[:])
        xf_ts = []
        for dc in range(8):
            u1 = S.add("dve", lambda h, dc=dc: h.tensor_mul(out=xf16[:, dc, :], in0=x1t16[:, dc, :], in1=rstd_sb[:]), after=[la1, x1t_ts[dc]])
            u2 = S.add("dve", lambda h, dc=dc: h.tensor_add(out=xf16[:, dc, :], in0=xf16[:, dc, :], in1=mrs_sb[:]), after=[la2, u1])
            xf_ts.append(u2)

        vt_ts = []
        qk_ts = [None] * 16
        vtmp_guard = []
        for dc in range(24):
            slot = dc % 2
            lw = S.dma("sp", lambda h, dc=dc, slot=slot: h.dma_start(
                out=iobuf[slot][:], in_=e.scrI[dc]), after=scr_done + io_cons[slot] + [ag])
            io_cons[slot] = []
            bank = pb[dc % 2]
            pairs = [(iobuf[slot][:, kc, :], xf16[:, kc, :], [lw, xf_ts[kc]]) for kc in range(8)]
            mt = pe_acc(S, bank, pairs)
            io_cons[slot].append(mt)
            if dc < 16:
                ev = rd(S, "act", bank, lambda h, dc=dc, bap=bank.ap: h.activation(
                    out=qk16[:, dc, :], in_=bap[:, 0:512], func=AF.Identity, bias=inpb_s[:, dc:dc + 1]),
                    hsum_guard + (fo_guard if dc < 8 else []))
                qk_ts[dc] = ev
            else:
                ev = rd(S, "act", bank, lambda h, dc=dc, bap=bank.ap: h.activation(
                    out=vtmp[:], in_=bap[:, 0:512], func=AF.Identity, bias=inpb_s[:, dc:dc + 1]),
                    vtmp_guard + hsum_guard)
                tloc = []
                for sub in range(4):
                    bankb = pbb[sub % 2]
                    tt = S.add("pe", lambda h, dc=dc, sub=sub, bap=bankb.ap: h.transpose(
                        bap[:, 0:128], vtmp[:, sub * 128:(sub + 1) * 128], e.ident[:]),
                        after=[ev] + bankb.war())
                    bankb.writer = tt
                    ev2 = rd(S, "act", bankb, lambda h, dc=dc, sub=sub, bap=bankb.ap: h.copy(
                        out=vtm16[:, sub, (dc - 16) * 128:(dc - 15) * 128], in_=bap[:, 0:128]), prev_big8)
                    tloc.append(ev2)
                    vt_ts.append(ev2)
                vtmp_guard = tloc

        ctxev = [[None] * 4 for _ in range(8)]
        for sub in range(4):
            scols = slice(sub * 128, (sub + 1) * 128)
            red_ts = []
            for hh_ in range(16):
                dc = hh_ // 2
                po = 64 * (hh_ % 2)
                war = pb[2].war()
                if pb[2].writer is not None:
                    war.append(pb[2].writer)
                gm = S.add("pe", (lambda dc=dc, po=po, scols=scols: (lambda h: h.matmul(
                    pb[2].ap[:, 0:128], qk16[po:po + 64, 8 + dc, scols],
                    qk16[po:po + 64, dc, scols], start=True, stop=True)))(),
                    after=[qk_ts[8 + dc], qk_ts[dc]] + war)
                pb[2].writer = gm
                me = rd(S, "dve", pb[2], lambda h, bap=pb[2].ap: h.tensor_mul(
                    out=msb2[:], in0=bap[:, 0:128], in1=blockm[:]), ld2[:1] + att_guard + red_ts[-1:])
                bankb = pbb[0]
                tt = S.add("pe", lambda h, bap=bankb.ap: h.transpose(
                    bap[:, 0:128], msb2[:], e.ident[:]), after=[me] + bankb.war())
                bankb.writer = tt
                red = rd(S, "dve", bankb, lambda h, hh_=hh_, bap=bankb.ap: h.tensor_reduce(
                    out=scoret[:, hh_, :], in_=bap[:, 0:128].rearrange("p (g s) -> p s g", s=8),
                    axis=AX.X, op=ALU.add), att_guard)
                red_ts.append(red)
            f1 = S.add("dve", lambda h: h.tensor_reduce(out=mx16[:], in_=scoret[:], axis=AX.X, op=ALU.max), after=red_ts + att_guard)
            f2 = S.add("dve", lambda h: h.tensor_tensor(
                out=scoret[:], in0=scoret[:], in1=bc(mx16[:].unsqueeze(2), (128, 16, 8)), op=ALU.subtract), after=[f1])
            f3 = S.add("act", lambda h: h.activation(out=scoret[:], in_=scoret[:], func=AF.Exp, scale=0.125), after=[f2])
            f4 = S.add("dve", lambda h: h.tensor_reduce(out=mx16[:], in_=scoret[:], axis=AX.X, op=ALU.add), after=[f3])
            f5 = S.add("dve", lambda h: h.reciprocal(out=mx16[:], in_=mx16[:]), after=[f4])
            f6 = S.add("dve", lambda h: h.tensor_tensor(
                out=attn16[:], in0=scoret[:], in1=bc(mx16[:].unsqueeze(2), (128, 16, 8)), op=ALU.mult), after=[f5])
            att_guard = [f6]
            for hh_ in range(16):
                dc = hh_ // 2
                po = 64 * (hh_ % 2)
                bankb = pbb[1]
                tt = S.add("pe", lambda h, hh_=hh_, bap=bankb.ap: h.transpose(
                    bap[0:8, 0:128], attn16[:, hh_, :], e.ident[:]), after=[f6] + bankb.war())
                bankb.writer = tt
                ea = rd(S, "act", bankb, lambda h, bap=bankb.ap: h.copy(out=atT[:], in_=bap[0:8, 0:128]),
                        att_guard if att_guard else [])
                war3 = pb[3].war()
                if pb[3].writer is not None:
                    war3.append(pb[3].writer)
                rp = S.add("pe", lambda h: h.matmul(pb[3].ap[:, 0:128], repm[:], atT[:],
                                                    start=True, stop=True), after=[ea] + war3)
                pb[3].writer = rp
                am = rd(S, "dve", pb[3], lambda h, bap=pb[3].ap: h.tensor_mul(
                    out=asb[:], in0=bap[:, 0:128], in1=blockm[:]), att_guard)
                bank = pb[4 + (hh_ % 2)]
                war4 = bank.war()
                if bank.writer is not None:
                    war4.append(bank.writer)
                cm = S.add("pe", (lambda hh_=hh_, sub=sub, po=po, bank=bank: (lambda h: h.matmul(
                    bank.ap[po:po + 64, 0:128], vtm16[:, sub, hh_ * 64:(hh_ + 1) * 64],
                    asb[:], start=True, stop=True)))(),
                    after=[am] + war4 + vt_ts)
                bank.writer = cm
                ce = rd(S, "act", bank, lambda h, hh_=hh_, dc=dc, po=po, sub=sub, bank=bank: h.copy(
                    out=ctx16[po:po + 64, dc, sub * 128:(sub + 1) * 128], in_=bank.ap[po:po + 64, 0:128]))
                ctxev[dc][sub] = ce
                att_guard = [ea, am, ce]

        hs_ts = []
        for dc in range(8):
            slot = dc % 2
            lw = S.dma("sp", lambda h, dc=dc, slot=slot: h.dma_start(
                out=iobuf[slot][:], in_=e.scrO[dc]), after=scr_done + io_cons[slot] + [ag])
            io_cons[slot] = []
            bank = pb[dc % 2]
            pairs = [(iobuf[slot][:, kc, :], ctx16[:, kc, :], [lw] + [ctxev[kc][su] for su in range(4)]) for kc in range(8)]
            mt = pe_acc(S, bank, pairs)
            io_cons[slot].append(mt)
            t1 = rd(S, "act", bank, lambda h, dc=dc, bap=bank.ap: h.activation(
                out=t16s[:], in_=bap[:, 0:512], func=AF.Identity, bias=outb_s[:, dc:dc + 1]),
                hs_ts[-1:])
            t2 = S.add("dve", lambda h, dc=dc: h.tensor_add(out=hsum16[:, dc, :], in0=t16s[:], in1=xf16[:, dc, :]), after=[t1, xf_ts[dc]])
            hs_ts.append(t2)
        s1t, s2t = ln_sums([hsum16[:, dc, :] for dc in range(8)], [hs_ts for dc in range(8)])
        lb1, lb2 = ln_rows2(pb[0], pb[1], rstd_sb[:], mrs_sb[:])
        hsum_guard = [lb1]
        hh_ts = []
        for dc in range(8):
            u1 = S.add("dve", lambda h, dc=dc: h.tensor_mul(out=hh16[:, dc, :], in0=hsum16[:, dc, :], in1=rstd_sb[:]), after=[lb1, hs_ts[dc]] + prev_moe)
            u2 = S.add("dve", lambda h, dc=dc: h.tensor_add(out=hh16[:, dc, :], in0=hh16[:, dc, :], in1=mrs_sb[:]), after=[lb2, u1])
            u3 = S.add("dve", lambda h, dc=dc: h.tensor_scalar(out=hh16[:, dc, :], in0=hh16[:, dc, :],
                scalar1=g1_s[:, dc:dc + 1], scalar2=bb1_s[:, dc:dc + 1], op0=ALU.mult, op1=ALU.add), after=[u2])
            hh_ts.append(u3)

        mk_ts = []
        for sub in range(4):
            scols = slice(sub * 128, (sub + 1) * 128)
            bank = pb[2]
            pairs = [(hh16[:, kc, scols], rw16[:, kc, :], [hh_ts[kc]]) for kc in range(8)]
            mt = pe_acc_sl(S, bank, bank.ap[:, 0:8], pairs)
            g1 = rd(S, "dve", bank, lambda h, bap=bank.ap: h.tensor_add(out=logit2[:], in0=bap[:, 0:8], in1=rb_s[:]), ld2[:1] + asg_guard)
            g2 = S.add("dve", lambda h: h.tensor_reduce(out=m2s[:], in_=logit2[:], axis=AX.X, op=ALU.max), after=[g1])
            g3 = S.add("dve", lambda h: h.tensor_tensor(out=msk2[:], in0=logit2[:], in1=bc(m2s[:], (128, 8)), op=ALU.is_ge), after=[g2])
            g4 = S.add("dve", lambda h: h.tensor_mul(out=tmp8[:], in0=msk2[:], in1=idxc[:]), after=[g3] + ld2[3:4])
            g5 = S.add("dve", lambda h: h.tensor_reduce(out=i1s[:], in_=tmp8[:], axis=AX.X, op=ALU.max), after=[g4])
            g6 = S.add("dve", lambda h: h.scalar_tensor_tensor(out=tmp8[:], in0=msk2[:], scalar=-1e30, in1=logit2[:], op0=ALU.mult, op1=ALU.add), after=[g3, g5])
            g7 = S.add("dve", lambda h: h.tensor_reduce(out=m2s[:], in_=tmp8[:], axis=AX.X, op=ALU.max), after=[g6])
            g8 = S.add("dve", lambda h: h.tensor_tensor(out=msk2[:], in0=tmp8[:], in1=bc(m2s[:], (128, 8)), op=ALU.is_ge), after=[g7])
            g9 = S.add("dve", lambda h: h.tensor_mul(out=tmp8[:], in0=msk2[:], in1=idxc[:]), after=[g8])
            g10 = S.add("dve", lambda h: h.tensor_reduce(out=m2s[:], in_=tmp8[:], axis=AX.X, op=ALU.max), after=[g9])
            g11 = S.add("dve", lambda h: h.tensor_max(out=i1s[:], in0=i1s[:], in1=m2s[:]), after=[g10, g5])
            g12 = S.add("dve", lambda h: h.tensor_tensor(out=msk2[:], in0=bc(i1s[:], (128, 8)), in1=idxc[:], op=ALU.is_equal), after=[g11])
            g13 = S.add("dve", lambda h: h.tensor_copy(out=mask216[:], in_=msk2[:]), after=[g12])
            bankb = pbb[1]
            tt = S.add("pe", lambda h, bap=bankb.ap: h.transpose(
                bap[0:8, 0:128], mask216[:], e.ident[:]), after=[g13] + bankb.war())
            bankb.writer = tt
            mv = rd(S, "act", bankb, lambda h, sub=sub, bap=bankb.ap: h.copy(
                out=maskrows[:, sub * 128:(sub + 1) * 128], in_=bap[0:8, 0:128]), prev_moe)
            mk_ts.append(mv)
            asg_guard = [g13, mv]
        mrep_ts = []
        for ee in range(8):
            bank = pb[3]
            war = bank.war()
            if bank.writer is not None:
                war.append(bank.writer)
            mm_ = S.add("pe", (lambda ee=ee: (lambda h: h.matmul(
                pb[3].ap[:, 0:512], onehots[:, ee, :], maskrows[:], start=True, stop=True)))(),
                after=mk_ts + war + ld2[2:3])
            bank.writer = mm_
            mrep_ts.append(rd(S, "act", bank, lambda h, ee=ee, bap=bank.ap: h.copy(
                out=mrep16[:, ee, :], in_=bap[:, 0:512]), prev_moe))

        he_ts = {}
        r2_ts = []
        r2a_ts = {}
        for df in range(2):
            for ee in range(8):
                slot = ee % 2
                lw1 = S.dma("sp", lambda h, ee=ee, df=df, slot=slot: h.dma_start(
                    out=w1buf[slot][:], in_=e.scr1[ee][:, :, df * 512:(df + 1) * 512]),
                    after=scr_done + w1_cons[slot] + [ag])
                w1_cons[slot] = []
                for dci in range(4):
                    dc = df * 4 + dci
                    bank = pb[dci % 2]
                    pairs = [(w1buf[slot][:, kc, dci * 128:(dci + 1) * 128], hh16[:, kc, :], [lw1, hh_ts[kc]]) for kc in range(8)]
                    mt = pe_acc_sl(S, bank, bank.ap[:, 0:512], pairs)
                    w1_cons[slot].append(mt)
                    ev = rd(S, "act", bank, lambda h, ee=ee, dci=dci, dc=dc, bap=bank.ap: h.activation(
                        out=he16[:, ee, dci, :], in_=bap[:, 0:512], func=AF.Gelu,
                        bias=b1p_s[:, ee, dc:dc + 1]), he_rd.pop((ee, dci), []))
                    mk = S.add("dve", lambda h, ee=ee, dci=dci: h.tensor_mul(
                        out=he16[:, ee, dci, :], in0=he16[:, ee, dci, :],
                        in1=mrep16[:, ee, :]),
                        after=[ev, mrep_ts[ee]])
                    he_ts[(ee, dci)] = [mk]
            for dc in range(8):
                slot2 = dc % 2
                lw2 = []
                for ee in range(8):
                    lw2.append(S.dma("sp", lambda h, ee=ee, dc=dc, df=df, slot2=slot2: h.dma_start(
                        out=w2buf[slot2][:, ee, :, :], in_=e.scr2[ee, dc][:, df * 4:(df + 1) * 4, :]),
                        after=scr_done + w2_cons[slot2] + [ag]))
                w2_cons[slot2] = []
                bank = pb[2 + dc % 2]
                war = bank.war()
                if bank.writer is not None:
                    war.append(bank.writer)
                last = None
                seq = [(ee, kci) for ee in range(8) for kci in range(4)]
                for idx, (ee, kci) in enumerate(seq):
                    first = (idx == 0)
                    closes = (df == 0 and idx == len(seq) - 1)
                    aft = [lw2[ee]] + he_ts[(ee, kci)] + (war if first else [last])
                    last = S.add("pe", (lambda ee=ee, kci=kci, first=first, closes=closes, bank=bank, slot2=slot2: (lambda h: h.matmul(
                        bank.ap[:, 0:512], w2buf[slot2][:, ee, kci, :], he16[:, ee, kci, :],
                        start=first, stop=closes)))(), after=aft)
                    w2_cons[slot2].append(last)
                    he_rd.setdefault((ee, kci), []).append(last)
                if df == 0:
                    bank.writer = last
                    r2a_ts[dc] = rd(S, "dve", bank, lambda h, dc=dc, bank=bank: h.tensor_add(
                        out=r2acc[:, dc, :], in0=bank.ap[:, 0:512],
                        in1=hh16[:, dc, :]), [hh_ts[dc]])
                else:
                    last = S.add("pe", (lambda dc=dc, bank=bank: (lambda h: h.matmul(
                        bank.ap[:, 0:512], b2_16[:, dc * 128:(dc + 1) * 128],
                        maskrows[:], start=False, stop=True)))(),
                        after=[last] + mk_ts + ld2[-1:])
                    bank.writer = last
                    r2_ts.append(rd(S, "dve", bank, lambda h, dc=dc, bank=bank: h.tensor_add(
                        out=r2_16[:, dc, :], in0=bank.ap[:, 0:512],
                        in1=r2acc[:, dc, :]), [r2a_ts[dc]]))

        s1t, s2t = ln_sums([r2_16[:, dc, :] for dc in range(8)], [r2_ts for dc in range(8)])
        lc1, lc2 = ln_rows2(pb[0], pb[1], rstd_sb[:], mrs_sb[:])
        fo_ts = []
        for dc in range(8):
            u1 = S.add("dve", lambda h, dc=dc: h.tensor_mul(out=fo16[:, dc, :], in0=r2_16[:, dc, :], in1=rstd_sb[:]), after=[lc1] + r2_ts)
            u2 = S.add("dve", lambda h, dc=dc: h.tensor_add(out=fo16[:, dc, :], in0=fo16[:, dc, :], in1=mrs_sb[:]), after=[lc2, u1])
            u3 = S.add("dve", lambda h, dc=dc: h.tensor_scalar(out=fo16[:, dc, :], in0=fo16[:, dc, :],
                scalar1=g2_s[:, dc:dc + 1], scalar2=bb2_s[:, dc:dc + 1], op0=ALU.mult, op1=ALU.add), after=[u2])
            fo_ts.append(u3)
        s1t, s2t = ln_sums([fo16[:, dc, :] for dc in range(8)], [[fo_ts[dc]] for dc in range(8)])
        ld1, ld2_ = ln_rows2(pb[0], pb[1], rstd_sb[:], mrs_sb[:])
        prev_big8 = [s1t, s2t]
        prev_moe = r2_ts[-2:]
        out_ts = []
        for dc in range(8):
            chain = out_ts[-1:] if out_ts else prev_out
            lx = S.dma("sp", lambda h, dc=dc, j=j: h.dma_start(
                out=xg[:],
                in_=e.xloc.rearrange("(u c p) t -> u c p t", c=8, p=128)[:, dc, :, j * 64:(j + 1) * 64]
                    .rearrange("u p t -> p u t")),
                after=chain + x1t_ts + [fo_ts[dc]] + prev_ro)
            ro = S.add("dve", lambda h, dc=dc: h.tensor_copy(
                out=x2c[:].rearrange("p (q u) -> p q u", u=8),
                in_=xg[:].rearrange("p u q -> p q u")), after=[lx] + chain)
            w1_ = S.add("dve", lambda h, dc=dc: h.tensor_mul(out=tf32[:], in0=fo16[:, dc, :], in1=rstd_sb[:]), after=[ld1, fo_ts[dc]] + chain)
            w2_ = S.add("dve", lambda h: h.tensor_add(out=tf32[:], in0=tf32[:], in1=mrs_sb[:]), after=[ld2_, w1_])
            w3_ = S.add("dve", lambda h: h.tensor_add(out=x2c[:], in0=x2c[:], in1=tf32[:]), after=[w2_, ro])
            st_ = S.dma("sp", lambda h, dc=dc, j=j: h.dma_start(
                out=e.out_fm[dc * 128:(dc + 1) * 128, j * 512:(j + 1) * 512], in_=x2c[:]), after=[w3_])
            out_ts.append(st_)
            prev_ro = [ro]
        out_all += out_ts
        fo_guard = [w3_]
        prev_out = out_ts[-1:]
    S.barrier(extra=out_all)


# version 05
# speedup vs baseline: 1.0030x; 1.0030x over previous
import numpy as np
import ml_dtypes
import concourse.bass as bass
import concourse.mybir as mybir
from concourse.bass_utils import run_bass_kernel_spmd

F32 = mybir.dt.float32
BF16 = mybir.dt.bfloat16
AF = mybir.ActivationFunctionType
ALU = mybir.AluOpType
AX = mybir.AxisListType
NBF = ml_dtypes.bfloat16

ENGS = ("pe", "dve", "act", "pool", "sp")
FCONST_VALS = (1e-5, 1e-6, -1.0, 1.0, 1.0 / 256, 0.125, 2.0, 0.5)
ENG_ATTR = {"pe": "tensor", "dve": "vector", "act": "scalar",
            "pool": "gpsimd", "sp": "sync"}
NQSEM = 8


class Op:
    __slots__ = ("fn", "after", "chan", "inc", "seq", "val")

    def __init__(self, fn, after, chan, inc):
        self.fn = fn
        self.after = after
        self.chan = chan
        self.inc = inc
        self.val = None


class Sched:

    def __init__(self):
        self.ops = {e: [] for e in ENGS}
        self.qcount = {"sp": 0, "pool": 0}
        self.prologue = []
        self.preprologue = set()

    def add(self, eng, fn, after=()):
        op = Op(fn, [t for t in after if t is not None], eng, 1)
        self.ops[eng].append(op)
        return op

    def dma(self, eng, fn, after=()):
        i = self.qcount[eng]
        self.qcount[eng] = i + 1
        chan = f"{eng}q{i % NQSEM}"
        op = Op(fn, [t for t in after if t is not None], chan, 16)
        self.ops[eng].append(op)
        return op

    def barrier(self, engines=ENGS, extra=()):
        marks = list(extra)
        for e in engines:
            if self.ops[e]:
                marks.append(self.ops[e][-1])
        out = []
        for e in engines:
            if self.ops[e]:
                out.append(self.add(e, lambda h: h.nop(nofuse=True), after=marks))
        return out

    def emit(self, nc, block, ctx):
        referenced = set()
        for t in self.prologue:
            referenced.add(id(t))
        for e in ENGS:
            for op in self.ops[e]:
                for t in op.after:
                    referenced.add(id(t))
        counters = {}
        for e in ENGS:
            for op in self.ops[e]:
                if id(op) in referenced or op.inc == 16:
                    counters[op.chan] = counters.get(op.chan, 0) + op.inc
                    op.val = counters[op.chan]
        sems = {c: ctx.enter_context(nc.semaphore(f"s_{c}")) for c in counters}

        def make_body(e):
            def body(h):
                watermark = {}
                for op in self.ops[e]:
                    pro = [] if id(op) in self.preprologue else list(self.prologue)
                    for t in pro + op.after:
                        assert t.val is not None, "dep on unreferenced op?"
                        if t.val > watermark.get(t.chan, 0):
                            h.wait_ge(sems[t.chan], t.val)
                            watermark[t.chan] = t.val
                    if op.inc == 16 and op.val is not None and op.val > 16 and \
                            op.val - 16 > watermark.get(op.chan, 0):
                        h.wait_ge(sems[op.chan], op.val - 16)
                        watermark[op.chan] = op.val - 16
                    try:
                        instr = op.fn(h)
                    except Exception:
                        import traceback as _tb
                        print(f"EMIT FAIL eng={e} idx={self.ops[e].index(op)}")
                        _tb.print_exc()
                        raise
                    if op.val is not None:
                        ii = instr[-1] if isinstance(instr, (list, tuple)) else instr
                        ii.then_inc(sems[op.chan], op.inc)
            return body

        for e in ENGS:
            if self.ops[e]:
                getattr(block, ENG_ATTR[e])(make_body(e))


from contextlib import ExitStack

T = 512
NST = 8
NCH = 8
NC4 = 4
NCORE = 8


def reorder(S):
    import heapq
    ECOST = {"pe": 0.25, "dve": 0.40, "act": 0.35, "pool": 1.0, "sp": 0.30}
    DMA_LAT = 8.0
    DMA_ISSUE = 0.15

    order = []
    engof = {}
    for eng in ENGS:
        for op in S.ops[eng]:
            order.append((eng, op))
            engof[id(op)] = eng
    pro = list(S.prologue)

    def deps_of(op):
        d = op.after if id(op) in S.preprologue else op.after + pro
        return d

    cost = {}
    for eng, op in order:
        cost[id(op)] = DMA_LAT if op.inc == 16 else ECOST[eng]

    height = {id(op): cost[id(op)] for _, op in order}
    for eng, op in reversed(order):
        h = height[id(op)]
        for d in deps_of(op):
            if height[id(d)] < h + cost[id(d)]:
                height[id(d)] = h + cost[id(d)]

    consumers = {id(op): [] for _, op in order}
    indeg = {}
    for seq, (eng, op) in enumerate(order):
        ds = {id(d): d for d in deps_of(op)}
        indeg[id(op)] = len(ds)
        for did, d in ds.items():
            consumers[did].append(op)

    seqno = {id(op): i for i, (_, op) in enumerate(order)}
    rt = {id(op): 0.0 for _, op in order}
    future = {e: [] for e in ENGS}
    avail = {e: [] for e in ENGS}
    free = {e: 0.0 for e in ENGS}
    for _, op in order:
        if indeg[id(op)] == 0:
            heapq.heappush(future[engof[id(op)]], (0.0, seqno[id(op)], op))
    newops = {e: [] for e in ENGS}
    n = len(order)
    fin = {}
    while n:
        best = None
        for eng in ENGS:
            fq, aq = future[eng], avail[eng]
            while fq and fq[0][0] <= free[eng]:
                rt_, sq_, op_ = heapq.heappop(fq)
                pri = sq_ if eng in ("sp", "pool") else -height[id(op_)]
                heapq.heappush(aq, (pri, sq_, op_))
            if aq:
                cand = free[eng]
            elif fq:
                cand = max(free[eng], fq[0][0])
            else:
                continue
            if best is None or cand < best[0]:
                best = (cand, eng)
        start, eng = best
        if not avail[eng]:
            free[eng] = future[eng][0][0]
            continue
        _, _, op = heapq.heappop(avail[eng])
        f = start + cost[id(op)]
        free[eng] = start + (DMA_ISSUE if op.inc == 16 else cost[id(op)])
        fin[id(op)] = f
        newops[eng].append(op)
        for c_ in consumers[id(op)]:
            cid = id(c_)
            if f > rt[cid]:
                rt[cid] = f
            indeg[cid] -= 1
            if indeg[cid] == 0:
                heapq.heappush(future[engof[cid]], (rt[cid], seqno[cid], c_))
        n -= 1
    for eng in ENGS:
        assert len(newops[eng]) == len(S.ops[eng])
        S.ops[eng] = newops[eng]
    return max(fin.values()) if fin else 0.0


class Bank:

    def __init__(self, ap):
        self.ap = ap
        self.readers = []
        self.writer = None

    def war(self, extra=()):
        d = list(self.readers) + list(extra)
        self.readers = []
        return d

    def read_dep(self):
        return [self.writer] if self.writer is not None else []


def pe_acc(S, bank, pairs, extra_after=()):
    n = len(pairs)
    war = bank.war(extra_after)
    t = None
    for i, (lh, rh, deps) in enumerate(pairs):
        aft = list(deps) + (war if i == 0 else [t])
        st = (i == 0)
        sp = (i == n - 1)
        if i == 0 and bank.writer is not None:
            aft.append(bank.writer)
        t = S.add("pe", (lambda lh=lh, rh=rh, st=st, sp=sp, out=bank.ap[:, :]:
                         (lambda h: h.matmul(out, lh, rh, start=st, stop=sp)))(),
                  after=aft)
    bank.writer = t
    return t


def pe_acc_sl(S, bank, out_ap, pairs, extra_after=()):
    n = len(pairs)
    war = bank.war(extra_after)
    t = None
    for i, (lh, rh, deps) in enumerate(pairs):
        aft = list(deps) + (war if i == 0 else [t])
        if i == 0 and bank.writer is not None:
            aft.append(bank.writer)
        st = (i == 0)
        sp = (i == n - 1)
        t = S.add("pe", (lambda lh=lh, rh=rh, st=st, sp=sp, out=out_ap:
                         (lambda h: h.matmul(out, lh, rh, start=st, stop=sp)))(),
                  after=aft)
    bank.writer = t
    return t


def rd(S, eng, bank, fn, extra=()):
    t = S.add(eng, fn, after=bank.read_dep() + list(extra))
    bank.readers.append(t)
    return t


def build(dbg=None):
    nc = bass.Bass("TRN2", target_bir_lowering=False)
    S = Sched()
    ctx = ExitStack()

    dram = lambda n, sh, dt, kind: nc.dram_tensor(n, sh, dt, kind=kind).ap()
    x_fm = dram("x_fm", [1024, 4096], F32, "ExternalInput")
    wT = {n: dram(n, [1024, 1024], F32, "ExternalInput")
          for n in ("wqT", "wkT", "wvT", "woT")}
    w1T = dram("w1T", [1088, 1024], F32, "ExternalInput")
    wbT = dram("wbT", [1024, 4], F32, "ExternalInput")
    w2gT = dram("w2gT", [1024, 16], F32, "ExternalInput")
    convw = {n: dram(n, [1024, 4], F32, "ExternalInput")
             for n in ("convq", "convk", "convv")}
    firS = dram("firS", [1024, 5], F32, "ExternalInput")
    firL = dram("firL", [1024, 63], F32, "ExternalInput")
    b1d = dram("b1", [128, 8], F32, "ExternalInput")
    b2g_rep = dram("b2g_rep", [128, 16], F32, "ExternalInput")
    lt_row = dram("lt_row", [1, 16], F32, "ExternalInput")
    fp_row = dram("fp_row", [1, 16], F32, "ExternalInput")
    onw = dram("onw", [128, 1024], F32, "ExternalInput")
    identD = dram("identc", [128, 128], BF16, "ExternalInput")
    ones1D = dram("ones1c", [1, 128], BF16, "ExternalInput")
    onescolD = dram("onescolc", [128, 1], BF16, "ExternalInput")
    ind1D = dram("ind1c", [1024, 4], BF16, "ExternalInput")
    mtriu1nD = dram("mtriu1n", [128, 128], BF16, "ExternalInput")
    mtril1nD = dram("mtril1n", [128, 128], BF16, "ExternalInput")
    mtriu0D = dram("mtriu0", [128, 128], BF16, "ExternalInput")
    fconstD = dram("fconsts", [128, 8], F32, "ExternalInput")
    colbD = dram("colbase", [1, 1], mybir.dt.int32, "ExternalInput")

    agin = dram("agin", [1024, 4096], F32, "Internal")
    agout = dram("agout", [8192, 4096], F32, "Internal")
    inpT = dram("inpT", [1024, 3072], F32, "ExternalInput")
    inpB = dram("inpB", [128, 24], F32, "ExternalInput")
    outwT = dram("outwT", [1024, 1024], F32, "ExternalInput")
    outB = dram("outB", [128, 8], F32, "ExternalInput")
    ln1g = dram("ln1g", [128, 8], F32, "ExternalInput")
    ln1b = dram("ln1b", [128, 8], F32, "ExternalInput")
    ln2g = dram("ln2g", [128, 8], F32, "ExternalInput")
    ln2b = dram("ln2b", [128, 8], F32, "ExternalInput")
    rwT = dram("rwT", [1024, 8], F32, "ExternalInput")
    rbrep = dram("rbrep", [128, 8], F32, "ExternalInput")
    w1pT = dram("w1pT", [8, 1024, 1024], F32, "ExternalInput")
    w2pT = dram("w2pT", [8, 1024, 1024], F32, "ExternalInput")
    b1p2 = dram("b1p2", [128, 8, 8], F32, "ExternalInput")
    b2p2 = dram("b2p2", [8, 1024], F32, "ExternalInput")
    blockmD = dram("blockm", [128, 128], BF16, "ExternalInput")
    repmD = dram("repm", [8, 128], BF16, "ExternalInput")
    onehotsD = dram("onehots", [8, 8, 128], BF16, "ExternalInput")
    idxcD = dram("idxc", [128, 8], F32, "ExternalInput")
    scr1 = dram("scr1", [8, 128, 8, 1024], BF16, "Internal")
    scr2 = dram("scr2", [8, 8, 128, 8, 128], BF16, "Internal")
    scrI = dram("scrI", [24, 128, 8, 128], BF16, "Internal")
    scrO = dram("scrO", [8, 128, 8, 128], BF16, "Internal")
    xloc = dram("xloc", [8192, 512], F32, "Internal")
    out_fm = dram("out_fm", [1024, 4096], F32, "ExternalOutput")

    dbg_out = {}
    if dbg == "qkv":
        for n in ("dq", "dk", "dv", "dxh"):
            dbg_out[n] = dram(n, [1024, 4096], BF16, "ExternalOutput")
    if dbg in ("x1", "x1s"):
        dbg_out["dx1"] = dram("dx1", [1024, 4096], F32, "ExternalOutput")
    if dbg == "x1s":
        for n in ("dps", "dpl"):
            dbg_out[n] = dram(n, [1024, 4096], BF16, "ExternalOutput")
        dbg_out["ddel"] = dram("ddel", [4096, 1024], BF16, "ExternalOutput")
        dbg_out["dsr"] = dram("dsr", [64, 4096], BF16, "ExternalOutput")
        dbg_out["dS0"] = dram("dS0", [128, 2048], F32, "ExternalOutput")
        dbg_out["dua"] = dram("dua", [128, 256], BF16, "ExternalOutput")
        dbg_out["dkh"] = dram("dkh", [128, 256], BF16, "ExternalOutput")
        dbg_out["dvb"] = dram("dvb", [128, 256], BF16, "ExternalOutput")
        dbg_out["dkraw"] = dram("dkraw", [128, 256], BF16, "ExternalOutput")
        dbg_out["dkh2"] = dram("dkh2", [128, 128], BF16, "ExternalOutput")
        dbg_out["dwt"] = dram("dwt", [4096, 16], F32, "ExternalOutput")

    ctx1 = ExitStack()
    sbg = lambda n, sh, dt: ctx.enter_context(nc.sbuf_tensor(n, sh, dt))
    sb = lambda n, sh, dt: ctx1.enter_context(nc.sbuf_tensor(n, sh, dt))
    ps = lambda n, sh, dt: ctx.enter_context(nc.psum_tensor(n, sh, dt))

    pb = [Bank(ps(f"pb{i}", [128, 512], F32)) for i in range(6)]
    pbb = [Bank(ps(f"pbb{i}", [128, 1024], BF16)) for i in range(2)]

    ident = sbg("ident", [128, 128], BF16)
    ones1 = sbg("ones1", [1, 128], BF16)
    onescol = sbg("onescol", [128, 1], BF16)
    mtriu1n = sbg("mtriu1n_s", [128, 128], BF16)
    mtril1n = sbg("mtril1n_s", [128, 128], BF16)
    mtriu0 = sbg("mtriu0_s", [128, 128], BF16)
    fconst = sbg("fconst", [128, 8], F32)
    colb = sbg("colb", [1, 1], mybir.dt.int32)
    for _i, _v in enumerate(FCONST_VALS):
        nc.const_aps.register(F32, _v, fconst[:, _i:_i + 1])
    ind1 = sb("ind1", [128, 8, 4], BF16)
    ctaps = {n: sb(f"ct_{n}", [128, 8, 4], F32) for n in ("convq", "convk", "convv")}
    ftapS = sb("ftapS", [128, 8, 5], F32)
    ftapL = sb("ftapL", [128, 8, 63], F32)
    b1sb = sb("b1sb", [128, 8], F32)
    b2gr = sb("b2gr", [128, 16], F32)
    onw16 = sb("onw16", [128, 1024], BF16)
    trow = sb("trow", [1, 16], F32)
    trow2 = sb("trow2", [1, 16], F32)
    trow3 = sb("trow3", [1, 16], F32)
    trow4 = sb("trow4", [1, 4], F32)
    trow4b = sb("trow4b", [1, 16], F32)
    trowb = sb("trowb", [1, 16], BF16)
    invtau_rep = sb("invtau_rep", [128, 16], F32)
    nf_rep = sb("nf_rep", [128, 16], F32)
    w1tail = [sb(f"w1tail{g}", [16, 1024], BF16) for g in range(4)]

    wslot = [sb(f"wslot{i}", [128, 8, 1024], BF16) for i in range(2)]
    wslot_cons = [[], []]

    xres = sb("xres", [128, 512], F32)
    xo16 = sb("xo16", [128, 8, 512], BF16)
    xhat = sb("xhat", [128, 8, 512], BF16)
    raw = sb("raw", [128, 8, 516], BF16)
    q16 = sb("q16", [128, 8, 512], BF16)
    k16 = sb("k16", [128, 8, 512], BF16)
    v16 = sb("v16", [128, 8, 576], BF16)
    v16o = sb("v16o", [128, 8, 576], BF16)
    ps16 = sb("ps16_", [128, 8, 512], BF16)
    pl16 = sb("pl16_", [128, 8, 512], BF16)
    gacc = sb("gacc", [128, 8, 512], BF16)
    delta16 = sb("delta16", [128, 1024], BF16)
    rawtail = {n: sb(f"rtail_{n}", [128, 8, 4], BF16) for n in ("q", "k", "v")}
    attnout = sb("attnout", [128, 8, 512], BF16)
    gp = sb("gp", [128, 7, 128], BF16)
    ap_ = sb("ap_", [128, 6, 128], BF16)
    msb = sb("msb", [128, 128], BF16)
    attnT = sb("attnT", [128, 128], BF16)
    khat = sb("khat", [128, 256], BF16)
    vb16 = sb("vb16", [128, 256], BF16)
    uadj = sb("uadj", [128, 256], BF16)
    S32 = sb("S32", [128, 8, 256], F32)
    S16 = sb("S16", [128, 8, 256], BF16)
    stm = sb("stm", [128, 16], F32)
    stm16 = sb("stm16", [128, 16], BF16)
    statrow = [sb(f"strow{p}", [16, 128], BF16) for p in range(4)]
    sqs = sb("sqs", [128, 256], F32)
    pt16 = sb("pt16", [128, 3, 1024], BF16)
    beta_tm = sb("beta_tm", [128, 4, 4], F32)
    rq_tm = sb("rq_tm", [128, 4, 4], F32)
    rk_tm = sb("rk_tm", [128, 4, 4], F32)
    nbrk_tm = sb("nbrk_tm", [128, 4, 4], F32)
    brk_tm = sb("brk_tm", [128, 4, 4], F32)
    row4a = sb("row4a", [1, 2048], F32)
    row4b = sb("row4b", [1, 2048], F32)
    rkrow16 = sb("rkrow16", [1, 2048], BF16)
    brkrow16 = sb("brkrow16", [1, 2048], BF16)
    wts = sb("wts", [128, 4, 16], F32)
    o16 = sb("o16", [128, 1024], BF16)
    ss4 = sb("ss4", [128, 4], F32)
    rstd4 = sb("rstd4", [128, 4], F32)
    lgt = sb("lgt", [128, 16], F32)
    r_s1 = row4a[0:1, 0:512]
    r_s2 = row4a[0:1, 512:1024]
    r_t1 = row4a[0:1, 1024:1536]
    r_t2 = row4a[0:1, 1536:2048]
    r_b1 = sb("r_b1", [1, 512], BF16)
    r_b2 = sb("r_b2", [1, 512], BF16)
    rstd0 = sb("rstd0", [128, 512], BF16)
    mrs0 = sb("mrs0", [128, 512], BF16)
    rstdA = sb("rstdA", [128, 512], BF16)
    mrsA = sb("mrsA", [128, 512], BF16)
    x1c = sb("x1c", [128, 512], F32)
    stg = [sb(f"stg{i}", [128, 1024], BF16) for i in range(1)]
    cacc = sb("cacc", [128, 512], BF16)
    env = dict(locals())
    return _build_rest(env)


def bc(ap3, shape):
    return ap3.to_broadcast(shape)


def _build_rest(E):
    from types import SimpleNamespace
    e = SimpleNamespace(**E)
    nc, S, ctx = e.nc, e.S, e.ctx
    dbg = e.dbg

    cl = []

    def ld(dst, src, eng="sp"):
        t = S.dma(eng, lambda h, d=dst, s=src: h.dma_start(out=d, in_=s))
        cl.append(t)
        return t

    ld(e.ident[:], e.identD)
    ld(e.ones1[:], e.ones1D)
    ld(e.onescol[:], e.onescolD)
    ld(e.ind1[:], e.ind1D.rearrange("(c p) f -> p c f", p=128))
    ld(e.mtriu1n[:], e.mtriu1nD)
    ld(e.mtril1n[:], e.mtril1nD)
    ld(e.mtriu0[:], e.mtriu0D)
    for n in ("convq", "convk", "convv"):
        ld(e.ctaps[n][:], e.convw[n].rearrange("(c p) f -> p c f", p=128))
    ld(e.ftapS[:], e.firS.rearrange("(c p) f -> p c f", p=128))
    ld(e.ftapL[:], e.firL.rearrange("(c p) f -> p c f", p=128))
    ld(e.b1sb[:], e.b1d)
    ld(e.b2gr[:], e.b2g_rep)
    ld(e.onw16[:], e.onw, eng="pool")
    wb16 = e.ctx1.enter_context(nc.sbuf_tensor("wb16", [128, 8, 4], BF16))
    w2g16 = e.ctx1.enter_context(nc.sbuf_tensor("w2g16", [128, 8, 16], BF16))
    ld(wb16[:], e.wbT.rearrange("(c p) f -> p c f", p=128), eng="pool")
    ld(w2g16[:], e.w2gT.rearrange("(c p) f -> p c f", p=128), eng="pool")
    for g in range(4):
        ld(e.w1tail[g][:], e.w1T[1024 + 16 * g:1024 + 16 * (g + 1), :], eng="pool")
    ld(e.fconst[:], e.fconstD)
    ld(e.colb[:], e.colbD)
    S.prologue = cl[:]
    for _e in ENGS:
        for _o in S.ops[_e]:
            S.preprologue.add(id(_o))
    tl = ld(e.trow[:], e.lt_row)
    tf = ld(e.trow2[:], e.fp_row)

    a1 = S.add("act", lambda h: h.activation(out=e.trow3[:], in_=e.trow[:], func=AF.Abs), after=[tl])
    a2 = S.add("act", lambda h: h.activation(out=e.trow4b[:], in_=e.trow3[:], func=AF.Exp, scale=-1.0), after=[a1])
    a3 = S.add("act", lambda h: h.activation(out=e.trow3[:], in_=e.trow4b[:], func=AF.Ln, bias=1.0), after=[a2])
    a4 = S.add("dve", lambda h: h.tensor_scalar_max(out=e.trow[:], in0=e.trow[:], scalar1=0.0), after=[tl, a1])
    a5 = S.add("dve", lambda h: h.tensor_add(out=e.trow[:], in0=e.trow[:], in1=e.trow3[:]), after=[a3, a4])
    a6 = S.add("dve", lambda h: h.tensor_scalar_add(out=e.trow[:], in0=e.trow[:], scalar1=0.25), after=[a5])
    a7 = S.add("dve", lambda h: h.reciprocal(out=e.trow[:], in_=e.trow[:]), after=[a6])
    a8 = S.add("dve", lambda h: h.tensor_copy(out=e.trowb[:], in_=e.trow[:]), after=[a7])
    b1_ = S.add("act", lambda h: h.activation(out=e.trow2[:], in_=e.trow2[:], func=AF.Sigmoid), after=[tf])
    b2_ = S.add("dve", lambda h: h.tensor_scalar(out=e.trow2[:], in0=e.trow2[:], scalar1=0.07, scalar2=0.005, op0=ALU.mult, op1=ALU.add), after=[b1_])
    b3_ = S.add("dve", lambda h: h.tensor_scalar_min(out=e.trow2[:], in0=e.trow2[:], scalar1=0.25), after=[b2_])
    b4_ = S.add("dve", lambda h: h.tensor_reduce(out=e.trow4[:], in_=e.trow2[:].rearrange("o (g f) -> o g f", f=4), axis=AX.X, op=ALU.add), after=[b3_])
    b5_ = S.add("dve", lambda h: h.tensor_scalar_min(out=e.trow4[:], in0=e.trow4[:], scalar1=0.99), after=[b4_])
    b6_ = S.add("dve", lambda h: h.reciprocal(out=e.trow4[:], in_=e.trow4[:]), after=[b5_])
    b7_ = S.add("dve", lambda h: h.tensor_tensor(out=e.trow2[:].rearrange("o (g f) -> o g f", f=4), in0=e.trow2[:].rearrange("o (g f) -> o g f", f=4), in1=bc(e.trow4[:].unsqueeze(2), (1, 4, 4)), op=ALU.mult), after=[b6_])
    b8_ = S.add("dve", lambda h: h.tensor_scalar(out=e.trow2[:], in0=e.trow2[:], scalar1=0.99, scalar2=1e-9, op0=ALU.mult, op1=ALU.add), after=[b7_])
    b9_ = S.add("dve", lambda h: h.tensor_copy(out=e.trow3[:], in_=e.trow2[:]), after=[b8_])
    nfb = e.ctx1.enter_context(nc.sbuf_tensor("nfb16", [1, 16], BF16))
    b10 = S.add("dve", lambda h: h.tensor_copy(out=nfb[:], in_=e.trow3[:]), after=[b9_])
    t1 = pe_acc_sl(S, e.pb[0], e.pb[0].ap[:, 0:16], [(e.ones1[:], e.trowb[:], [a8] + cl[:1])])
    r1 = rd(S, "act", e.pb[0], lambda h: h.copy(out=e.invtau_rep[:], in_=e.pb[0].ap[:, 0:16]))
    t2 = pe_acc_sl(S, e.pb[1], e.pb[1].ap[:, 0:16], [(e.ones1[:], nfb[:], [b10])])
    r2 = rd(S, "act", e.pb[1], lambda h: h.copy(out=e.nf_rep[:], in_=e.pb[1].ap[:, 0:16]))
    boot = [r1, r2] + cl

    bootq = []

    def cast_pair(src_f32, dst_bf16, stage_cols=1024):
        def mk(slot, src_f32=src_f32, dst_bf16=dst_bf16, stage_cols=stage_cols):
            lf = lambda h: h.dma_start(out=e.stg[slot][:, 0:stage_cols], in_=src_f32)
            shp = dst_bf16.shape
            src2 = e.stg[slot][:, 0:stage_cols]
            dst2 = dst_bf16
            if len(shp) == 3:
                src2 = src2.rearrange("p (d t) -> p d t", d=shp[0])
                dst2 = dst_bf16.rearrange("d p t -> p d t")
            sf = lambda h: h.dma_start(out=dst2, in_=src2)
            return lf, sf
        bootq.append(mk)

    for ee in range(8):
        for kcc in range(8):
            cast_pair(e.w1pT[ee, kcc * 128:(kcc + 1) * 128, :], e.scr1[ee, :, kcc, :])
    for ee in range(8):
        for kcc in range(8):
            cast_pair(e.w2pT[ee, kcc * 128:(kcc + 1) * 128, :],
                      e.scr2[ee, :, :, kcc, :].rearrange("d p t -> p d t") if False else e.scr2[ee, :, :, kcc, :])
    for kcc in range(8):
        for gg in range(3):
            cast_pair(e.inpT[kcc * 128:(kcc + 1) * 128, gg * 1024:(gg + 1) * 1024],
                      e.scrI[gg * 8:(gg + 1) * 8, :, kcc, :])
    for kcc in range(8):
        cast_pair(e.outwT[kcc * 128:(kcc + 1) * 128, :], e.scrO[:, :, kcc, :])

    boot_state = {"i": 0, "slot_prev": [None], "done": []}

    def drain_boot(n):
        for _ in range(n):
            i = boot_state["i"]
            if i >= len(bootq):
                return
            boot_state["i"] = i + 1
            slot = 0
            lf, sf = bootq[i](slot)
            prev = boot_state["slot_prev"][slot]
            lt = S.dma("pool", lf, after=([prev] if prev else []))
            st_ = S.dma("pool", sf, after=[lt])
            boot_state["slot_prev"][slot] = st_
            boot_state["done"].append(st_)

    x1_all = []

    pb, pbb = e.pb, e.pbb
    wslot_cons = [[], []]
    wload = [None, None]
    S16_last = {}

    def load_w(slot, srcT, nchunks=8):
        after = wslot_cons[slot][:]
        wslot_cons[slot] = []
        t = S.dma("pool", lambda h, d=e.wslot[slot][:, 0:nchunks, :],
                  s=srcT[0:128 * nchunks, :].rearrange("(c p) o -> p c o", p=128):
                  h.dma_start(out=d, in_=s), after=after)
        wload[slot] = t
        return t

    def use_w(slot):
        def dep(t):
            wslot_cons[slot].append(t)
        return wload[slot], dep

    NT = 512

    def ones_mm_rows(x16ap_chunks, bank, deps):
        pairs = [(e.onescol[:], x16ap_chunks[dc], deps) for dc in range(8)]
        return pe_acc_sl(S, bank, bank.ap[0:1, :], pairs)

    rows_done = [[]]

    def ln_rows(bank1, bank2, eps, outb1, outb2, extra=()):
        t1 = rd(S, "dve", bank1, lambda h: h.tensor_scalar_mul(out=e.r_t1, in0=bank1.ap[0:1, :], scalar1=1.0 / 1024), list(extra) + rows_done[0])
        t2 = S.add("act", lambda h: h.activation(out=e.r_t2, in_=e.r_t1, func=AF.Square), after=[t1])
        t3 = rd(S, "dve", bank2, lambda h: h.scalar_tensor_tensor(out=e.r_t2, in0=bank2.ap[0:1, :], scalar=1.0 / 1024, in1=e.r_t2, op0=ALU.mult, op1=ALU.subtract), [t2])
        t4 = S.add("act", lambda h, ep=eps: h.activation(out=e.r_t2, in_=e.r_t2, func=AF.Sqrt, bias=ep), after=[t3])
        t5 = S.add("dve", lambda h: h.reciprocal(out=e.r_t2, in_=e.r_t2), after=[t4])
        t6 = S.add("dve", lambda h: h.tensor_mul(out=e.r_t1, in0=e.r_t1, in1=e.r_t2), after=[t5])
        t7 = S.add("dve", lambda h: h.tensor_scalar_mul(out=e.r_t1, in0=e.r_t1, scalar1=-1.0), after=[t6])
        t8 = S.add("dve", lambda h: h.tensor_copy(out=e.r_b1[:], in_=e.r_t2), after=[t5])
        t9 = S.add("dve", lambda h: h.tensor_copy(out=e.r_b2[:], in_=e.r_t1), after=[t7])
        tb1 = pe_acc(S, pb[4], [(e.ones1[:], e.r_b1[:], [t8])])
        o1 = rd(S, "act", pb[4], lambda h: h.copy(out=outb1, in_=pb[4].ap[:]))
        tb2 = pe_acc(S, pb[5], [(e.ones1[:], e.r_b2[:], [t9])])
        o2 = rd(S, "act", pb[5], lambda h: h.copy(out=outb2, in_=pb[5].ap[:]))
        rows_done[0] = [o1, o2]
        return o1, o2

    x16v = e.xo16
    dbg_ts = []

    for s in range(NST):
        t0 = s * NT
        first_st = (s == 0)
        stdeps = [] if first_st else [bar_t]

        tx16 = S.dma("pool", lambda h, t0=t0: h.dma_start(
            out=x16v[:], in_=e.x_fm.rearrange("(c p) t -> p c t", p=128)[:, :, t0:t0 + NT]),
            after=boot + stdeps)
        if first_st:
            vh = S.add("dve", lambda h: h.memset(e.v16[:, :, 0:64], 0.0), after=stdeps)
        else:
            vh = S.add("dve", lambda h: h.tensor_copy(out=e.v16[:, :, 0:64], in_=e.v16[:, :, 512:576]), after=stdeps)

        sqt = [S.add("act", lambda h, dc=dc: h.activation(out=e.gacc[:, dc, :], in_=x16v[:, dc, :], func=AF.Square), after=[tx16] + stdeps) for dc in range(8)]
        s1t = ones_mm_rows([x16v[:, dc, :] for dc in range(8)], pb[0], [tx16])
        s2t = ones_mm_rows([e.gacc[:, dc, :] for dc in range(8)], pb[1], sqt)
        lr1, lr2 = ln_rows(pb[0], pb[1], 1e-5, e.rstd0[:], e.mrs0[:])
        xh_t = []
        for dc in range(8):
            u1 = S.add("dve", lambda h, dc=dc: h.tensor_mul(out=e.xhat[:, dc, :], in0=x16v[:, dc, :], in1=e.rstd0[:]), after=[lr1, tx16] + stdeps)
            u2 = S.add("dve", lambda h, dc=dc: h.tensor_add(out=e.xhat[:, dc, :], in0=e.xhat[:, dc, :], in1=e.mrs0[:]), after=[lr2, u1])
            xh_t.append(u2)

        if dbg == "qkv":
            dbg_ts.append(S.dma("sp", lambda h, t0=t0: h.dma_start(
                out=e.dbg_out["dxh"].rearrange("(c p) t -> p c t", p=128)[:, :, t0:t0 + NT],
                in_=e.xhat[:]), after=xh_t))

        def proj(name, wname, slot, outbuf, out_off, rawtail, ctap, prev_conv_ts):
            lw = load_w(slot, e.wT[wname])
            if first_st:
                hh = S.add("dve", lambda h: h.memset(e.raw[:, :, 0:4], 0.0), after=prev_conv_ts)
            else:
                hh = S.add("dve", lambda h, rt=rawtail: h.tensor_copy(out=e.raw[:, :, 0:4], in_=rt[:]), after=prev_conv_ts + stdeps)
            ev = []
            for dc in range(8):
                bank = pb[dc % 4]
                _, dep = use_w(slot)
                pairs = [(e.wslot[slot][:, kc, dc * 128:(dc + 1) * 128], e.xhat[:, kc, :], [lw, xh_t[kc]]) for kc in range(8)]
                mt = pe_acc(S, bank, pairs, extra_after=prev_conv_ts)
                dep(mt)
                ev.append(rd(S, "act", bank, lambda h, dc=dc, bap=bank.ap: h.copy(out=e.raw[:, dc, 4:516], in_=bap[:, 0:512])))
            st_ = S.add("dve", lambda h, rt=rawtail: h.tensor_copy(out=rt[:], in_=e.raw[:, :, 512:516]), after=ev + [hh])
            conv_ts = [st_]
            for dc in range(8):
                tt = S.add("dve", lambda h, dc=dc: h.tensor_scalar_mul(
                    out=e.cacc[:, 0:512], in0=e.raw[:, dc, 4:516], scalar1=ctap[:, dc, 3:4]),
                    after=[ev[dc], hh] + ([] if not conv_ts else [conv_ts[-1]]))
                for j in (1, 2, 3):
                    tt = S.add("dve", lambda h, dc=dc, j=j: h.scalar_tensor_tensor(
                        out=e.cacc[:, 0:512], in0=e.raw[:, dc, 4 - j:516 - j], scalar=ctap[:, dc, 3 - j:4 - j],
                        in1=e.cacc[:, 0:512], op0=ALU.mult, op1=ALU.add), after=[tt])
                si = S.add("act", lambda h, dc=dc, oo=out_off: h.activation(
                    out=outbuf[:, dc, oo:oo + 512], in_=e.cacc[:, 0:512], func=AF.Silu), after=[tt])
                conv_ts.append(si)
            return conv_ts[1:]

        qc_ts = proj("q", "wqT", 0, e.q16, 0, e.rawtail["q"], e.ctaps["convq"], [])
        kc_ts = proj("k", "wkT", 1, e.k16, 0, e.rawtail["k"], e.ctaps["convk"], qc_ts)
        vc_ts = proj("v", "wvT", 0, e.v16, 64, e.rawtail["v"], e.ctaps["convv"], kc_ts)
        bt_ts = []
        for tc in range(4):
            bank = pb[tc % 2]
            pairs = [(e.xhat[:, kc, tc * 128:(tc + 1) * 128], wb16[:, kc, :], [xh_t[kc]] + boot) for kc in range(8)]
            mt = pe_acc_sl(S, bank, bank.ap[:, 0:4], pairs)
            bt_ts.append(rd(S, "act", bank, lambda h, tc=tc, bap=bank.ap: h.activation(
                out=e.beta_tm[:, tc, :], in_=bap[:, 0:4], func=AF.Sigmoid)))

        def l2pass(src16, out_tm, conv_ts):
            sq_ts = [S.add("act", lambda h, dc=dc: h.activation(out=e.gacc[:, dc, :], in_=src16[:, dc, :], func=AF.Square), after=conv_ts) for dc in range(8)]
            outs = []
            for tc in range(4):
                bank = pb[2 + tc % 2]
                pairs = [(e.gacc[:, kc, tc * 128:(tc + 1) * 128], e.ind1[:, kc, :], [sq_ts[kc]]) for kc in range(8)]
                mt = pe_acc_sl(S, bank, bank.ap[:, 0:4], pairs)
                gacc_rd.append(mt)
                q1 = rd(S, "act", bank, lambda h, tc=tc, bap=bank.ap: h.activation(
                    out=out_tm[:, tc, :], in_=bap[:, 0:4], func=AF.Sqrt, bias=1e-6))
                q2 = S.add("dve", lambda h, tc=tc: h.reciprocal(out=out_tm[:, tc, :], in_=out_tm[:, tc, :]), after=[q1])
                outs.append(q2)
            return sq_ts, outs

        gacc_rd = [s2t]
        _, rq_ts = l2pass(e.q16, e.rq_tm, qc_ts)
        ksq_ts, rk_ts = l2pass(e.k16, e.rk_tm, kc_ts)
        ev_k, ev_b = [], []
        for hh_ in range(4):
            pairs = [(e.ind1[:, kc, hh_:hh_ + 1], e.gacc[:, kc, :], [ksq_ts[kc]]) for kc in range(8)]
            mt = pe_acc_sl(S, pb[4], pb[4].ap[0:1, :], pairs)
            gacc_rd.append(mt)
            ev_k.append(rd(S, "act", pb[4], lambda h, hh_=hh_: h.copy(
                out=e.row4a[0:1, hh_ * 512:(hh_ + 1) * 512], in_=pb[4].ap[0:1, 0:512]), rows_done[0]))
            pairs = [(wb16[:, kc, hh_:hh_ + 1], e.xhat[:, kc, :], [xh_t[kc]]) for kc in range(8)]
            mt = pe_acc_sl(S, pb[5], pb[5].ap[0:1, :], pairs)
            ev_b.append(rd(S, "act", pb[5], lambda h, hh_=hh_: h.activation(
                out=e.row4b[0:1, hh_ * 512:(hh_ + 1) * 512], in_=pb[5].ap[0:1, 0:512], func=AF.Sigmoid)))
        kr1 = S.add("act", lambda h: h.activation(out=e.row4a[:], in_=e.row4a[:], func=AF.Sqrt, bias=1e-6), after=ev_k)
        kr2 = S.add("dve", lambda h: h.reciprocal(out=e.row4a[:], in_=e.row4a[:]), after=[kr1])
        kr3 = S.add("dve", lambda h: h.tensor_copy(out=e.rkrow16[:], in_=e.row4a[:]), after=[kr2])
        br2 = S.add("dve", lambda h: h.tensor_mul(out=e.row4b[:], in0=e.row4b[:], in1=e.row4a[:]), after=ev_b + [kr2])
        br3 = S.add("dve", lambda h: h.tensor_copy(out=e.brkrow16[:], in_=e.row4b[:]), after=[br2])
        sc_ts = []
        for tc in range(4):
            c1 = S.add("dve", lambda h, tc=tc: h.tensor_mul(out=e.brk_tm[:, tc, :], in0=e.beta_tm[:, tc, :], in1=e.rk_tm[:, tc, :]), after=[bt_ts[tc], rk_ts[tc]])
            c2 = S.add("dve", lambda h, tc=tc: h.tensor_scalar_mul(out=e.nbrk_tm[:, tc, :], in0=e.brk_tm[:, tc, :], scalar1=-1.0), after=[c1])
            sc_ts.append(c2)

        vo_t = S.add("dve", lambda h: h.tensor_copy(out=e.v16o[:, :, 1:576], in_=e.v16[:, :, 0:575]), after=vc_ts + [vh])
        fir_ts = {"s": [], "l": []}
        for dc in range(8):
            tt = S.add("dve", lambda h, dc=dc: h.tensor_scalar_mul(
                out=e.ps16[:, dc, :], in0=e.v16[:, dc, 64:576], scalar1=e.ftapS[:, dc, 4:5]),
                after=vc_ts + [vh])
            for j in range(1, 5):
                src = e.v16 if j % 2 == 0 else e.v16o
                off = 64 - j if j % 2 == 0 else 65 - j
                tt = S.add("dve", lambda h, dc=dc, j=j, src=src, off=off: h.scalar_tensor_tensor(
                    out=e.ps16[:, dc, :], in0=src[:, dc, off:off + 512], scalar=e.ftapS[:, dc, 4 - j:5 - j],
                    in1=e.ps16[:, dc, :], op0=ALU.mult, op1=ALU.add), after=[tt, vo_t])
            fir_ts["s"].append(tt)
        lacc_prev = []
        for dc in range(8):
            gts = []
            for g in range(8):
                ntap = 8 if g < 7 else 7
                tt = None
                for jj in range(ntap):
                    j = g * 8 + jj
                    src = e.v16 if j % 2 == 0 else e.v16o
                    off = 64 - j if j % 2 == 0 else 65 - j
                    if jj == 0:
                        tt = S.add("dve", lambda h, dc=dc, g=g, j=j, src=src, off=off: h.tensor_scalar_mul(
                            out=e.gacc[:, g, :], in0=src[:, dc, off:off + 512], scalar1=e.ftapL[:, dc, 62 - j:63 - j]),
                            after=vc_ts + [vo_t] + lacc_prev + gacc_rd)
                    else:
                        tt = S.add("dve", lambda h, dc=dc, g=g, j=j, src=src, off=off: h.scalar_tensor_tensor(
                            out=e.gacc[:, g, :], in0=src[:, dc, off:off + 512], scalar=e.ftapL[:, dc, 62 - j:63 - j],
                            in1=e.gacc[:, g, :], op0=ALU.mult, op1=ALU.add), after=[tt])
                gts.append(tt)
            c1 = S.add("dve", lambda h: h.tensor_add(out=e.gacc[:, 0:4, :], in0=e.gacc[:, 0:4, :], in1=e.gacc[:, 4:8, :]), after=gts)
            c2 = S.add("dve", lambda h: h.tensor_add(out=e.gacc[:, 0:2, :], in0=e.gacc[:, 0:2, :], in1=e.gacc[:, 2:4, :]), after=[c1])
            c3 = S.add("dve", lambda h, dc=dc: h.tensor_add(out=e.pl16[:, dc, :], in0=e.gacc[:, 0, :], in1=e.gacc[:, 1, :]), after=[c2])
            lacc_prev = [c3]
            fir_ts["l"].append(c3)

        wstat_rd = []
        x16_rd = sqt + xh_t + [s1t]
        oevac_ts = []
        wstat_ts = []
        l1w = load_w(1, e.w1T)
        h1_ts = []
        scan_guard = []
        sqs_guard = []
        stm_guard = []
        prev_tail = []
        prev_ott = []
        prev_on = []
        prev_rm = []
        prev_lg = []
        stat_tt_prev = []
        for c in range(4):
            ccols = slice(c * 128, (c + 1) * 128)
            od_list = []
            for hh_ in range(4):
                FIRST = first_st and c == 0
                kq_dep = [qc_ts[2 * hh_], qc_ts[2 * hh_ + 1], kc_ts[2 * hh_], kc_ts[2 * hh_ + 1]]
                mM = pe_acc_sl(S, pb[0], pb[0].ap[:, 0:128],
                               [(e.k16[:, 2 * hh_ + kc, ccols], e.k16[:, 2 * hh_ + kc, ccols], kq_dep) for kc in (0, 1)])
                eM = rd(S, "act", pb[0], lambda h, bap=pb[0].ap: h.copy(out=e.msb[:], in_=bap[:, 0:128]), scan_guard)
                bb1 = pe_acc_sl(S, pb[2], pb[2].ap[:, 0:128],
                                [(e.ones1[:], e.brkrow16[0:1, hh_ * 512 + c * 128: hh_ * 512 + (c + 1) * 128], [br3])])
                g1 = rd(S, "dve", pb[2], lambda h, bap=pb[2].ap: h.tensor_mul(out=e.gp[:, 0, :], in0=e.msb[:], in1=bap[:, 0:128]), [eM] + scan_guard)
                g2 = S.add("dve", lambda h: h.tensor_mul(out=e.gp[:, 0, :], in0=e.gp[:, 0, :], in1=e.mtriu1n[:]), after=[g1])
                g3 = S.add("dve", lambda h, c=c, hh_=hh_: h.tensor_scalar_mul(out=e.gp[:, 0, :], in0=e.gp[:, 0, :], scalar1=e.rk_tm[:, c, hh_:hh_ + 1]), after=[g2, rk_ts[c]])
                bb2 = pe_acc_sl(S, pb[2], pb[2].ap[:, 0:128],
                                [(e.ones1[:], e.rkrow16[0:1, hh_ * 512 + c * 128: hh_ * 512 + (c + 1) * 128], [kr3])])
                a1_ = rd(S, "dve", pb[2], lambda h, bap=pb[2].ap: h.tensor_mul(out=e.ap_[:, 0, :], in0=e.msb[:], in1=bap[:, 0:128]), [eM] + scan_guard)
                a2_ = S.add("dve", lambda h: h.tensor_mul(out=e.ap_[:, 0, :], in0=e.ap_[:, 0, :], in1=e.mtril1n[:]), after=[a1_])
                a3_ = S.add("dve", lambda h, c=c, hh_=hh_: h.tensor_scalar_mul(out=e.ap_[:, 0, :], in0=e.ap_[:, 0, :], scalar1=e.brk_tm[:, c, hh_:hh_ + 1]), after=[a2_, sc_ts[c]])
                mN = pe_acc_sl(S, pb[1], pb[1].ap[:, 0:128],
                               [(e.k16[:, 2 * hh_ + kc, ccols], e.q16[:, 2 * hh_ + kc, ccols], kq_dep) for kc in (0, 1)])
                eN = rd(S, "act", pb[1], lambda h, bap=pb[1].ap: h.copy(out=e.msb[:], in_=bap[:, 0:128]), [g1, a1_])
                at1 = S.add("dve", lambda h: h.tensor_mul(out=e.attnT[:], in0=e.msb[:], in1=e.mtriu0[:]), after=[eN] + scan_guard)
                at2 = S.add("dve", lambda h, c=c, hh_=hh_: h.tensor_scalar_mul(out=e.attnT[:], in0=e.attnT[:], scalar1=e.rk_tm[:, c, hh_:hh_ + 1]), after=[at1, rk_ts[c]])
                gprev, aprev = g3, a3_
                for lv in range(1, 7):
                    mg = pe_acc_sl(S, pb[2], pb[2].ap[:, 0:128],
                                   [(e.ap_[:, lv - 1, :], e.gp[:, lv - 1, :], [gprev, aprev])])
                    eg = rd(S, "act", pb[2], lambda h, lv=lv, bap=pb[2].ap: h.copy(out=e.gp[:, lv, :], in_=bap[:, 0:128]))
                    if lv < 6:
                        ma = pe_acc_sl(S, pb[3], pb[3].ap[:, 0:128],
                                       [(e.gp[:, lv - 1, :], e.ap_[:, lv - 1, :], [gprev, aprev])])
                        ea = rd(S, "act", pb[3], lambda h, lv=lv, bap=pb[3].ap: h.copy(out=e.ap_[:, lv, :], in_=bap[:, 0:128]))
                        aprev = ea
                    gprev = eg
                gchain = gprev
                tr_ts = []
                for kc in (0, 1):
                    bankb = pbb[kc]
                    tv = S.add("pe", lambda h, kc=kc, hh_=hh_, c=c, bap=bankb.ap: h.transpose(
                        bap[:, 128:256], e.v16[:, 2 * hh_ + kc, 64 + c * 128:64 + (c + 1) * 128], e.ident[:]),
                        after=vc_ts + bankb.war() + cl[:1] + scan_guard)
                    bankb.writer = tv
                    ev_ = rd(S, "act", bankb, lambda h, kc=kc, c=c, hh_=hh_, bap=bankb.ap: h.activation(
                        out=e.vb16[:, kc * 128:(kc + 1) * 128], in_=bap[:, 128:256], func=AF.Copy,
                        scale=e.beta_tm[:, c, hh_:hh_ + 1]), [bt_ts[c]])
                    tk = S.add("pe", lambda h, kc=kc, hh_=hh_, c=c, bap=bankb.ap: h.transpose(
                        bap[:, 0:128], e.k16[:, 2 * hh_ + kc, c * 128:(c + 1) * 128], e.ident[:]),
                        after=kc_ts + bankb.war() + cl[:1] + scan_guard)
                    bankb.writer = tk
                    ek_ = rd(S, "act", bankb, lambda h, kc=kc, c=c, hh_=hh_, bap=bankb.ap: h.activation(
                        out=e.khat[:, kc * 128:(kc + 1) * 128], in_=bap[:, 0:128], func=AF.Copy,
                        scale=e.rk_tm[:, c, hh_:hh_ + 1]), [rk_ts[c]])
                    if dbg == "x1s" and first_st and c == 0 and hh_ == 0 and kc == 0:
                        ex2 = rd(S, "act", bankb, lambda h, bap=bankb.ap: h.copy(
                            out=e.o16[:, 0:128], in_=bap[:, 0:128]))
                        dbg_ts.append(S.dma("sp", lambda h: h.dma_start(
                            out=e.dbg_out["dkh2"][:], in_=e.o16[:, 0:128]), after=[ex2]))
                        tr_ts += [dbg_ts[-1]]
                    tr_ts += [ev_, ek_]
                if FIRST:
                    yt = S.add("dve", lambda h: h.tensor_copy(out=e.uadj[:], in_=e.vb16[:]), after=tr_ts + scan_guard)
                else:
                    sdep = [S16_last[(hh_, 0)], S16_last[(hh_, 1)]]
                    mz = pe_acc_sl(S, pb[4], pb[4].ap[:, 0:256],
                                   [(e.k16[:, 2 * hh_ + kc, ccols], e.S16[:, 2 * hh_ + kc, :], kq_dep + sdep) for kc in (0, 1)])
                    yt = rd(S, "dve", pb[4], lambda h, c=c, hh_=hh_, bap=pb[4].ap: h.scalar_tensor_tensor(
                        out=e.uadj[:], in0=bap[:, 0:256], scalar=e.nbrk_tm[:, c, hh_:hh_ + 1],
                        in1=e.vb16[:], op0=ALU.mult, op1=ALU.add), tr_ts + [sc_ts[c]] + scan_guard)
                owar = pb[5].war()
                if pb[5].writer is not None:
                    owar.append(pb[5].writer)
                sreaders = []
                oprev = None
                if not FIRST:
                    sdep = [S16_last[(hh_, 0)], S16_last[(hh_, 1)]]
                    m1 = S.add("pe", lambda h, hh_=hh_, c=c: h.matmul(
                        pb[5].ap[:, 0:256], e.q16[:, 2 * hh_ + 0, c * 128:(c + 1) * 128],
                        e.S16[:, 2 * hh_ + 0, :], start=True, stop=False),
                        after=kq_dep + sdep + owar)
                    m2 = S.add("pe", lambda h, hh_=hh_, c=c: h.matmul(
                        pb[5].ap[:, 0:256], e.q16[:, 2 * hh_ + 1, c * 128:(c + 1) * 128],
                        e.S16[:, 2 * hh_ + 1, :], start=False, stop=False), after=[m1])
                    sreaders += [m1, m2]
                    oprev = m2
                for lv in range(7):
                    mapp = pe_acc_sl(S, pb[4], pb[4].ap[:, 0:256],
                                     [(e.gp[:, lv, :], e.uadj[:], [yt, gchain])])
                    yt = rd(S, "dve", pb[4], lambda h, bap=pb[4].ap: h.tensor_add(
                        out=e.uadj[:], in0=bap[:, 0:256], in1=e.uadj[:]))
                mo = S.add("pe", lambda h, FIRST=FIRST: h.matmul(
                    pb[5].ap[:, 0:256], e.attnT[:], e.uadj[:], start=FIRST, stop=True),
                    after=[at2, yt] + ([oprev] if oprev is not None else owar))
                pb[5].writer = mo
                od = rd(S, "act", pb[5], lambda h, c=c, hh_=hh_, bap=pb[5].ap: h.activation(
                    out=e.delta16[:, hh_ * 256:(hh_ + 1) * 256], in_=bap[:, 0:256],
                    func=AF.Copy, scale=e.rq_tm[:, c, hh_:hh_ + 1]), [rq_ts[c]] + prev_tail)
                od_list.append(od)
                scan_guard = [od, at2]
                if not FIRST:
                    sreaders.append(mz)
                s16ts = []
                for kc in (0, 1):
                    bank = pb[2 + kc]
                    mds = pe_acc_sl(S, bank, bank.ap[:, 0:256],
                                    [(e.khat[:, kc * 128:(kc + 1) * 128], e.uadj[:], [yt] + tr_ts)])
                    if FIRST:
                        evs = rd(S, "dve", bank, lambda h, hh_=hh_, kc=kc, bap=bank.ap: h.tensor_copy(
                            out=e.S32[:, 2 * hh_ + kc, :], in_=bap[:, 0:256]))
                    else:
                        evs = rd(S, "dve", bank, lambda h, hh_=hh_, kc=kc, bap=bank.ap: h.tensor_add(
                            out=e.S32[:, 2 * hh_ + kc, :], in0=bap[:, 0:256], in1=e.S32[:, 2 * hh_ + kc, :]),
                            [S16_last.get((hh_, kc))] if (hh_, kc) in S16_last else [])
                    s16t = S.add("dve", lambda h, hh_=hh_, kc=kc: h.tensor_copy(
                        out=e.S16[:, 2 * hh_ + kc, :], in_=e.S32[:, 2 * hh_ + kc, :]),
                        after=[evs] + sreaders)
                    S16_last[(hh_, kc)] = s16t
                    s16ts.append(s16t)
                scan_guard = [od] + s16ts
                if dbg == "x1s" and first_st and c == 0 and hh_ == 0:
                    dbg_ts.append(S.dma("sp", lambda h: h.dma_start(
                        out=e.dbg_out["dua"][:], in_=e.uadj[:]), after=[yt]))
                    dbg_ts.append(S.dma("sp", lambda h: h.dma_start(
                        out=e.dbg_out["dkh"][:], in_=e.khat[:]), after=tr_ts))
                    dbg_ts.append(S.dma("sp", lambda h: h.dma_start(
                        out=e.dbg_out["dvb"][:], in_=e.vb16[:]), after=tr_ts))
                    dbg_ts.append(S.dma("sp", lambda h: h.dma_start(
                        out=e.dbg_out["dkraw"][:].rearrange("p (a t) -> p a t", a=2), in_=e.k16[:, 0:2, 0:128]), after=kc_ts))
                    scan_guard = scan_guard + dbg_ts[-4:]

            pt_ts = []
            for pth, (srcbuf, soff, deps) in enumerate(
                    ((e.ps16, 0, fir_ts["s"]), (e.pl16, 0, fir_ts["l"]), (e.v16, 64, vc_ts))):
                for cc in range(8):
                    bankb = pbb[cc % 2]
                    tt = S.add("pe", lambda h, srcbuf=srcbuf, soff=soff, cc=cc, c=c, bap=bankb.ap: h.transpose(
                        bap[:, 0:128], srcbuf[:, cc, soff + c * 128:soff + (c + 1) * 128], e.ident[:]),
                        after=list(deps) + bankb.war() + cl[:1])
                    bankb.writer = tt
                    pt_ts.append(rd(S, "act", bankb, lambda h, pth=pth, cc=cc, bap=bankb.ap: h.copy(
                        out=e.pt16[:, pth, cc * 128:(cc + 1) * 128], in_=bap[:, 0:128]), prev_tail))

            for pth in range(4):
                if pth == 3:
                    src = e.pt16[:, 2, :]
                    sdeps = pt_ts[16:24]
                elif pth == 2:
                    src = e.delta16[:]
                    sdeps = [od]
                else:
                    src = e.pt16[:, pth, :]
                    sdeps = pt_ts[pth * 8:(pth + 1) * 8]
                st_ts = []
                for hh_ in range(4):
                    sl_ = src[:, hh_ * 256:(hh_ + 1) * 256]
                    r1_ = S.add("dve", lambda h, sl_=sl_, hh_=hh_: h.tensor_reduce(
                        out=e.stm[:, 4 * hh_:4 * hh_ + 1], in_=sl_, axis=AX.X, op=ALU.add), after=sdeps + stm_guard)
                    r2_ = S.add("act", lambda h, sl_=sl_, hh_=hh_: h.activation(
                        out=e.sqs[:], in_=sl_, func=AF.Square, accum_out=e.stm[:, 4 * hh_ + 1:4 * hh_ + 2]), after=sdeps + sqs_guard + stm_guard)
                    sqs_guard = [r2_]
                    r3_ = S.add("act", lambda h, sl_=sl_, hh_=hh_: h.activation(
                        out=e.sqs[:], in_=sl_, func=AF.Abs, accum_out=e.stm[:, 4 * hh_ + 2:4 * hh_ + 3]), after=sdeps + [r2_])
                    sqs_guard = [r3_]
                    st_ts += [r1_, r2_, r3_]
                f1 = S.add("dve", lambda h: h.tensor_scalar_mul(
                    out=e.stm[:].rearrange("p (g f) -> p f g", f=4)[:, 0, :],
                    in0=e.stm[:].rearrange("p (g f) -> p f g", f=4)[:, 0, :], scalar1=1.0 / 256), after=st_ts)
                f2 = S.add("act", lambda h: h.activation(
                    out=e.stm[:].rearrange("p (g f) -> p f g", f=4)[:, 3, :],
                    in_=e.stm[:].rearrange("p (g f) -> p f g", f=4)[:, 1, :], func=AF.Sqrt), after=st_ts)
                f3 = S.add("act", lambda h: h.activation(
                    out=e.sqs[:, 0:4], in_=e.stm[:].rearrange("p (g f) -> p f g", f=4)[:, 0, :],
                    func=AF.Square), after=[f1] + sqs_guard)
                sqs_guard = [f3]
                f4 = S.add("dve", lambda h: h.scalar_tensor_tensor(
                    out=e.stm[:].rearrange("p (g f) -> p f g", f=4)[:, 1, :],
                    in0=e.stm[:].rearrange("p (g f) -> p f g", f=4)[:, 1, :], scalar=1.0 / 256,
                    in1=e.sqs[:, 0:4], op0=ALU.mult, op1=ALU.subtract), after=[f2, f3])
                f5 = S.add("dve", lambda h: h.tensor_scalar_mul(
                    out=e.stm[:].rearrange("p (g f) -> p f g", f=4)[:, 2, :],
                    in0=e.stm[:].rearrange("p (g f) -> p f g", f=4)[:, 2, :], scalar1=1.0 / 256), after=st_ts)
                f6 = S.add("dve", lambda h: h.tensor_copy(out=e.stm16[:], in_=e.stm[:]), after=[f1, f2, f4, f5] + stat_tt_prev)
                sqs_guard = [f3, f4]
                stm_guard = [f6]
                bankb = pbb[pth % 2]
                tt = S.add("pe", lambda h, bap=bankb.ap: h.transpose(
                    bap[0:16, 0:128], e.stm16[:], e.ident[:]), after=[f6] + bankb.war() + cl[:1])
                bankb.writer = tt
                stat_tt_prev = [tt]
                wstat_ts.append(rd(S, "act", bankb, lambda h, pth=pth, bap=bankb.ap: h.copy(
                    out=e.statrow[pth][:], in_=bap[0:16, 0:128]), wstat_rd))

            wstat_rd = wstat_rd[-8:]
            g_ts = []
            for dc in range(8):
                bank = pb[dc % 2]
                _, dep = use_w(1)
                pairs = [(e.wslot[1][:, kc, dc * 128:(dc + 1) * 128], e.xhat[:, kc, c * 128:(c + 1) * 128], [l1w, xh_t[kc]]) for kc in range(8)]
                pairs += [(e.w1tail[g][:, dc * 128:(dc + 1) * 128], e.statrow[g][:], [wstat_ts[-4 + g]] + boot) for g in range(4)]
                mt = pe_acc_sl(S, bank, bank.ap[:, 0:128], pairs)
                dep(mt)
                wstat_rd.append(mt)
                g_ts.append(rd(S, "act", bank, lambda h, dc=dc, c=c, bap=bank.ap: h.activation(
                    out=e.raw[:, dc, c * 128:(c + 1) * 128], in_=bap[:, 0:128], func=AF.Gelu,
                    bias=e.b1sb[:, dc:dc + 1])))
            h1_ts += g_ts
            pairs = [(e.raw[:, kc, c * 128:(c + 1) * 128], w2g16[:, kc, :], [g_ts[kc]]) for kc in range(8)]
            mt = pe_acc_sl(S, pb[2], pb[2].ap[:, 0:16], pairs)
            lg1 = rd(S, "dve", pb[2], lambda h, bap=pb[2].ap: h.tensor_add(out=e.lgt[:], in0=bap[:, 0:16], in1=e.b2gr[:]), boot + prev_lg)
            lg2 = S.add("dve", lambda h: h.tensor_mul(out=e.lgt[:], in0=e.lgt[:], in1=e.invtau_rep[:]), after=[lg1])
            lg3 = S.add("dve", lambda h: h.tensor_reduce(out=e.ss4[:], in_=e.lgt[:].rearrange("p (g f) -> p g f", f=4), axis=AX.X, op=ALU.max), after=[lg2] + prev_rm)
            lg4 = S.add("dve", lambda h: h.tensor_tensor(
                out=e.lgt[:].rearrange("p (g f) -> p g f", f=4), in0=e.lgt[:].rearrange("p (g f) -> p g f", f=4),
                in1=bc(e.ss4[:].unsqueeze(2), (128, 4, 4)), op=ALU.subtract), after=[lg3])
            lg5 = S.add("act", lambda h: h.activation(out=e.lgt[:], in_=e.lgt[:], func=AF.Exp), after=[lg4])
            lg6 = S.add("dve", lambda h: h.tensor_reduce(out=e.ss4[:], in_=e.lgt[:].rearrange("p (g f) -> p g f", f=4), axis=AX.X, op=ALU.add), after=[lg5])
            lg7 = S.add("dve", lambda h: h.reciprocal(out=e.ss4[:], in_=e.ss4[:]), after=[lg6])
            lg8 = S.add("dve", lambda h: h.tensor_tensor(
                out=e.lgt[:].rearrange("p (g f) -> p g f", f=4), in0=e.lgt[:].rearrange("p (g f) -> p g f", f=4),
                in1=bc(e.ss4[:].unsqueeze(2), (128, 4, 4)), op=ALU.mult), after=[lg7])
            lg9 = S.add("dve", lambda h: h.tensor_max(out=e.lgt[:], in0=e.lgt[:], in1=e.nf_rep[:]), after=[lg8])
            lg10 = S.add("dve", lambda h: h.tensor_reduce(out=e.ss4[:], in_=e.lgt[:].rearrange("p (g f) -> p g f", f=4), axis=AX.X, op=ALU.add), after=[lg9])
            lg11 = S.add("dve", lambda h: h.tensor_scalar_add(out=e.ss4[:], in0=e.ss4[:], scalar1=1e-8), after=[lg10])
            lg12 = S.add("dve", lambda h: h.reciprocal(out=e.ss4[:], in_=e.ss4[:]), after=[lg11])
            lg13 = S.add("dve", lambda h, c=c: h.tensor_tensor(
                out=e.wts[:, c, :].rearrange("p (g f) -> p g f", f=4), in0=e.lgt[:].rearrange("p (g f) -> p g f", f=4),
                in1=bc(e.ss4[:].unsqueeze(2), (128, 4, 4)), op=ALU.mult), after=[lg12])

            bl_ts = []
            for hh_ in range(4):
                hsl = slice(hh_ * 256, (hh_ + 1) * 256)
                t1_ = S.add("dve", lambda h, hh_=hh_, c=c: h.tensor_scalar_mul(
                    out=e.o16[:, hh_ * 256:(hh_ + 1) * 256], in0=e.pt16[:, 0, hh_ * 256:(hh_ + 1) * 256],
                    scalar1=e.wts[:, c, 4 * hh_:4 * hh_ + 1]), after=[lg13] + pt_ts[0:8] + prev_ott)
                for pi, (srcap,) in enumerate(((e.pt16[:, 1, hsl],), (e.delta16[:, hsl],), (e.pt16[:, 2, hsl],))):
                    deps = [t1_] + (pt_ts[8:16] if pi == 0 else ([od] if pi == 1 else pt_ts[16:24]))
                    t1_ = S.add("dve", lambda h, srcap=srcap, hh_=hh_, c=c, pi=pi: h.scalar_tensor_tensor(
                        out=e.o16[:, hh_ * 256:(hh_ + 1) * 256], in0=srcap,
                        scalar=e.wts[:, c, 4 * hh_ + 1 + pi:4 * hh_ + 2 + pi],
                        in1=e.o16[:, hh_ * 256:(hh_ + 1) * 256], op0=ALU.mult, op1=ALU.add), after=deps)
                r_ = S.add("act", lambda h, hh_=hh_: h.activation(
                    out=e.sqs[:], in_=e.o16[:, hh_ * 256:(hh_ + 1) * 256], func=AF.Square,
                    accum_out=e.ss4[:, hh_:hh_ + 1]), after=[t1_] + sqs_guard)
                sqs_guard = [r_]
                bl_ts.append((t1_, r_))
            rm1 = S.add("act", lambda h: h.activation(out=e.rstd4[:], in_=e.ss4[:], func=AF.Sqrt, scale=1.0 / 256, bias=1e-5), after=[r for _, r in bl_ts] + [lg13] + prev_on)
            rm2 = S.add("dve", lambda h: h.reciprocal(out=e.rstd4[:], in_=e.rstd4[:]), after=[rm1])
            on_ts = []
            for hh_ in range(4):
                n1 = S.add("dve", lambda h, hh_=hh_: h.tensor_scalar_mul(
                    out=e.o16[:, hh_ * 256:(hh_ + 1) * 256], in0=e.o16[:, hh_ * 256:(hh_ + 1) * 256],
                    scalar1=e.rstd4[:, hh_:hh_ + 1]), after=[rm2, bl_ts[hh_][0]])
                on_ts.append(n1)
            onm = S.add("dve", lambda h: h.tensor_mul(out=e.o16[:], in0=e.o16[:], in1=e.onw16[:]), after=on_ts + boot)
            for cc in range(8):
                bankb = pbb[cc % 2]
                tt = S.add("pe", lambda h, cc=cc, bap=bankb.ap: h.transpose(
                    bap[:, 0:128], e.o16[:, cc * 128:(cc + 1) * 128], e.ident[:]),
                    after=[onm] + bankb.war() + cl[:1])
                bankb.writer = tt
                oevac_ts.append(rd(S, "act", bankb, lambda h, cc=cc, c=c, bap=bankb.ap: h.copy(
                    out=e.xo16[:, cc, c * 128:(c + 1) * 128], in_=bap[:, 0:128]), x16_rd))

            prev_tail = stm_guard + [onm]
            prev_ott = oevac_ts[-2:]
            prev_on = list(on_ts)
            prev_rm = [rm1]
            prev_lg = [lg13]

            if dbg == "x1s":
                bar_extra = [S.dma("sp", lambda h, t0=t0, c=c: h.dma_start(
                    out=e.dbg_out["ddel"][t0 + c * 128:t0 + (c + 1) * 128, :],
                    in_=e.delta16[:]), after=od_list)]
                if s == 0 and c == 0:
                    bar_extra.append(S.dma("sp", lambda h: h.dma_start(
                        out=e.dbg_out["dS0"][:],
                        in_=e.S32[:].rearrange("p a b -> p (a b)")),
                        after=[S16_last[(hx, kx)] for hx in range(4) for kx in (0, 1)]))
                scan_guard = scan_guard + bar_extra

        lwo = load_w(0, e.wT["woT"])
        ao_ts = []
        for dc in range(8):
            bank = pb[dc % 4]
            _, dep = use_w(0)
            pairs = [(e.wslot[0][:, kc, dc * 128:(dc + 1) * 128], e.xo16[:, kc, :], [lwo] + oevac_ts) for kc in range(8)]
            mt = pe_acc(S, bank, pairs)
            dep(mt)
            ao_ts.append(rd(S, "act", bank, lambda h, dc=dc, bap=bank.ap: h.copy(
                out=e.attnout[:, dc, :], in_=bap[:, 0:512])))
        sqa = [S.add("act", lambda h, dc=dc: h.activation(out=e.gacc[:, dc, :], in_=e.attnout[:, dc, :], func=AF.Square), after=[ao_ts[dc]] + fir_ts["l"]) for dc in range(8)]
        sA1 = ones_mm_rows([e.attnout[:, dc, :] for dc in range(8)], pb[0], ao_ts)
        sA2 = ones_mm_rows([e.gacc[:, dc, :] for dc in range(8)], pb[1], sqa)
        la1, la2 = ln_rows(pb[0], pb[1], 1e-5, e.rstdA[:], e.mrsA[:])
        x1_ts = []
        for dc in range(8):
            xr = S.dma("sp", lambda h, dc=dc, t0=t0: h.dma_start(
                out=e.xres[:], in_=e.x_fm.rearrange("(c p) t -> p c t", p=128)[:, dc, t0:t0 + NT]),
                after=x1_ts[-2:] + stdeps)
            w1_ = S.add("dve", lambda h, dc=dc: h.tensor_mul(out=e.x1c[:], in0=e.attnout[:, dc, :], in1=e.rstdA[:]), after=[la1, ao_ts[dc]] + x1_ts[-2:])
            w2_ = S.add("dve", lambda h: h.tensor_add(out=e.x1c[:], in0=e.x1c[:], in1=e.mrsA[:]), after=[la2, w1_])
            w3_ = S.add("dve", lambda h: h.tensor_add(out=e.x1c[:], in0=e.x1c[:], in1=e.xres[:]), after=[w2_, xr])
            st_ = S.dma("sp", lambda h, dc=dc, t0=t0: h.dma_start(
                out=e.agin[dc * 128:(dc + 1) * 128, t0:t0 + NT], in_=e.x1c[:]), after=[w3_])
            if dbg in ("x1", "x1s"):
                st2_ = S.dma("sp", lambda h, dc=dc, t0=t0: h.dma_start(
                    out=e.dbg_out["dx1"][dc * 128:(dc + 1) * 128, t0:t0 + NT], in_=e.x1c[:]), after=[w3_])
                x1_ts.append(st2_)
            x1_ts.append(st_)

        if dbg == "qkv":
            for nm, buf, off in (("dq", e.q16, 0), ("dk", e.k16, 0), ("dv", e.v16, 64)):
                dbg_ts.append(S.dma("sp", lambda h, nm=nm, buf=buf, off=off, t0=t0: h.dma_start(
                    out=e.dbg_out[nm].rearrange("(c p) t -> p c t", p=128)[:, :, t0:t0 + NT],
                    in_=buf[:, :, off:off + 512]), after=qc_ts + kc_ts + vc_ts))
        if dbg == "x1s":
            dbg_ts.append(S.dma("sp", lambda h, t0=t0: h.dma_start(
                out=e.dbg_out["dps"].rearrange("(c p) t -> p c t", p=128)[:, :, t0:t0 + NT],
                in_=e.ps16[:]), after=fir_ts["s"]))
            dbg_ts.append(S.dma("sp", lambda h, t0=t0: h.dma_start(
                out=e.dbg_out["dpl"].rearrange("(c p) t -> p c t", p=128)[:, :, t0:t0 + NT],
                in_=e.pl16[:]), after=fir_ts["l"]))
            dbg_ts.append(S.dma("sp", lambda h, t0=t0: h.dma_start(
                out=e.dbg_out["dwt"][t0:t0 + NT, :].rearrange("(n p) f -> p n f", p=128),
                in_=e.wts[:]), after=[x1_ts[0]]))

        x1_all += [t for t in x1_ts]
        drain_boot(24)
        bar = S.barrier(extra=x1_ts[-1:])
        bar_t = bar[0]
    drain_boot(len(bootq))
    pbar = S.barrier()
    ag = S.add("pool", lambda h: h.collective_compute(
        "AllGather", ALU.bypass, replica_groups=[list(range(NCORE))],
        ins=[e.agin.opt()], outs=[e.agout.opt()]),
        after=x1_all + boot_state["done"] + pbar)
    e.ctx1.close()
    if dbg is None or dbg == "p2":
        _phase2(e, S, ctx, ag, boot_state["done"], dbg)
    return nc, S, ctx, e, dbg_ts


def finish(res):
    nc, S, ctx, e, dbg_ts = res
    import os
    if os.environ.get("K1_NOREORDER") != "1":
        est = reorder(S)
        print(f"[k1] reorder: est makespan {est:.0f}us")
    with nc.Block() as block:
        S.emit(nc, block, ctx)
    return nc


def host_maps(inputs):
    f32 = np.float32
    x = np.asarray(inputs["L0_attn_hidden_states"], f32)
    g = lambda n: np.asarray(inputs[n], f32)
    shared = {
        "wqT": np.ascontiguousarray(g("L0_attn_Wq").T),
        "wkT": np.ascontiguousarray(g("L0_attn_Wk").T),
        "wvT": np.ascontiguousarray(g("L0_attn_Wv").T),
        "woT": np.ascontiguousarray(g("L0_attn_Wo").T),
        "w1T": np.ascontiguousarray(g("L0_attn_gate_W1").T),
        "wbT": np.ascontiguousarray(g("L0_attn_Wb").T),
        "w2gT": np.ascontiguousarray(g("L0_attn_gate_W2").T),
        "convq": np.ascontiguousarray(g("L0_attn_conv_q_w")),
        "convk": np.ascontiguousarray(g("L0_attn_conv_k_w")),
        "convv": np.ascontiguousarray(g("L0_attn_conv_v_w")),
        "firS": np.ascontiguousarray(g("L0_attn_fir_short_w").reshape(1024, 5)),
        "firL": np.ascontiguousarray(g("L0_attn_fir_long_w").reshape(1024, 63)),
        "b1": np.ascontiguousarray(g("L0_attn_gate_b1").reshape(8, 128).T),
        "b2g_rep": np.ascontiguousarray(np.broadcast_to(g("L0_attn_gate_b2"), (128, 16))),
        "lt_row": np.ascontiguousarray(g("L0_attn_log_temp").reshape(1, 16)),
        "fp_row": np.ascontiguousarray(g("L0_attn_floor_param").reshape(1, 16)),
        "onw": np.ascontiguousarray(np.broadcast_to(np.tile(g("L0_attn_o_norm_w"), 4), (128, 1024))),
        "identc": np.eye(128, dtype=NBF),
        "ones1c": np.ones((1, 128), NBF),
        "onescolc": np.ones((128, 1), NBF),
        "ind1c": np.repeat(np.eye(4, dtype=NBF), 256, axis=0),
        "mtriu1n": (-np.triu(np.ones((128, 128), f32), 1)).astype(NBF),
        "mtril1n": (-np.tril(np.ones((128, 128), f32), -1)).astype(NBF),
        "mtriu0": np.triu(np.ones((128, 128), f32), 0).astype(NBF),
        "fconsts": np.ascontiguousarray(np.broadcast_to(np.array(FCONST_VALS, f32), (128, 8))),
    }
    f = lambda n: np.asarray(inputs[n], f32)
    oh = np.zeros((8, 8, 128), NBF)
    for ee in range(8):
        oh[ee, ee, :] = 1
    gidx = np.arange(128) // 8
    shared.update({
        "inpT": np.ascontiguousarray(f("L0_ffn_in_proj_w").T),
        "inpB": np.ascontiguousarray(f("L0_ffn_in_proj_b").reshape(24, 128).T),
        "outwT": np.ascontiguousarray(f("L0_ffn_out_w").T),
        "outB": np.ascontiguousarray(f("L0_ffn_out_b").reshape(8, 128).T),
        "ln1g": np.ascontiguousarray(f("L0_ffn_ln1_g").reshape(8, 128).T),
        "ln1b": np.ascontiguousarray(f("L0_ffn_ln1_b").reshape(8, 128).T),
        "ln2g": np.ascontiguousarray(f("L0_ffn_ln2_g").reshape(8, 128).T),
        "ln2b": np.ascontiguousarray(f("L0_ffn_ln2_b").reshape(8, 128).T),
        "rwT": np.ascontiguousarray(f("L0_ffn_router_w").T),
        "rbrep": np.ascontiguousarray(np.broadcast_to(f("L0_ffn_router_b"), (128, 8))),
        "w1pT": np.ascontiguousarray(f("L0_ffn_w1").transpose(0, 2, 1)),
        "w2pT": np.ascontiguousarray(f("L0_ffn_w2").transpose(0, 2, 1)),
        "b1p2": np.ascontiguousarray(f("L0_ffn_b1").reshape(8, 8, 128).transpose(2, 0, 1)),
        "b2p2": np.ascontiguousarray(f("L0_ffn_b2")),
        "blockm": (gidx[:, None] == gidx[None, :]).astype(NBF),
        "repm": np.ascontiguousarray(np.eye(8, dtype=NBF)[:, np.tile(np.arange(8), 16)].reshape(8, 128)) if False else np.ascontiguousarray((np.arange(8)[:, None] == (np.arange(128) % 8)[None, :]).astype(NBF)),
        "onehots": oh,
        "idxc": np.ascontiguousarray(np.broadcast_to(np.arange(8, dtype=f32), (128, 8))),
    })
    maps = []
    for b in range(8):
        m = dict(shared)
        m["x_fm"] = np.ascontiguousarray(x[b].T)
        m["colbase"] = np.array([[b]], np.int32)
        maps.append(m)
    return maps


def kernel(**inputs):
    maps = host_maps(inputs)
    nc = finish(build(dbg=None))
    res = run_bass_kernel_spmd(nc, maps, core_ids=list(range(NCORE)))
    out = np.empty((8, 4096, 1024), np.float32)
    for c in range(NCORE):
        a = res.results[c]["out_fm"].reshape(1024, 512, 8)
        out[:, 512 * c:512 * (c + 1), :] = a.transpose(2, 1, 0)
    return out


def _phase2(e, S, ctx, ag, scr_done, dbg):
    nc = e.nc
    pb, pbb = e.pb, e.pbb
    sb = lambda n, sh, dt: ctx.enter_context(nc.sbuf_tensor(n, sh, dt))

    blockm = sb("blockm_s", [128, 128], BF16)
    repm = sb("repm_s", [8, 128], BF16)
    onehots = sb("onehots_s", [8, 8, 128], BF16)
    idxc = sb("idxc_s", [128, 8], F32)
    inpb_s = sb("inpb_s", [128, 24], F32)
    outb_s = sb("outb_s", [128, 8], F32)
    g1_s = sb("g1_s", [128, 8], F32)
    bb1_s = sb("bb1_s", [128, 8], F32)
    g2_s = sb("g2_s", [128, 8], F32)
    bb2_s = sb("bb2_s", [128, 8], F32)
    rw16 = sb("rw16", [128, 8, 8], BF16)
    rb_s = sb("rb_s", [128, 8], F32)
    b1p_s = sb("b1p_s", [128, 8, 8], F32)
    b2_16 = sb("b2_16", [8, 1024], BF16)

    ld2 = []

    def ld(dst, src, eng="sp"):
        t = S.dma(eng, lambda h, d=dst, s=src: h.dma_start(out=d, in_=s), after=[ag])
        ld2.append(t)
        return t
    ld(blockm[:], e.blockmD)
    ld(repm[:], e.repmD)
    ld(onehots[:], e.onehotsD)
    ld(idxc[:], e.idxcD)
    ld(inpb_s[:], e.inpB)
    ld(outb_s[:], e.outB)
    ld(g1_s[:], e.ln1g)
    ld(bb1_s[:], e.ln1b)
    ld(g2_s[:], e.ln2g)
    ld(bb2_s[:], e.ln2b)
    ld(rw16[:], e.rwT.rearrange("(c p) f -> p c f", p=128), eng="pool")
    ld(rb_s[:], e.rbrep)
    ld(b1p_s[:], e.b1p2)
    ld(b2_16[:], e.b2p2, eng="pool")
    colb_sb = e.colb
    xsl = S.dma("sp", lambda h: h.dma_start(
        out=e.xloc[:, :],
        in_=e.agout[:, bass.ts(h.partition_id(), 512)]), after=[ag])
    ld2.append(xsl)

    xg = sb("xg", [128, 8, 64], F32)
    x1t16 = sb("x1t16", [128, 8, 512], BF16)
    xf16 = sb("xf16", [128, 8, 512], BF16)
    qk16 = sb("qk16", [128, 16, 512], BF16)
    vtmp = sb("vtmp", [128, 512], BF16)
    big8 = sb("big8", [128, 4096], BF16)
    vtm16 = big8[:].rearrange("p (a b) -> p a b", a=4)
    ctx16 = sb("ctx16", [128, 8, 512], BF16)
    t16s = sb("t16s", [128, 512], BF16)

    hsum16 = x1t16
    hh16 = sb("hh16", [128, 8, 512], BF16)
    he16 = sb("he16", [128, 8, 4, 512], BF16)
    w1buf = [sb(f"w1b{i}", [128, 8, 512], BF16) for i in range(2)]
    w2buf = [sb(f"w2b{i}", [128, 8, 4, 128], BF16) for i in range(2)]
    r2acc = sb("r2acc", [128, 8, 512], F32)
    iobuf = [sb(f"iob{i}", [128, 8, 128], BF16) for i in range(2)]
    mrep16 = sb("mrep16", [128, 8, 512], BF16)
    maskrows = sb("maskrows", [8, 512], BF16)
    sqc = t16s
    scoret = sb("scoret", [128, 16, 8], F32)
    attn16 = sb("attn16", [128, 16, 8], BF16)
    msb2 = sb("msb2", [128, 128], BF16)
    atT = sb("atT", [8, 128], BF16)
    asb = sb("asb", [128, 128], BF16)
    logit2 = sb("logit2", [128, 8], F32)
    m2s = sb("m2s", [128, 1], F32)
    i1s = sb("i1s", [128, 1], F32)
    msk2 = sb("msk2", [128, 8], F32)
    mask216 = sb("mask216", [128, 8], BF16)
    r2_16 = big8[:].rearrange("p (c t) -> p c t", c=8)
    fo16 = qk16[:, 0:8, :]
    x2c = sb("x2c", [128, 512], F32)
    rows2 = sb("rows2", [1, 1024], F32)
    rb1_2 = sb("rb1_2", [1, 512], BF16)
    rb2_2 = sb("rb2_2", [1, 512], BF16)
    rstd_sb = sb("rstd_sb2", [128, 512], BF16)
    mrs_sb = sb("mrs_sb2", [128, 512], BF16)
    mx16 = sb("mx16", [128, 16], F32)
    tmp8 = sb("tmp8", [128, 8], F32)
    tf32 = sb("tf32", [128, 512], F32)

    r_t1 = rows2[0:1, 0:512]
    r_t2 = rows2[0:1, 512:1024]

    rows2_done = [[]]

    def ln_rows2(bank1, bank2, outb1, outb2, extra=()):
        t1 = rd(S, "dve", bank1, lambda h: h.tensor_scalar_mul(out=r_t1, in0=bank1.ap[0:1, :], scalar1=1.0 / 1024), list(extra) + rows2_done[0])
        t2 = S.add("act", lambda h: h.activation(out=r_t2, in_=r_t1, func=AF.Square), after=[t1])
        t3 = rd(S, "dve", bank2, lambda h: h.scalar_tensor_tensor(out=r_t2, in0=bank2.ap[0:1, :], scalar=1.0 / 1024, in1=r_t2, op0=ALU.mult, op1=ALU.subtract), [t2])
        t4 = S.add("act", lambda h: h.activation(out=r_t2, in_=r_t2, func=AF.Sqrt, bias=1e-5), after=[t3])
        t5 = S.add("dve", lambda h: h.reciprocal(out=r_t2, in_=r_t2), after=[t4])
        t6 = S.add("dve", lambda h: h.tensor_mul(out=r_t1, in0=r_t1, in1=r_t2), after=[t5])
        t7 = S.add("dve", lambda h: h.tensor_scalar_mul(out=r_t1, in0=r_t1, scalar1=-1.0), after=[t6])
        t8 = S.add("dve", lambda h: h.tensor_copy(out=rb1_2[:], in_=r_t2), after=[t5])
        t9 = S.add("dve", lambda h: h.tensor_copy(out=rb2_2[:], in_=r_t1), after=[t7])
        tb1 = pe_acc(S, pb[4], [(e.ones1[:], rb1_2[:], [t8])])
        o1 = rd(S, "act", pb[4], lambda h, outb1=outb1: h.copy(out=outb1, in_=pb[4].ap[:]))
        tb2 = pe_acc(S, pb[5], [(e.ones1[:], rb2_2[:], [t9])])
        o2 = rd(S, "act", pb[5], lambda h, outb2=outb2: h.copy(out=outb2, in_=pb[5].ap[:]))
        rows2_done[0] = [o1, o2]
        return o1, o2

    sq_state = {"last": None}

    def ln_sums(chunks_ap, chunk_deps):
        s1t = pe_acc_sl(S, pb[0], pb[0].ap[0:1, :],
                        [(e.onescol[:], chunks_ap[dc], chunk_deps[dc]) for dc in range(8)])
        prevsq = sq_state["last"]
        war = pb[1].war()
        if pb[1].writer is not None:
            war.append(pb[1].writer)
        last = None
        for dc in range(8):
            dep = list(chunk_deps[dc]) + ([prevsq] if prevsq else [])
            sq = S.add("act", lambda h, src=chunks_ap[dc]: h.activation(out=sqc[:], in_=src, func=AF.Square), after=dep)
            mm_ = S.add("pe", (lambda dc=dc, st=(dc == 0), sp=(dc == 7): (lambda h: h.matmul(
                pb[1].ap[0:1, :], e.onescol[:], sqc[:], start=st, stop=sp)))(),
                after=[sq] + (war if dc == 0 else []))
            prevsq = mm_
            last = mm_
        pb[1].writer = last
        sq_state["last"] = last
        return s1t, last

    out_all = []
    io_cons = [[], []]
    w1_cons = [[], []]
    w2_cons = [[], []]
    he_rd = {}
    prev_ro = []
    hsum_guard = []
    fo_guard = []
    prev_out = []
    prev_big8 = []
    prev_moe = []
    att_guard = []
    asg_guard = []
    NPOS = 64

    for j in range(8):
        tiledeps = [ag] + ld2
        x1t_ts = []
        for dc in range(8):
            lx = S.dma("sp", lambda h, dc=dc, j=j: h.dma_start(
                out=xg[:],
                in_=e.xloc.rearrange("(u c p) t -> u c p t", c=8, p=128)[:, dc, :, j * 64:(j + 1) * 64]
                    .rearrange("u p t -> p u t")),
                after=tiledeps + prev_ro + hsum_guard)
            ro = S.add("dve", lambda h, dc=dc: h.tensor_copy(
                out=x1t16[:, dc, :].rearrange("p (q u) -> p q u", u=8),
                in_=xg[:].rearrange("p u q -> p q u")), after=[lx])
            prev_ro = [ro]
            x1t_ts.append(ro)
        s1t, s2t = ln_sums([x1t16[:, dc, :] for dc in range(8)], [[x1t_ts[dc]] for dc in range(8)])
        la1, la2 = ln_rows2(pb[0], pb[1], rstd_sb[:], mrs_sb[:])
        xf_ts = []
        for dc in range(8):
            u1 = S.add("dve", lambda h, dc=dc: h.tensor_mul(out=xf16[:, dc, :], in0=x1t16[:, dc, :], in1=rstd_sb[:]), after=[la1, x1t_ts[dc]])
            u2 = S.add("dve", lambda h, dc=dc: h.tensor_add(out=xf16[:, dc, :], in0=xf16[:, dc, :], in1=mrs_sb[:]), after=[la2, u1])
            xf_ts.append(u2)

        vt_ts = []
        qk_ts = [None] * 16
        vtmp_guard = []
        for dc in range(24):
            slot = dc % 2
            lw = S.dma("sp", lambda h, dc=dc, slot=slot: h.dma_start(
                out=iobuf[slot][:], in_=e.scrI[dc]), after=scr_done + io_cons[slot] + [ag])
            io_cons[slot] = []
            bank = pb[dc % 2]
            pairs = [(iobuf[slot][:, kc, :], xf16[:, kc, :], [lw, xf_ts[kc]]) for kc in range(8)]
            mt = pe_acc(S, bank, pairs)
            io_cons[slot].append(mt)
            if dc < 16:
                ev = rd(S, "act", bank, lambda h, dc=dc, bap=bank.ap: h.activation(
                    out=qk16[:, dc, :], in_=bap[:, 0:512], func=AF.Identity, bias=inpb_s[:, dc:dc + 1]),
                    hsum_guard + (fo_guard if dc < 8 else []))
                qk_ts[dc] = ev
            else:
                ev = rd(S, "act", bank, lambda h, dc=dc, bap=bank.ap: h.activation(
                    out=vtmp[:], in_=bap[:, 0:512], func=AF.Identity, bias=inpb_s[:, dc:dc + 1]),
                    vtmp_guard + hsum_guard)
                tloc = []
                for sub in range(4):
                    bankb = pbb[sub % 2]
                    tt = S.add("pe", lambda h, dc=dc, sub=sub, bap=bankb.ap: h.transpose(
                        bap[:, 0:128], vtmp[:, sub * 128:(sub + 1) * 128], e.ident[:]),
                        after=[ev] + bankb.war())
                    bankb.writer = tt
                    ev2 = rd(S, "act", bankb, lambda h, dc=dc, sub=sub, bap=bankb.ap: h.copy(
                        out=vtm16[:, sub, (dc - 16) * 128:(dc - 15) * 128], in_=bap[:, 0:128]), prev_big8)
                    tloc.append(ev2)
                    vt_ts.append(ev2)
                vtmp_guard = tloc

        ctxev = [[None] * 4 for _ in range(8)]
        for sub in range(4):
            scols = slice(sub * 128, (sub + 1) * 128)
            red_ts = []
            for hh_ in range(16):
                dc = hh_ // 2
                po = 64 * (hh_ % 2)
                war = pb[2].war()
                if pb[2].writer is not None:
                    war.append(pb[2].writer)
                gm = S.add("pe", (lambda dc=dc, po=po, scols=scols: (lambda h: h.matmul(
                    pb[2].ap[:, 0:128], qk16[po:po + 64, 8 + dc, scols],
                    qk16[po:po + 64, dc, scols], start=True, stop=True)))(),
                    after=[qk_ts[8 + dc], qk_ts[dc]] + war)
                pb[2].writer = gm
                me = rd(S, "dve", pb[2], lambda h, bap=pb[2].ap: h.tensor_mul(
                    out=msb2[:], in0=bap[:, 0:128], in1=blockm[:]), ld2[:1] + att_guard + red_ts[-1:])
                bankb = pbb[0]
                tt = S.add("pe", lambda h, bap=bankb.ap: h.transpose(
                    bap[:, 0:128], msb2[:], e.ident[:]), after=[me] + bankb.war())
                bankb.writer = tt
                red = rd(S, "dve", bankb, lambda h, hh_=hh_, bap=bankb.ap: h.tensor_reduce(
                    out=scoret[:, hh_, :], in_=bap[:, 0:128].rearrange("p (g s) -> p s g", s=8),
                    axis=AX.X, op=ALU.add), att_guard)
                red_ts.append(red)
            f1 = S.add("dve", lambda h: h.tensor_reduce(out=mx16[:], in_=scoret[:], axis=AX.X, op=ALU.max), after=red_ts + att_guard)
            f2 = S.add("dve", lambda h: h.tensor_tensor(
                out=scoret[:], in0=scoret[:], in1=bc(mx16[:].unsqueeze(2), (128, 16, 8)), op=ALU.subtract), after=[f1])
            f3 = S.add("act", lambda h: h.activation(out=scoret[:], in_=scoret[:], func=AF.Exp, scale=0.125), after=[f2])
            f4 = S.add("dve", lambda h: h.tensor_reduce(out=mx16[:], in_=scoret[:], axis=AX.X, op=ALU.add), after=[f3])
            f5 = S.add("dve", lambda h: h.reciprocal(out=mx16[:], in_=mx16[:]), after=[f4])
            f6 = S.add("dve", lambda h: h.tensor_tensor(
                out=attn16[:], in0=scoret[:], in1=bc(mx16[:].unsqueeze(2), (128, 16, 8)), op=ALU.mult), after=[f5])
            att_guard = [f6]
            for hh_ in range(16):
                dc = hh_ // 2
                po = 64 * (hh_ % 2)
                bankb = pbb[1]
                tt = S.add("pe", lambda h, hh_=hh_, bap=bankb.ap: h.transpose(
                    bap[0:8, 0:128], attn16[:, hh_, :], e.ident[:]), after=[f6] + bankb.war())
                bankb.writer = tt
                ea = rd(S, "act", bankb, lambda h, bap=bankb.ap: h.copy(out=atT[:], in_=bap[0:8, 0:128]),
                        att_guard if att_guard else [])
                war3 = pb[3].war()
                if pb[3].writer is not None:
                    war3.append(pb[3].writer)
                rp = S.add("pe", lambda h: h.matmul(pb[3].ap[:, 0:128], repm[:], atT[:],
                                                    start=True, stop=True), after=[ea] + war3)
                pb[3].writer = rp
                am = rd(S, "dve", pb[3], lambda h, bap=pb[3].ap: h.tensor_mul(
                    out=asb[:], in0=bap[:, 0:128], in1=blockm[:]), att_guard)
                bank = pb[4 + (hh_ % 2)]
                war4 = bank.war()
                if bank.writer is not None:
                    war4.append(bank.writer)
                cm = S.add("pe", (lambda hh_=hh_, sub=sub, po=po, bank=bank: (lambda h: h.matmul(
                    bank.ap[po:po + 64, 0:128], vtm16[:, sub, hh_ * 64:(hh_ + 1) * 64],
                    asb[:], start=True, stop=True)))(),
                    after=[am] + war4 + vt_ts)
                bank.writer = cm
                ce = rd(S, "act", bank, lambda h, hh_=hh_, dc=dc, po=po, sub=sub, bank=bank: h.copy(
                    out=ctx16[po:po + 64, dc, sub * 128:(sub + 1) * 128], in_=bank.ap[po:po + 64, 0:128]))
                ctxev[dc][sub] = ce
                att_guard = [ea, am, ce]

        hs_ts = []
        for dc in range(8):
            slot = dc % 2
            lw = S.dma("sp", lambda h, dc=dc, slot=slot: h.dma_start(
                out=iobuf[slot][:], in_=e.scrO[dc]), after=scr_done + io_cons[slot] + [ag])
            io_cons[slot] = []
            bank = pb[dc % 2]
            pairs = [(iobuf[slot][:, kc, :], ctx16[:, kc, :], [lw] + [ctxev[kc][su] for su in range(4)]) for kc in range(8)]
            mt = pe_acc(S, bank, pairs)
            io_cons[slot].append(mt)
            t1 = rd(S, "act", bank, lambda h, dc=dc, bap=bank.ap: h.activation(
                out=t16s[:], in_=bap[:, 0:512], func=AF.Identity, bias=outb_s[:, dc:dc + 1]),
                hs_ts[-1:])
            t2 = S.add("dve", lambda h, dc=dc: h.tensor_add(out=hsum16[:, dc, :], in0=t16s[:], in1=xf16[:, dc, :]), after=[t1, xf_ts[dc]])
            hs_ts.append(t2)
        s1t, s2t = ln_sums([hsum16[:, dc, :] for dc in range(8)], [hs_ts for dc in range(8)])
        lb1, lb2 = ln_rows2(pb[0], pb[1], rstd_sb[:], mrs_sb[:])
        hsum_guard = [lb1]
        hh_ts = []
        for dc in range(8):
            u1 = S.add("dve", lambda h, dc=dc: h.tensor_mul(out=hh16[:, dc, :], in0=hsum16[:, dc, :], in1=rstd_sb[:]), after=[lb1, hs_ts[dc]] + prev_moe)
            u2 = S.add("dve", lambda h, dc=dc: h.tensor_add(out=hh16[:, dc, :], in0=hh16[:, dc, :], in1=mrs_sb[:]), after=[lb2, u1])
            u3 = S.add("dve", lambda h, dc=dc: h.tensor_scalar(out=hh16[:, dc, :], in0=hh16[:, dc, :],
                scalar1=g1_s[:, dc:dc + 1], scalar2=bb1_s[:, dc:dc + 1], op0=ALU.mult, op1=ALU.add), after=[u2])
            hh_ts.append(u3)

        mk_ts = []
        for sub in range(4):
            scols = slice(sub * 128, (sub + 1) * 128)
            bank = pb[2]
            pairs = [(hh16[:, kc, scols], rw16[:, kc, :], [hh_ts[kc]]) for kc in range(8)]
            mt = pe_acc_sl(S, bank, bank.ap[:, 0:8], pairs)
            g1 = rd(S, "dve", bank, lambda h, bap=bank.ap: h.tensor_add(out=logit2[:], in0=bap[:, 0:8], in1=rb_s[:]), ld2[:1] + asg_guard)
            g2 = S.add("dve", lambda h: h.tensor_reduce(out=m2s[:], in_=logit2[:], axis=AX.X, op=ALU.max), after=[g1])
            g3 = S.add("dve", lambda h: h.tensor_tensor(out=msk2[:], in0=logit2[:], in1=bc(m2s[:], (128, 8)), op=ALU.is_ge), after=[g2])
            g4 = S.add("dve", lambda h: h.tensor_mul(out=tmp8[:], in0=msk2[:], in1=idxc[:]), after=[g3] + ld2[3:4])
            g5 = S.add("dve", lambda h: h.tensor_reduce(out=i1s[:], in_=tmp8[:], axis=AX.X, op=ALU.max), after=[g4])
            g6 = S.add("dve", lambda h: h.scalar_tensor_tensor(out=tmp8[:], in0=msk2[:], scalar=-1e30, in1=logit2[:], op0=ALU.mult, op1=ALU.add), after=[g3, g5])
            g7 = S.add("dve", lambda h: h.tensor_reduce(out=m2s[:], in_=tmp8[:], axis=AX.X, op=ALU.max), after=[g6])
            g8 = S.add("dve", lambda h: h.tensor_tensor(out=msk2[:], in0=tmp8[:], in1=bc(m2s[:], (128, 8)), op=ALU.is_ge), after=[g7])
            g9 = S.add("dve", lambda h: h.tensor_mul(out=tmp8[:], in0=msk2[:], in1=idxc[:]), after=[g8])
            g10 = S.add("dve", lambda h: h.tensor_reduce(out=m2s[:], in_=tmp8[:], axis=AX.X, op=ALU.max), after=[g9])
            g11 = S.add("dve", lambda h: h.tensor_max(out=i1s[:], in0=i1s[:], in1=m2s[:]), after=[g10, g5])
            g12 = S.add("dve", lambda h: h.tensor_tensor(out=msk2[:], in0=bc(i1s[:], (128, 8)), in1=idxc[:], op=ALU.is_equal), after=[g11])
            g13 = S.add("dve", lambda h: h.tensor_copy(out=mask216[:], in_=msk2[:]), after=[g12])
            bankb = pbb[1]
            tt = S.add("pe", lambda h, bap=bankb.ap: h.transpose(
                bap[0:8, 0:128], mask216[:], e.ident[:]), after=[g13] + bankb.war())
            bankb.writer = tt
            mv = rd(S, "act", bankb, lambda h, sub=sub, bap=bankb.ap: h.copy(
                out=maskrows[:, sub * 128:(sub + 1) * 128], in_=bap[0:8, 0:128]), prev_moe)
            mk_ts.append(mv)
            asg_guard = [g13, mv]
        mrep_ts = []
        for ee in range(8):
            bank = pb[3]
            war = bank.war()
            if bank.writer is not None:
                war.append(bank.writer)
            mm_ = S.add("pe", (lambda ee=ee: (lambda h: h.matmul(
                pb[3].ap[:, 0:512], onehots[:, ee, :], maskrows[:], start=True, stop=True)))(),
                after=mk_ts + war + ld2[2:3])
            bank.writer = mm_
            mrep_ts.append(rd(S, "act", bank, lambda h, ee=ee, bap=bank.ap: h.copy(
                out=mrep16[:, ee, :], in_=bap[:, 0:512]), prev_moe))

        he_ts = {}
        r2_ts = []
        r2a_ts = {}
        for df in range(2):
            for ee in range(8):
                slot = ee % 2
                lw1 = S.dma("sp", lambda h, ee=ee, df=df, slot=slot: h.dma_start(
                    out=w1buf[slot][:], in_=e.scr1[ee][:, :, df * 512:(df + 1) * 512]),
                    after=scr_done + w1_cons[slot] + [ag])
                w1_cons[slot] = []
                for dci in range(4):
                    dc = df * 4 + dci
                    bank = pb[dci % 2]
                    pairs = [(w1buf[slot][:, kc, dci * 128:(dci + 1) * 128], hh16[:, kc, :], [lw1, hh_ts[kc]]) for kc in range(8)]
                    mt = pe_acc_sl(S, bank, bank.ap[:, 0:512], pairs)
                    w1_cons[slot].append(mt)
                    ev = rd(S, "act", bank, lambda h, ee=ee, dci=dci, dc=dc, bap=bank.ap: h.activation(
                        out=he16[:, ee, dci, :], in_=bap[:, 0:512], func=AF.Gelu,
                        bias=b1p_s[:, ee, dc:dc + 1]), he_rd.pop((ee, dci), []))
                    mk = S.add("dve", lambda h, ee=ee, dci=dci: h.tensor_mul(
                        out=he16[:, ee, dci, :], in0=he16[:, ee, dci, :],
                        in1=mrep16[:, ee, :]),
                        after=[ev, mrep_ts[ee]])
                    he_ts[(ee, dci)] = [mk]
            for dc in range(8):
                slot2 = dc % 2
                lw2 = []
                for ee in range(8):
                    lw2.append(S.dma("sp", lambda h, ee=ee, dc=dc, df=df, slot2=slot2: h.dma_start(
                        out=w2buf[slot2][:, ee, :, :], in_=e.scr2[ee, dc][:, df * 4:(df + 1) * 4, :]),
                        after=scr_done + w2_cons[slot2] + [ag]))
                w2_cons[slot2] = []
                bank = pb[2 + dc % 2]
                war = bank.war()
                if bank.writer is not None:
                    war.append(bank.writer)
                last = None
                seq = [(ee, kci) for ee in range(8) for kci in range(4)]
                for idx, (ee, kci) in enumerate(seq):
                    first = (idx == 0)
                    closes = (df == 0 and idx == len(seq) - 1)
                    aft = [lw2[ee]] + he_ts[(ee, kci)] + (war if first else [last])
                    last = S.add("pe", (lambda ee=ee, kci=kci, first=first, closes=closes, bank=bank, slot2=slot2: (lambda h: h.matmul(
                        bank.ap[:, 0:512], w2buf[slot2][:, ee, kci, :], he16[:, ee, kci, :],
                        start=first, stop=closes)))(), after=aft)
                    w2_cons[slot2].append(last)
                    he_rd.setdefault((ee, kci), []).append(last)
                if df == 0:
                    bank.writer = last
                    r2a_ts[dc] = rd(S, "dve", bank, lambda h, dc=dc, bank=bank: h.tensor_add(
                        out=r2acc[:, dc, :], in0=bank.ap[:, 0:512],
                        in1=hh16[:, dc, :]), [hh_ts[dc]])
                else:
                    last = S.add("pe", (lambda dc=dc, bank=bank: (lambda h: h.matmul(
                        bank.ap[:, 0:512], b2_16[:, dc * 128:(dc + 1) * 128],
                        maskrows[:], start=False, stop=True)))(),
                        after=[last] + mk_ts + ld2[-1:])
                    bank.writer = last
                    r2_ts.append(rd(S, "dve", bank, lambda h, dc=dc, bank=bank: h.tensor_add(
                        out=r2_16[:, dc, :], in0=bank.ap[:, 0:512],
                        in1=r2acc[:, dc, :]), [r2a_ts[dc]]))

        s1t, s2t = ln_sums([r2_16[:, dc, :] for dc in range(8)], [r2_ts for dc in range(8)])
        lc1, lc2 = ln_rows2(pb[0], pb[1], rstd_sb[:], mrs_sb[:])
        fo_ts = []
        for dc in range(8):
            u1 = S.add("dve", lambda h, dc=dc: h.tensor_mul(out=fo16[:, dc, :], in0=r2_16[:, dc, :], in1=rstd_sb[:]), after=[lc1] + r2_ts)
            u2 = S.add("dve", lambda h, dc=dc: h.tensor_add(out=fo16[:, dc, :], in0=fo16[:, dc, :], in1=mrs_sb[:]), after=[lc2, u1])
            u3 = S.add("dve", lambda h, dc=dc: h.tensor_scalar(out=fo16[:, dc, :], in0=fo16[:, dc, :],
                scalar1=g2_s[:, dc:dc + 1], scalar2=bb2_s[:, dc:dc + 1], op0=ALU.mult, op1=ALU.add), after=[u2])
            fo_ts.append(u3)
        s1t, s2t = ln_sums([fo16[:, dc, :] for dc in range(8)], [[fo_ts[dc]] for dc in range(8)])
        ld1, ld2_ = ln_rows2(pb[0], pb[1], rstd_sb[:], mrs_sb[:])
        prev_big8 = [s1t, s2t]
        prev_moe = r2_ts[-2:]
        out_ts = []
        for dc in range(8):
            chain = out_ts[-1:] if out_ts else prev_out
            lx = S.dma("sp", lambda h, dc=dc, j=j: h.dma_start(
                out=xg[:],
                in_=e.xloc.rearrange("(u c p) t -> u c p t", c=8, p=128)[:, dc, :, j * 64:(j + 1) * 64]
                    .rearrange("u p t -> p u t")),
                after=chain + x1t_ts + [fo_ts[dc]] + prev_ro)
            ro = S.add("dve", lambda h, dc=dc: h.tensor_copy(
                out=x2c[:].rearrange("p (q u) -> p q u", u=8),
                in_=xg[:].rearrange("p u q -> p q u")), after=[lx] + chain)
            w1_ = S.add("dve", lambda h, dc=dc: h.tensor_mul(out=tf32[:], in0=fo16[:, dc, :], in1=rstd_sb[:]), after=[ld1, fo_ts[dc]] + chain)
            w2_ = S.add("dve", lambda h: h.tensor_add(out=tf32[:], in0=tf32[:], in1=mrs_sb[:]), after=[ld2_, w1_])
            w3_ = S.add("dve", lambda h: h.tensor_add(out=x2c[:], in0=x2c[:], in1=tf32[:]), after=[w2_, ro])
            st_ = S.dma("sp", lambda h, dc=dc, j=j: h.dma_start(
                out=e.out_fm[dc * 128:(dc + 1) * 128, j * 512:(j + 1) * 512], in_=x2c[:]), after=[w3_])
            out_ts.append(st_)
            prev_ro = [ro]
        out_all += out_ts
        fo_guard = [w3_]
        prev_out = out_ts[-1:]
    S.barrier(extra=out_all)


# version 06 (final)
# speedup vs baseline: 1.2170x; 1.2134x over previous
import numpy as np
import ml_dtypes
import concourse.bass as bass
import concourse.mybir as mybir
from concourse.bass_utils import run_bass_kernel_spmd

F32 = mybir.dt.float32
BF16 = mybir.dt.bfloat16
AF = mybir.ActivationFunctionType
ALU = mybir.AluOpType
AX = mybir.AxisListType
NBF = ml_dtypes.bfloat16

ENGS = ("pe", "dve", "act", "pool", "sp")
FCONST_VALS = (1e-5, 1e-6, -1.0, 1.0, 1.0 / 256, 0.125, 2.0, 0.5)
ENG_ATTR = {"pe": "tensor", "dve": "vector", "act": "scalar",
            "pool": "gpsimd", "sp": "sync"}
NQSEM = 8


class Op:
    __slots__ = ("fn", "after", "chan", "inc", "seq", "val")

    def __init__(self, fn, after, chan, inc):
        self.fn = fn
        self.after = after
        self.chan = chan
        self.inc = inc
        self.val = None


class Sched:

    def __init__(self):
        self.ops = {e: [] for e in ENGS}
        self.qcount = {"sp": 0, "pool": 0}
        self.prologue = []
        self.preprologue = set()

    def add(self, eng, fn, after=()):
        op = Op(fn, [t for t in after if t is not None], eng, 1)
        self.ops[eng].append(op)
        return op

    def dma(self, eng, fn, after=()):
        i = self.qcount[eng]
        self.qcount[eng] = i + 1
        chan = f"{eng}q{i % NQSEM}"
        op = Op(fn, [t for t in after if t is not None], chan, 16)
        self.ops[eng].append(op)
        return op

    def barrier(self, engines=ENGS, extra=()):
        marks = list(extra)
        for e in engines:
            if self.ops[e]:
                marks.append(self.ops[e][-1])
        out = []
        for e in engines:
            if self.ops[e]:
                out.append(self.add(e, lambda h: h.nop(nofuse=True), after=marks))
        return out

    def emit(self, nc, block, ctx):
        referenced = set()
        for t in self.prologue:
            referenced.add(id(t))
        for e in ENGS:
            for op in self.ops[e]:
                for t in op.after:
                    referenced.add(id(t))
        counters = {}
        for e in ENGS:
            for op in self.ops[e]:
                if id(op) in referenced or op.inc == 16:
                    counters[op.chan] = counters.get(op.chan, 0) + op.inc
                    op.val = counters[op.chan]
        sems = {c: ctx.enter_context(nc.semaphore(f"s_{c}")) for c in counters}

        def make_body(e):
            def body(h):
                watermark = {}
                for op in self.ops[e]:
                    pro = [] if id(op) in self.preprologue else list(self.prologue)
                    for t in pro + op.after:
                        assert t.val is not None, "dep on unreferenced op?"
                        if t.val > watermark.get(t.chan, 0):
                            h.wait_ge(sems[t.chan], t.val)
                            watermark[t.chan] = t.val
                    if op.inc == 16 and op.val is not None and op.val > 16 and \
                            op.val - 16 > watermark.get(op.chan, 0):
                        h.wait_ge(sems[op.chan], op.val - 16)
                        watermark[op.chan] = op.val - 16
                    try:
                        instr = op.fn(h)
                    except Exception:
                        import traceback as _tb
                        print(f"EMIT FAIL eng={e} idx={self.ops[e].index(op)}")
                        _tb.print_exc()
                        raise
                    if op.val is not None:
                        ii = instr[-1] if isinstance(instr, (list, tuple)) else instr
                        ii.then_inc(sems[op.chan], op.inc)
            return body

        for e in ENGS:
            if self.ops[e]:
                getattr(block, ENG_ATTR[e])(make_body(e))


from contextlib import ExitStack

T = 512
NST = 8
NCH = 8
NC4 = 4
NCORE = 8


class Bank:

    def __init__(self, ap):
        self.ap = ap
        self.readers = []
        self.writer = None

    def war(self, extra=()):
        d = list(self.readers) + list(extra)
        self.readers = []
        return d

    def read_dep(self):
        return [self.writer] if self.writer is not None else []


def pe_acc(S, bank, pairs, extra_after=()):
    n = len(pairs)
    war = bank.war(extra_after)
    t = None
    for i, (lh, rh, deps) in enumerate(pairs):
        aft = list(deps) + (war if i == 0 else [])
        st = (i == 0)
        sp = (i == n - 1)
        if i == 0 and bank.writer is not None:
            aft.append(bank.writer)
        t = S.add("pe", (lambda lh=lh, rh=rh, st=st, sp=sp, out=bank.ap[:, :]:
                         (lambda h: h.matmul(out, lh, rh, start=st, stop=sp)))(),
                  after=aft)
    bank.writer = t
    return t


def pe_acc_sl(S, bank, out_ap, pairs, extra_after=()):
    n = len(pairs)
    war = bank.war(extra_after)
    t = None
    for i, (lh, rh, deps) in enumerate(pairs):
        aft = list(deps) + (war if i == 0 else [])
        if i == 0 and bank.writer is not None:
            aft.append(bank.writer)
        st = (i == 0)
        sp = (i == n - 1)
        t = S.add("pe", (lambda lh=lh, rh=rh, st=st, sp=sp, out=out_ap:
                         (lambda h: h.matmul(out, lh, rh, start=st, stop=sp)))(),
                  after=aft)
    bank.writer = t
    return t


def rd(S, eng, bank, fn, extra=()):
    t = S.add(eng, fn, after=bank.read_dep() + list(extra))
    bank.readers.append(t)
    return t


def build(dbg=None):
    nc = bass.Bass("TRN2", target_bir_lowering=False)
    S = Sched()
    ctx = ExitStack()

    dram = lambda n, sh, dt, kind: nc.dram_tensor(n, sh, dt, kind=kind).ap()
    x_fm = dram("x_fm", [1024, 4096], F32, "ExternalInput")
    wT = {n: dram(n, [1024, 1024], F32, "ExternalInput")
          for n in ("wqT", "wkT", "wvT", "woT")}
    w1T = dram("w1T", [1088, 1024], F32, "ExternalInput")
    wbT = dram("wbT", [1024, 4], F32, "ExternalInput")
    w2gT = dram("w2gT", [1024, 16], F32, "ExternalInput")
    convw = {n: dram(n, [1024, 4], F32, "ExternalInput")
             for n in ("convq", "convk", "convv")}
    firS = dram("firS", [1024, 5], F32, "ExternalInput")
    firL = dram("firL", [1024, 63], F32, "ExternalInput")
    b1d = dram("b1", [128, 8], F32, "ExternalInput")
    b2g_rep = dram("b2g_rep", [128, 16], F32, "ExternalInput")
    lt_row = dram("lt_row", [1, 16], F32, "ExternalInput")
    fp_row = dram("fp_row", [1, 16], F32, "ExternalInput")
    onw = dram("onw", [128, 1024], F32, "ExternalInput")
    identD = dram("identc", [128, 128], BF16, "ExternalInput")
    ones1D = dram("ones1c", [1, 128], BF16, "ExternalInput")
    onescolD = dram("onescolc", [128, 1], BF16, "ExternalInput")
    ind1D = dram("ind1c", [1024, 4], BF16, "ExternalInput")
    mtriu1nD = dram("mtriu1n", [128, 128], BF16, "ExternalInput")
    mtril1nD = dram("mtril1n", [128, 128], BF16, "ExternalInput")
    mtriu0D = dram("mtriu0", [128, 128], BF16, "ExternalInput")
    fconstD = dram("fconsts", [128, 8], F32, "ExternalInput")
    colbD = dram("colbase", [1, 1], mybir.dt.int32, "ExternalInput")

    agin = dram("agin", [1024, 4096], F32, "Internal")
    agout = nc.dram_tensor("agout", [8192, 4096], F32, kind="Internal", addr_space="Shared").ap()
    inpT = dram("inpT", [1024, 3072], F32, "ExternalInput")
    inpB = dram("inpB", [128, 24], F32, "ExternalInput")
    outwT = dram("outwT", [1024, 1024], F32, "ExternalInput")
    outB = dram("outB", [128, 8], F32, "ExternalInput")
    ln1g = dram("ln1g", [128, 8], F32, "ExternalInput")
    ln1b = dram("ln1b", [128, 8], F32, "ExternalInput")
    ln2g = dram("ln2g", [128, 8], F32, "ExternalInput")
    ln2b = dram("ln2b", [128, 8], F32, "ExternalInput")
    rwT = dram("rwT", [1024, 8], F32, "ExternalInput")
    rbrep = dram("rbrep", [128, 8], F32, "ExternalInput")
    w1pT = dram("w1pT", [8, 1024, 1024], F32, "ExternalInput")
    w2pT = dram("w2pT", [8, 1024, 1024], F32, "ExternalInput")
    b1p2 = dram("b1p2", [128, 8, 8], F32, "ExternalInput")
    b2p2 = dram("b2p2", [8, 1024], F32, "ExternalInput")
    blockmD = dram("blockm", [128, 128], BF16, "ExternalInput")
    repmD = dram("repm", [8, 128], BF16, "ExternalInput")
    onehotsD = dram("onehots", [8, 8, 128], BF16, "ExternalInput")
    idxcD = dram("idxc", [128, 8], F32, "ExternalInput")
    scr1 = dram("scr1", [8, 128, 8, 1024], BF16, "Internal")
    scr2 = dram("scr2", [8, 8, 128, 8, 128], BF16, "Internal")
    scrI = dram("scrI", [24, 128, 8, 128], BF16, "Internal")
    scrO = dram("scrO", [8, 128, 8, 128], BF16, "Internal")
    xloc = dram("xloc", [8192, 512], F32, "Internal")
    out_fm = dram("out_fm", [1024, 4096], F32, "ExternalOutput")

    dbg_out = {}
    if dbg == "qkv":
        for n in ("dq", "dk", "dv", "dxh"):
            dbg_out[n] = dram(n, [1024, 4096], BF16, "ExternalOutput")
    if dbg in ("x1", "x1s"):
        dbg_out["dx1"] = dram("dx1", [1024, 4096], F32, "ExternalOutput")
    if dbg == "x1s":
        for n in ("dps", "dpl"):
            dbg_out[n] = dram(n, [1024, 4096], BF16, "ExternalOutput")
        dbg_out["ddel"] = dram("ddel", [4096, 1024], BF16, "ExternalOutput")
        dbg_out["dsr"] = dram("dsr", [64, 4096], BF16, "ExternalOutput")
        dbg_out["dS0"] = dram("dS0", [128, 2048], F32, "ExternalOutput")
        dbg_out["dua"] = dram("dua", [128, 256], BF16, "ExternalOutput")
        dbg_out["dkh"] = dram("dkh", [128, 256], BF16, "ExternalOutput")
        dbg_out["dvb"] = dram("dvb", [128, 256], BF16, "ExternalOutput")
        dbg_out["dkraw"] = dram("dkraw", [128, 256], BF16, "ExternalOutput")
        dbg_out["dkh2"] = dram("dkh2", [128, 128], BF16, "ExternalOutput")
        dbg_out["dwt"] = dram("dwt", [4096, 16], F32, "ExternalOutput")

    ctx1 = ExitStack()
    sbg = lambda n, sh, dt: ctx.enter_context(nc.sbuf_tensor(n, sh, dt))
    sb = lambda n, sh, dt: ctx1.enter_context(nc.sbuf_tensor(n, sh, dt))
    ps = lambda n, sh, dt: ctx.enter_context(nc.psum_tensor(n, sh, dt))

    pb = [Bank(ps(f"pb{i}", [128, 512], F32)) for i in range(6)]
    pbb = [Bank(ps(f"pbb{i}", [128, 1024], BF16)) for i in range(2)]

    ident = sbg("ident", [128, 128], BF16)
    ones1 = sbg("ones1", [1, 128], BF16)
    onescol = sbg("onescol", [128, 1], BF16)
    mtriu1n = sbg("mtriu1n_s", [128, 128], BF16)
    mtril1n = sbg("mtril1n_s", [128, 128], BF16)
    mtriu0 = sbg("mtriu0_s", [128, 128], BF16)
    fconst = sbg("fconst", [128, 8], F32)
    colb = sbg("colb", [1, 1], mybir.dt.int32)
    for _i, _v in enumerate(FCONST_VALS):
        nc.const_aps.register(F32, _v, fconst[:, _i:_i + 1])
    ind1 = sb("ind1", [128, 8, 4], BF16)
    ctaps = {n: sb(f"ct_{n}", [128, 8, 4], F32) for n in ("convq", "convk", "convv")}
    ftapS = sb("ftapS", [128, 8, 5], F32)
    ftapL = sb("ftapL", [128, 8, 63], F32)
    b1sb = sb("b1sb", [128, 8], F32)
    b2gr = sb("b2gr", [128, 16], F32)
    onw16 = sb("onw16", [128, 1024], BF16)
    trow = sb("trow", [1, 16], F32)
    trow2 = sb("trow2", [1, 16], F32)
    trow3 = sb("trow3", [1, 16], F32)
    trow4 = sb("trow4", [1, 4], F32)
    trow4b = sb("trow4b", [1, 16], F32)
    trowb = sb("trowb", [1, 16], BF16)
    invtau_rep = sb("invtau_rep", [128, 16], F32)
    nf_rep = sb("nf_rep", [128, 16], F32)
    w1tail = [sb(f"w1tail{g}", [16, 1024], BF16) for g in range(4)]

    wslot = [sb(f"wslot{i}", [128, 8, 1024], BF16) for i in range(2)]
    wslot_cons = [[], []]

    xres = sb("xres", [128, 512], F32)
    xo16 = sb("xo16", [128, 8, 512], BF16)
    xhat = sb("xhat", [128, 8, 512], BF16)
    raw = sb("raw", [128, 8, 516], BF16)
    q16 = sb("q16", [128, 8, 512], BF16)
    k16 = sb("k16", [128, 8, 512], BF16)
    v16 = sb("v16", [128, 8, 576], BF16)
    v16o = sb("v16o", [128, 8, 576], BF16)
    ps16 = sb("ps16_", [128, 8, 512], BF16)
    pl16 = sb("pl16_", [128, 8, 512], BF16)
    gacc = sb("gacc", [128, 8, 512], BF16)
    delta16 = sb("delta16", [128, 1024], BF16)
    rawtail = {n: sb(f"rtail_{n}", [128, 8, 4], BF16) for n in ("q", "k", "v")}
    attnout = sb("attnout", [128, 8, 512], BF16)
    gp = sb("gp", [128, 7, 128], BF16)
    ap_ = sb("ap_", [128, 6, 128], BF16)
    msb = sb("msb", [128, 128], BF16)
    attnT = sb("attnT", [128, 128], BF16)
    khat = sb("khat", [128, 256], BF16)
    vb16 = sb("vb16", [128, 256], BF16)
    uadj = sb("uadj", [128, 256], BF16)
    S32 = sb("S32", [128, 8, 256], F32)
    S16 = sb("S16", [128, 8, 256], BF16)
    stm = sb("stm", [128, 16], F32)
    stm16 = sb("stm16", [128, 16], BF16)
    statrow = [sb(f"strow{p}", [16, 128], BF16) for p in range(4)]
    sqs = sb("sqs", [128, 256], F32)
    pt16 = sb("pt16", [128, 3, 1024], BF16)
    beta_tm = sb("beta_tm", [128, 4, 4], F32)
    rq_tm = sb("rq_tm", [128, 4, 4], F32)
    rk_tm = sb("rk_tm", [128, 4, 4], F32)
    nbrk_tm = sb("nbrk_tm", [128, 4, 4], F32)
    brk_tm = sb("brk_tm", [128, 4, 4], F32)
    row4a = sb("row4a", [1, 2048], F32)
    row4b = sb("row4b", [1, 2048], F32)
    rkrow16 = sb("rkrow16", [1, 2048], BF16)
    brkrow16 = sb("brkrow16", [1, 2048], BF16)
    wts = sb("wts", [128, 4, 16], F32)
    o16 = sb("o16", [128, 1024], BF16)
    ss4 = sb("ss4", [128, 4], F32)
    rstd4 = sb("rstd4", [128, 4], F32)
    lgt = sb("lgt", [128, 16], F32)
    r_s1 = row4a[0:1, 0:512]
    r_s2 = row4a[0:1, 512:1024]
    r_t1 = row4a[0:1, 1024:1536]
    r_t2 = row4a[0:1, 1536:2048]
    r_b1 = sb("r_b1", [1, 512], BF16)
    r_b2 = sb("r_b2", [1, 512], BF16)
    rstd0 = sb("rstd0", [128, 512], BF16)
    mrs0 = sb("mrs0", [128, 512], BF16)
    rstdA = sb("rstdA", [128, 512], BF16)
    mrsA = sb("mrsA", [128, 512], BF16)
    x1c = sb("x1c", [128, 512], F32)
    stg = [sb(f"stg{i}", [128, 1024], BF16) for i in range(1)]
    cacc = sb("cacc", [128, 512], BF16)
    env = dict(locals())
    return _build_rest(env)


def bc(ap3, shape):
    return ap3.to_broadcast(shape)


def _build_rest(E):
    from types import SimpleNamespace
    e = SimpleNamespace(**E)
    nc, S, ctx = e.nc, e.S, e.ctx
    dbg = e.dbg

    cl = []

    def ld(dst, src, eng="sp"):
        t = S.dma(eng, lambda h, d=dst, s=src: h.dma_start(out=d, in_=s))
        cl.append(t)
        return t

    ld(e.ident[:], e.identD)
    ld(e.ones1[:], e.ones1D)
    ld(e.onescol[:], e.onescolD)
    ld(e.ind1[:], e.ind1D.rearrange("(c p) f -> p c f", p=128))
    ld(e.mtriu1n[:], e.mtriu1nD)
    ld(e.mtril1n[:], e.mtril1nD)
    ld(e.mtriu0[:], e.mtriu0D)
    for n in ("convq", "convk", "convv"):
        ld(e.ctaps[n][:], e.convw[n].rearrange("(c p) f -> p c f", p=128))
    ld(e.ftapS[:], e.firS.rearrange("(c p) f -> p c f", p=128))
    ld(e.ftapL[:], e.firL.rearrange("(c p) f -> p c f", p=128))
    ld(e.b1sb[:], e.b1d)
    ld(e.b2gr[:], e.b2g_rep)
    ld(e.onw16[:], e.onw, eng="pool")
    wb16 = e.ctx1.enter_context(nc.sbuf_tensor("wb16", [128, 8, 4], BF16))
    w2g16 = e.ctx1.enter_context(nc.sbuf_tensor("w2g16", [128, 8, 16], BF16))
    ld(wb16[:], e.wbT.rearrange("(c p) f -> p c f", p=128), eng="pool")
    ld(w2g16[:], e.w2gT.rearrange("(c p) f -> p c f", p=128), eng="pool")
    for g in range(4):
        ld(e.w1tail[g][:], e.w1T[1024 + 16 * g:1024 + 16 * (g + 1), :], eng="pool")
    ld(e.fconst[:], e.fconstD)
    ld(e.colb[:], e.colbD)
    S.prologue = cl[:]
    for _e in ENGS:
        for _o in S.ops[_e]:
            S.preprologue.add(id(_o))
    tl = ld(e.trow[:], e.lt_row)
    tf = ld(e.trow2[:], e.fp_row)

    a1 = S.add("act", lambda h: h.activation(out=e.trow3[:], in_=e.trow[:], func=AF.Abs), after=[tl])
    a2 = S.add("act", lambda h: h.activation(out=e.trow4b[:], in_=e.trow3[:], func=AF.Exp, scale=-1.0), after=[a1])
    a3 = S.add("act", lambda h: h.activation(out=e.trow3[:], in_=e.trow4b[:], func=AF.Ln, bias=1.0), after=[a2])
    a4 = S.add("dve", lambda h: h.tensor_scalar_max(out=e.trow[:], in0=e.trow[:], scalar1=0.0), after=[tl, a1])
    a5 = S.add("dve", lambda h: h.tensor_add(out=e.trow[:], in0=e.trow[:], in1=e.trow3[:]), after=[a3, a4])
    a6 = S.add("dve", lambda h: h.tensor_scalar_add(out=e.trow[:], in0=e.trow[:], scalar1=0.25), after=[a5])
    a7 = S.add("dve", lambda h: h.reciprocal(out=e.trow[:], in_=e.trow[:]), after=[a6])
    a8 = S.add("dve", lambda h: h.tensor_copy(out=e.trowb[:], in_=e.trow[:]), after=[a7])
    b1_ = S.add("act", lambda h: h.activation(out=e.trow2[:], in_=e.trow2[:], func=AF.Sigmoid), after=[tf])
    b2_ = S.add("dve", lambda h: h.tensor_scalar(out=e.trow2[:], in0=e.trow2[:], scalar1=0.07, scalar2=0.005, op0=ALU.mult, op1=ALU.add), after=[b1_])
    b3_ = S.add("dve", lambda h: h.tensor_scalar_min(out=e.trow2[:], in0=e.trow2[:], scalar1=0.25), after=[b2_])
    b4_ = S.add("dve", lambda h: h.tensor_reduce(out=e.trow4[:], in_=e.trow2[:].rearrange("o (g f) -> o g f", f=4), axis=AX.X, op=ALU.add), after=[b3_])
    b5_ = S.add("dve", lambda h: h.tensor_scalar_min(out=e.trow4[:], in0=e.trow4[:], scalar1=0.99), after=[b4_])
    b6_ = S.add("dve", lambda h: h.reciprocal(out=e.trow4[:], in_=e.trow4[:]), after=[b5_])
    b7_ = S.add("dve", lambda h: h.tensor_tensor(out=e.trow2[:].rearrange("o (g f) -> o g f", f=4), in0=e.trow2[:].rearrange("o (g f) -> o g f", f=4), in1=bc(e.trow4[:].unsqueeze(2), (1, 4, 4)), op=ALU.mult), after=[b6_])
    b8_ = S.add("dve", lambda h: h.tensor_scalar(out=e.trow2[:], in0=e.trow2[:], scalar1=0.99, scalar2=1e-9, op0=ALU.mult, op1=ALU.add), after=[b7_])
    b9_ = S.add("dve", lambda h: h.tensor_copy(out=e.trow3[:], in_=e.trow2[:]), after=[b8_])
    nfb = e.ctx1.enter_context(nc.sbuf_tensor("nfb16", [1, 16], BF16))
    b10 = S.add("dve", lambda h: h.tensor_copy(out=nfb[:], in_=e.trow3[:]), after=[b9_])
    t1 = pe_acc_sl(S, e.pb[0], e.pb[0].ap[:, 0:16], [(e.ones1[:], e.trowb[:], [a8] + cl[:1])])
    r1 = rd(S, "act", e.pb[0], lambda h: h.copy(out=e.invtau_rep[:], in_=e.pb[0].ap[:, 0:16]))
    t2 = pe_acc_sl(S, e.pb[1], e.pb[1].ap[:, 0:16], [(e.ones1[:], nfb[:], [b10])])
    r2 = rd(S, "act", e.pb[1], lambda h: h.copy(out=e.nf_rep[:], in_=e.pb[1].ap[:, 0:16]))
    boot = [r1, r2] + cl

    bootq = []

    def cast_pair(src_f32, dst_bf16, stage_cols=1024):
        def mk(slot, src_f32=src_f32, dst_bf16=dst_bf16, stage_cols=stage_cols):
            lf = lambda h: h.dma_start(out=e.stg[slot][:, 0:stage_cols], in_=src_f32)
            shp = dst_bf16.shape
            src2 = e.stg[slot][:, 0:stage_cols]
            dst2 = dst_bf16
            if len(shp) == 3:
                src2 = src2.rearrange("p (d t) -> p d t", d=shp[0])
                dst2 = dst_bf16.rearrange("d p t -> p d t")
            sf = lambda h: h.dma_start(out=dst2, in_=src2)
            return lf, sf
        bootq.append(mk)

    for ee in range(8):
        for kcc in range(8):
            cast_pair(e.w1pT[ee, kcc * 128:(kcc + 1) * 128, :], e.scr1[ee, :, kcc, :])
    for ee in range(8):
        for kcc in range(8):
            cast_pair(e.w2pT[ee, kcc * 128:(kcc + 1) * 128, :],
                      e.scr2[ee, :, :, kcc, :].rearrange("d p t -> p d t") if False else e.scr2[ee, :, :, kcc, :])
    for kcc in range(8):
        for gg in range(3):
            cast_pair(e.inpT[kcc * 128:(kcc + 1) * 128, gg * 1024:(gg + 1) * 1024],
                      e.scrI[gg * 8:(gg + 1) * 8, :, kcc, :])
    for kcc in range(8):
        cast_pair(e.outwT[kcc * 128:(kcc + 1) * 128, :], e.scrO[:, :, kcc, :])

    boot_state = {"i": 0, "slot_prev": [None], "done": []}

    def drain_boot(n):
        for _ in range(n):
            i = boot_state["i"]
            if i >= len(bootq):
                return
            boot_state["i"] = i + 1
            slot = 0
            lf, sf = bootq[i](slot)
            prev = boot_state["slot_prev"][slot]
            lt = S.dma("pool", lf, after=([prev] if prev else []))
            st_ = S.dma("pool", sf, after=[lt])
            boot_state["slot_prev"][slot] = st_
            boot_state["done"].append(st_)

    x1_all = []

    pb, pbb = e.pb, e.pbb
    wslot_cons = [[], []]
    wload = [None, None]
    S16_last = {}

    def load_w(slot, srcT, nchunks=8):
        after = wslot_cons[slot][:]
        wslot_cons[slot] = []
        t = S.dma("pool", lambda h, d=e.wslot[slot][:, 0:nchunks, :],
                  s=srcT[0:128 * nchunks, :].rearrange("(c p) o -> p c o", p=128):
                  h.dma_start(out=d, in_=s), after=after)
        wload[slot] = t
        return t

    def use_w(slot):
        def dep(t):
            wslot_cons[slot].append(t)
        return wload[slot], dep

    NT = 512

    def ones_mm_rows(x16ap_chunks, bank, deps):
        pairs = [(e.onescol[:], x16ap_chunks[dc], deps) for dc in range(8)]
        return pe_acc_sl(S, bank, bank.ap[0:1, :], pairs)

    rows_done = [[]]

    def ln_rows(bank1, bank2, eps, outb1, outb2, extra=()):
        t1 = rd(S, "dve", bank1, lambda h: h.tensor_scalar_mul(out=e.r_t1, in0=bank1.ap[0:1, :], scalar1=1.0 / 1024), extra)
        t2 = S.add("act", lambda h: h.activation(out=e.r_t2, in_=e.r_t1, func=AF.Square), after=[t1])
        t3 = rd(S, "dve", bank2, lambda h: h.scalar_tensor_tensor(out=e.r_t2, in0=bank2.ap[0:1, :], scalar=1.0 / 1024, in1=e.r_t2, op0=ALU.mult, op1=ALU.subtract), [t2])
        t4 = S.add("act", lambda h, ep=eps: h.activation(out=e.r_t2, in_=e.r_t2, func=AF.Sqrt, bias=ep), after=[t3])
        t5 = S.add("dve", lambda h: h.reciprocal(out=e.r_t2, in_=e.r_t2), after=[t4])
        t6 = S.add("dve", lambda h: h.tensor_mul(out=e.r_t1, in0=e.r_t1, in1=e.r_t2), after=[t5])
        t7 = S.add("dve", lambda h: h.tensor_scalar_mul(out=e.r_t1, in0=e.r_t1, scalar1=-1.0), after=[t6])
        t8 = S.add("dve", lambda h: h.tensor_copy(out=e.r_b1[:], in_=e.r_t2), after=[t5])
        t9 = S.add("dve", lambda h: h.tensor_copy(out=e.r_b2[:], in_=e.r_t1), after=[t7])
        rows_done[0] = [t8, t9]
        tb1 = pe_acc(S, pb[4], [(e.ones1[:], e.r_b1[:], [t8])])
        o1 = rd(S, "act", pb[4], lambda h: h.copy(out=outb1, in_=pb[4].ap[:]))
        tb2 = pe_acc(S, pb[5], [(e.ones1[:], e.r_b2[:], [t9])])
        o2 = rd(S, "act", pb[5], lambda h: h.copy(out=outb2, in_=pb[5].ap[:]))
        return o1, o2

    x16v = e.xo16
    dbg_ts = []

    for s in range(NST):
        t0 = s * NT
        first_st = (s == 0)
        stdeps = [] if first_st else [bar_t]

        tx16 = S.dma("pool", lambda h, t0=t0: h.dma_start(
            out=x16v[:], in_=e.x_fm.rearrange("(c p) t -> p c t", p=128)[:, :, t0:t0 + NT]),
            after=boot + stdeps)
        if first_st:
            vh = S.add("dve", lambda h: h.memset(e.v16[:, :, 0:64], 0.0), after=stdeps)
        else:
            vh = S.add("dve", lambda h: h.tensor_copy(out=e.v16[:, :, 0:64], in_=e.v16[:, :, 512:576]), after=stdeps)

        sqt = [S.add("act", lambda h, dc=dc: h.activation(out=e.gacc[:, dc, :], in_=x16v[:, dc, :], func=AF.Square), after=[tx16] + stdeps) for dc in range(8)]
        s1t = ones_mm_rows([x16v[:, dc, :] for dc in range(8)], pb[0], [tx16])
        s2t = ones_mm_rows([e.gacc[:, dc, :] for dc in range(8)], pb[1], sqt)
        lr1, lr2 = ln_rows(pb[0], pb[1], 1e-5, e.rstd0[:], e.mrs0[:])
        xh_t = []
        for dc in range(8):
            u1 = S.add("dve", lambda h, dc=dc: h.tensor_mul(out=e.xhat[:, dc, :], in0=x16v[:, dc, :], in1=e.rstd0[:]), after=[lr1, tx16] + stdeps)
            u2 = S.add("dve", lambda h, dc=dc: h.tensor_add(out=e.xhat[:, dc, :], in0=e.xhat[:, dc, :], in1=e.mrs0[:]), after=[lr2, u1])
            xh_t.append(u2)

        if dbg == "qkv":
            dbg_ts.append(S.dma("sp", lambda h, t0=t0: h.dma_start(
                out=e.dbg_out["dxh"].rearrange("(c p) t -> p c t", p=128)[:, :, t0:t0 + NT],
                in_=e.xhat[:]), after=xh_t))

        def proj(name, wname, slot, outbuf, out_off, rawtail, ctap, prev_conv_ts):
            lw = load_w(slot, e.wT[wname])
            if first_st:
                hh = S.add("dve", lambda h: h.memset(e.raw[:, :, 0:4], 0.0), after=prev_conv_ts)
            else:
                hh = S.add("dve", lambda h, rt=rawtail: h.tensor_copy(out=e.raw[:, :, 0:4], in_=rt[:]), after=prev_conv_ts + stdeps)
            ev = []
            for dc in range(8):
                bank = pb[dc % 4]
                _, dep = use_w(slot)
                pairs = [(e.wslot[slot][:, kc, dc * 128:(dc + 1) * 128], e.xhat[:, kc, :], [lw, xh_t[kc]]) for kc in range(8)]
                mt = pe_acc(S, bank, pairs, extra_after=prev_conv_ts)
                dep(mt)
                ev.append(rd(S, "act", bank, lambda h, dc=dc, bap=bank.ap: h.copy(out=e.raw[:, dc, 4:516], in_=bap[:, 0:512])))
            st_ = S.add("dve", lambda h, rt=rawtail: h.tensor_copy(out=rt[:], in_=e.raw[:, :, 512:516]), after=ev + [hh])
            conv_ts = [st_]
            for dc in range(8):
                tt = S.add("dve", lambda h, dc=dc: h.tensor_scalar_mul(
                    out=e.cacc[:, 0:512], in0=e.raw[:, dc, 4:516], scalar1=ctap[:, dc, 3:4]),
                    after=[ev[dc], hh] + ([] if not conv_ts else [conv_ts[-1]]))
                for j in (1, 2, 3):
                    tt = S.add("dve", lambda h, dc=dc, j=j: h.scalar_tensor_tensor(
                        out=e.cacc[:, 0:512], in0=e.raw[:, dc, 4 - j:516 - j], scalar=ctap[:, dc, 3 - j:4 - j],
                        in1=e.cacc[:, 0:512], op0=ALU.mult, op1=ALU.add), after=[tt])
                si = S.add("act", lambda h, dc=dc, oo=out_off: h.activation(
                    out=outbuf[:, dc, oo:oo + 512], in_=e.cacc[:, 0:512], func=AF.Silu), after=[tt])
                conv_ts.append(si)
            return conv_ts[1:]

        qc_ts = proj("q", "wqT", 0, e.q16, 0, e.rawtail["q"], e.ctaps["convq"], [])
        kc_ts = proj("k", "wkT", 1, e.k16, 0, e.rawtail["k"], e.ctaps["convk"], qc_ts)
        vc_ts = proj("v", "wvT", 0, e.v16, 64, e.rawtail["v"], e.ctaps["convv"], kc_ts)
        bt_ts = []
        for tc in range(4):
            bank = pb[tc % 2]
            pairs = [(e.xhat[:, kc, tc * 128:(tc + 1) * 128], wb16[:, kc, :], [xh_t[kc]] + boot) for kc in range(8)]
            mt = pe_acc_sl(S, bank, bank.ap[:, 0:4], pairs)
            bt_ts.append(rd(S, "act", bank, lambda h, tc=tc, bap=bank.ap: h.activation(
                out=e.beta_tm[:, tc, :], in_=bap[:, 0:4], func=AF.Sigmoid)))

        def l2pass(src16, out_tm, conv_ts):
            sq_ts = [S.add("act", lambda h, dc=dc: h.activation(out=e.gacc[:, dc, :], in_=src16[:, dc, :], func=AF.Square), after=conv_ts) for dc in range(8)]
            outs = []
            for tc in range(4):
                bank = pb[2 + tc % 2]
                pairs = [(e.gacc[:, kc, tc * 128:(tc + 1) * 128], e.ind1[:, kc, :], [sq_ts[kc]]) for kc in range(8)]
                mt = pe_acc_sl(S, bank, bank.ap[:, 0:4], pairs)
                gacc_rd.append(mt)
                q1 = rd(S, "act", bank, lambda h, tc=tc, bap=bank.ap: h.activation(
                    out=out_tm[:, tc, :], in_=bap[:, 0:4], func=AF.Sqrt, bias=1e-6))
                q2 = S.add("dve", lambda h, tc=tc: h.reciprocal(out=out_tm[:, tc, :], in_=out_tm[:, tc, :]), after=[q1])
                outs.append(q2)
            return sq_ts, outs

        gacc_rd = [s2t]
        _, rq_ts = l2pass(e.q16, e.rq_tm, qc_ts)
        ksq_ts, rk_ts = l2pass(e.k16, e.rk_tm, kc_ts)
        ev_k, ev_b = [], []
        for hh_ in range(4):
            pairs = [(e.ind1[:, kc, hh_:hh_ + 1], e.gacc[:, kc, :], [ksq_ts[kc]]) for kc in range(8)]
            mt = pe_acc_sl(S, pb[4], pb[4].ap[0:1, :], pairs)
            gacc_rd.append(mt)
            ev_k.append(rd(S, "act", pb[4], lambda h, hh_=hh_: h.copy(
                out=e.row4a[0:1, hh_ * 512:(hh_ + 1) * 512], in_=pb[4].ap[0:1, 0:512]), rows_done[0]))
            pairs = [(wb16[:, kc, hh_:hh_ + 1], e.xhat[:, kc, :], [xh_t[kc]]) for kc in range(8)]
            mt = pe_acc_sl(S, pb[5], pb[5].ap[0:1, :], pairs)
            ev_b.append(rd(S, "act", pb[5], lambda h, hh_=hh_: h.activation(
                out=e.row4b[0:1, hh_ * 512:(hh_ + 1) * 512], in_=pb[5].ap[0:1, 0:512], func=AF.Sigmoid)))
        kr1 = S.add("act", lambda h: h.activation(out=e.row4a[:], in_=e.row4a[:], func=AF.Sqrt, bias=1e-6), after=ev_k)
        kr2 = S.add("dve", lambda h: h.reciprocal(out=e.row4a[:], in_=e.row4a[:]), after=[kr1])
        kr3 = S.add("dve", lambda h: h.tensor_copy(out=e.rkrow16[:], in_=e.row4a[:]), after=[kr2])
        br2 = S.add("dve", lambda h: h.tensor_mul(out=e.row4b[:], in0=e.row4b[:], in1=e.row4a[:]), after=ev_b + [kr2])
        br3 = S.add("dve", lambda h: h.tensor_copy(out=e.brkrow16[:], in_=e.row4b[:]), after=[br2])
        sc_ts = []
        for tc in range(4):
            c1 = S.add("dve", lambda h, tc=tc: h.tensor_mul(out=e.brk_tm[:, tc, :], in0=e.beta_tm[:, tc, :], in1=e.rk_tm[:, tc, :]), after=[bt_ts[tc], rk_ts[tc]])
            c2 = S.add("dve", lambda h, tc=tc: h.tensor_scalar_mul(out=e.nbrk_tm[:, tc, :], in0=e.brk_tm[:, tc, :], scalar1=-1.0), after=[c1])
            sc_ts.append(c2)

        vo_t = S.add("dve", lambda h: h.tensor_copy(out=e.v16o[:, :, 1:576], in_=e.v16[:, :, 0:575]), after=vc_ts + [vh])
        fir_ts = {"s": [], "l": []}
        for dc in range(8):
            tt = S.add("dve", lambda h, dc=dc: h.tensor_scalar_mul(
                out=e.ps16[:, dc, :], in0=e.v16[:, dc, 64:576], scalar1=e.ftapS[:, dc, 4:5]),
                after=vc_ts + [vh])
            for j in range(1, 5):
                src = e.v16 if j % 2 == 0 else e.v16o
                off = 64 - j if j % 2 == 0 else 65 - j
                tt = S.add("dve", lambda h, dc=dc, j=j, src=src, off=off: h.scalar_tensor_tensor(
                    out=e.ps16[:, dc, :], in0=src[:, dc, off:off + 512], scalar=e.ftapS[:, dc, 4 - j:5 - j],
                    in1=e.ps16[:, dc, :], op0=ALU.mult, op1=ALU.add), after=[tt, vo_t])
            fir_ts["s"].append(tt)
        lacc_prev = []
        for dc in range(8):
            gts = []
            for g in range(8):
                ntap = 8 if g < 7 else 7
                tt = None
                for jj in range(ntap):
                    j = g * 8 + jj
                    src = e.v16 if j % 2 == 0 else e.v16o
                    off = 64 - j if j % 2 == 0 else 65 - j
                    if jj == 0:
                        tt = S.add("dve", lambda h, dc=dc, g=g, j=j, src=src, off=off: h.tensor_scalar_mul(
                            out=e.gacc[:, g, :], in0=src[:, dc, off:off + 512], scalar1=e.ftapL[:, dc, 62 - j:63 - j]),
                            after=vc_ts + [vo_t] + lacc_prev + gacc_rd)
                    else:
                        tt = S.add("dve", lambda h, dc=dc, g=g, j=j, src=src, off=off: h.scalar_tensor_tensor(
                            out=e.gacc[:, g, :], in0=src[:, dc, off:off + 512], scalar=e.ftapL[:, dc, 62 - j:63 - j],
                            in1=e.gacc[:, g, :], op0=ALU.mult, op1=ALU.add), after=[tt])
                gts.append(tt)
            c1 = S.add("dve", lambda h: h.tensor_add(out=e.gacc[:, 0:4, :], in0=e.gacc[:, 0:4, :], in1=e.gacc[:, 4:8, :]), after=gts)
            c2 = S.add("dve", lambda h: h.tensor_add(out=e.gacc[:, 0:2, :], in0=e.gacc[:, 0:2, :], in1=e.gacc[:, 2:4, :]), after=[c1])
            c3 = S.add("dve", lambda h, dc=dc: h.tensor_add(out=e.pl16[:, dc, :], in0=e.gacc[:, 0, :], in1=e.gacc[:, 1, :]), after=[c2])
            lacc_prev = [c3]
            fir_ts["l"].append(c3)

        wstat_rd = []
        x16_rd = sqt + xh_t + [s1t]
        oevac_ts = []
        wstat_ts = []
        l1w = load_w(1, e.w1T)
        h1_ts = []
        scan_guard = []
        sqs_guard = []
        stm_guard = []
        for c in range(4):
            ccols = slice(c * 128, (c + 1) * 128)
            od_list = []
            for hh_ in range(4):
                FIRST = first_st and c == 0
                kq_dep = [qc_ts[2 * hh_], qc_ts[2 * hh_ + 1], kc_ts[2 * hh_], kc_ts[2 * hh_ + 1]]
                mM = pe_acc_sl(S, pb[0], pb[0].ap[:, 0:128],
                               [(e.k16[:, 2 * hh_ + kc, ccols], e.k16[:, 2 * hh_ + kc, ccols], kq_dep) for kc in (0, 1)])
                eM = rd(S, "act", pb[0], lambda h, bap=pb[0].ap: h.copy(out=e.msb[:], in_=bap[:, 0:128]), scan_guard)
                bb1 = pe_acc_sl(S, pb[2], pb[2].ap[:, 0:128],
                                [(e.ones1[:], e.brkrow16[0:1, hh_ * 512 + c * 128: hh_ * 512 + (c + 1) * 128], [br3])])
                g1 = rd(S, "dve", pb[2], lambda h, bap=pb[2].ap: h.tensor_mul(out=e.gp[:, 0, :], in0=e.msb[:], in1=bap[:, 0:128]), [eM] + scan_guard)
                g2 = S.add("dve", lambda h: h.tensor_mul(out=e.gp[:, 0, :], in0=e.gp[:, 0, :], in1=e.mtriu1n[:]), after=[g1])
                g3 = S.add("dve", lambda h, c=c, hh_=hh_: h.tensor_scalar_mul(out=e.gp[:, 0, :], in0=e.gp[:, 0, :], scalar1=e.rk_tm[:, c, hh_:hh_ + 1]), after=[g2, rk_ts[c]])
                bb2 = pe_acc_sl(S, pb[2], pb[2].ap[:, 0:128],
                                [(e.ones1[:], e.rkrow16[0:1, hh_ * 512 + c * 128: hh_ * 512 + (c + 1) * 128], [kr3])])
                a1_ = rd(S, "dve", pb[2], lambda h, bap=pb[2].ap: h.tensor_mul(out=e.ap_[:, 0, :], in0=e.msb[:], in1=bap[:, 0:128]), [eM] + scan_guard)
                a2_ = S.add("dve", lambda h: h.tensor_mul(out=e.ap_[:, 0, :], in0=e.ap_[:, 0, :], in1=e.mtril1n[:]), after=[a1_])
                a3_ = S.add("dve", lambda h, c=c, hh_=hh_: h.tensor_scalar_mul(out=e.ap_[:, 0, :], in0=e.ap_[:, 0, :], scalar1=e.brk_tm[:, c, hh_:hh_ + 1]), after=[a2_, sc_ts[c]])
                mN = pe_acc_sl(S, pb[1], pb[1].ap[:, 0:128],
                               [(e.k16[:, 2 * hh_ + kc, ccols], e.q16[:, 2 * hh_ + kc, ccols], kq_dep) for kc in (0, 1)])
                eN = rd(S, "act", pb[1], lambda h, bap=pb[1].ap: h.copy(out=e.msb[:], in_=bap[:, 0:128]), [g1, a1_])
                at1 = S.add("dve", lambda h: h.tensor_mul(out=e.attnT[:], in0=e.msb[:], in1=e.mtriu0[:]), after=[eN] + scan_guard)
                at2 = S.add("dve", lambda h, c=c, hh_=hh_: h.tensor_scalar_mul(out=e.attnT[:], in0=e.attnT[:], scalar1=e.rk_tm[:, c, hh_:hh_ + 1]), after=[at1, rk_ts[c]])
                gprev, aprev = g3, a3_
                for lv in range(1, 7):
                    mg = pe_acc_sl(S, pb[2], pb[2].ap[:, 0:128],
                                   [(e.ap_[:, lv - 1, :], e.gp[:, lv - 1, :], [gprev, aprev])])
                    eg = rd(S, "act", pb[2], lambda h, lv=lv, bap=pb[2].ap: h.copy(out=e.gp[:, lv, :], in_=bap[:, 0:128]))
                    if lv < 6:
                        ma = pe_acc_sl(S, pb[3], pb[3].ap[:, 0:128],
                                       [(e.gp[:, lv - 1, :], e.ap_[:, lv - 1, :], [gprev, aprev])])
                        ea = rd(S, "act", pb[3], lambda h, lv=lv, bap=pb[3].ap: h.copy(out=e.ap_[:, lv, :], in_=bap[:, 0:128]))
                        aprev = ea
                    gprev = eg
                gchain = gprev
                tr_ts = []
                for kc in (0, 1):
                    bankb = pbb[kc]
                    tv = S.add("pe", lambda h, kc=kc, hh_=hh_, c=c, bap=bankb.ap: h.transpose(
                        bap[:, 128:256], e.v16[:, 2 * hh_ + kc, 64 + c * 128:64 + (c + 1) * 128], e.ident[:]),
                        after=vc_ts + bankb.war() + cl[:1] + scan_guard)
                    bankb.writer = tv
                    ev_ = rd(S, "act", bankb, lambda h, kc=kc, c=c, hh_=hh_, bap=bankb.ap: h.activation(
                        out=e.vb16[:, kc * 128:(kc + 1) * 128], in_=bap[:, 128:256], func=AF.Copy,
                        scale=e.beta_tm[:, c, hh_:hh_ + 1]), [bt_ts[c]])
                    tk = S.add("pe", lambda h, kc=kc, hh_=hh_, c=c, bap=bankb.ap: h.transpose(
                        bap[:, 0:128], e.k16[:, 2 * hh_ + kc, c * 128:(c + 1) * 128], e.ident[:]),
                        after=kc_ts + bankb.war() + cl[:1] + scan_guard)
                    bankb.writer = tk
                    ek_ = rd(S, "act", bankb, lambda h, kc=kc, c=c, hh_=hh_, bap=bankb.ap: h.activation(
                        out=e.khat[:, kc * 128:(kc + 1) * 128], in_=bap[:, 0:128], func=AF.Copy,
                        scale=e.rk_tm[:, c, hh_:hh_ + 1]), [rk_ts[c]])
                    if dbg == "x1s" and first_st and c == 0 and hh_ == 0 and kc == 0:
                        ex2 = rd(S, "act", bankb, lambda h, bap=bankb.ap: h.copy(
                            out=e.o16[:, 0:128], in_=bap[:, 0:128]))
                        dbg_ts.append(S.dma("sp", lambda h: h.dma_start(
                            out=e.dbg_out["dkh2"][:], in_=e.o16[:, 0:128]), after=[ex2]))
                        tr_ts += [dbg_ts[-1]]
                    tr_ts += [ev_, ek_]
                if FIRST:
                    yt = S.add("dve", lambda h: h.tensor_copy(out=e.uadj[:], in_=e.vb16[:]), after=tr_ts + scan_guard)
                else:
                    sdep = [S16_last[(hh_, 0)], S16_last[(hh_, 1)]]
                    mz = pe_acc_sl(S, pb[4], pb[4].ap[:, 0:256],
                                   [(e.k16[:, 2 * hh_ + kc, ccols], e.S16[:, 2 * hh_ + kc, :], kq_dep + sdep) for kc in (0, 1)])
                    yt = rd(S, "dve", pb[4], lambda h, c=c, hh_=hh_, bap=pb[4].ap: h.scalar_tensor_tensor(
                        out=e.uadj[:], in0=bap[:, 0:256], scalar=e.nbrk_tm[:, c, hh_:hh_ + 1],
                        in1=e.vb16[:], op0=ALU.mult, op1=ALU.add), tr_ts + [sc_ts[c]] + scan_guard)
                owar = pb[5].war()
                if pb[5].writer is not None:
                    owar.append(pb[5].writer)
                sreaders = []
                oprev = None
                if not FIRST:
                    sdep = [S16_last[(hh_, 0)], S16_last[(hh_, 1)]]
                    m1 = S.add("pe", lambda h, hh_=hh_, c=c: h.matmul(
                        pb[5].ap[:, 0:256], e.q16[:, 2 * hh_ + 0, c * 128:(c + 1) * 128],
                        e.S16[:, 2 * hh_ + 0, :], start=True, stop=False),
                        after=kq_dep + sdep + owar)
                    m2 = S.add("pe", lambda h, hh_=hh_, c=c: h.matmul(
                        pb[5].ap[:, 0:256], e.q16[:, 2 * hh_ + 1, c * 128:(c + 1) * 128],
                        e.S16[:, 2 * hh_ + 1, :], start=False, stop=False), after=[m1])
                    sreaders += [m1, m2]
                    oprev = m2
                for lv in range(7):
                    mapp = pe_acc_sl(S, pb[4], pb[4].ap[:, 0:256],
                                     [(e.gp[:, lv, :], e.uadj[:], [yt, gchain])])
                    yt = rd(S, "dve", pb[4], lambda h, bap=pb[4].ap: h.tensor_add(
                        out=e.uadj[:], in0=bap[:, 0:256], in1=e.uadj[:]))
                mo = S.add("pe", lambda h, FIRST=FIRST: h.matmul(
                    pb[5].ap[:, 0:256], e.attnT[:], e.uadj[:], start=FIRST, stop=True),
                    after=[at2, yt] + ([oprev] if oprev is not None else owar))
                pb[5].writer = mo
                od = rd(S, "act", pb[5], lambda h, c=c, hh_=hh_, bap=pb[5].ap: h.activation(
                    out=e.delta16[:, hh_ * 256:(hh_ + 1) * 256], in_=bap[:, 0:256],
                    func=AF.Copy, scale=e.rq_tm[:, c, hh_:hh_ + 1]), [rq_ts[c]])
                od_list.append(od)
                scan_guard = [od, at2]
                if not FIRST:
                    sreaders.append(mz)
                for kc in (0, 1):
                    bank = pb[2 + kc]
                    mds = pe_acc_sl(S, bank, bank.ap[:, 0:256],
                                    [(e.khat[:, kc * 128:(kc + 1) * 128], e.uadj[:], [yt] + tr_ts)])
                    if FIRST:
                        evs = rd(S, "dve", bank, lambda h, hh_=hh_, kc=kc, bap=bank.ap: h.tensor_copy(
                            out=e.S32[:, 2 * hh_ + kc, :], in_=bap[:, 0:256]))
                    else:
                        evs = rd(S, "dve", bank, lambda h, hh_=hh_, kc=kc, bap=bank.ap: h.tensor_add(
                            out=e.S32[:, 2 * hh_ + kc, :], in0=bap[:, 0:256], in1=e.S32[:, 2 * hh_ + kc, :]),
                            [S16_last.get((hh_, kc))] if (hh_, kc) in S16_last else [])
                    s16t = S.add("dve", lambda h, hh_=hh_, kc=kc: h.tensor_copy(
                        out=e.S16[:, 2 * hh_ + kc, :], in_=e.S32[:, 2 * hh_ + kc, :]),
                        after=[evs] + sreaders)
                    S16_last[(hh_, kc)] = s16t
                    scan_guard = scan_guard[-1:] + [s16t]
                if dbg == "x1s" and first_st and c == 0 and hh_ == 0:
                    dbg_ts.append(S.dma("sp", lambda h: h.dma_start(
                        out=e.dbg_out["dua"][:], in_=e.uadj[:]), after=[yt]))
                    dbg_ts.append(S.dma("sp", lambda h: h.dma_start(
                        out=e.dbg_out["dkh"][:], in_=e.khat[:]), after=tr_ts))
                    dbg_ts.append(S.dma("sp", lambda h: h.dma_start(
                        out=e.dbg_out["dvb"][:], in_=e.vb16[:]), after=tr_ts))
                    dbg_ts.append(S.dma("sp", lambda h: h.dma_start(
                        out=e.dbg_out["dkraw"][:].rearrange("p (a t) -> p a t", a=2), in_=e.k16[:, 0:2, 0:128]), after=kc_ts))
                    scan_guard = scan_guard + dbg_ts[-4:]

            pt_ts = []
            for pth, (srcbuf, soff, deps) in enumerate(
                    ((e.ps16, 0, fir_ts["s"]), (e.pl16, 0, fir_ts["l"]), (e.v16, 64, vc_ts))):
                for cc in range(8):
                    bankb = pbb[cc % 2]
                    tt = S.add("pe", lambda h, srcbuf=srcbuf, soff=soff, cc=cc, c=c, bap=bankb.ap: h.transpose(
                        bap[:, 0:128], srcbuf[:, cc, soff + c * 128:soff + (c + 1) * 128], e.ident[:]),
                        after=list(deps) + bankb.war() + cl[:1])
                    bankb.writer = tt
                    pt_ts.append(rd(S, "act", bankb, lambda h, pth=pth, cc=cc, bap=bankb.ap: h.copy(
                        out=e.pt16[:, pth, cc * 128:(cc + 1) * 128], in_=bap[:, 0:128])))

            for pth in range(4):
                if pth == 3:
                    src = e.pt16[:, 2, :]
                    sdeps = pt_ts[16:24]
                elif pth == 2:
                    src = e.delta16[:]
                    sdeps = [od]
                else:
                    src = e.pt16[:, pth, :]
                    sdeps = pt_ts[pth * 8:(pth + 1) * 8]
                st_ts = []
                for hh_ in range(4):
                    sl_ = src[:, hh_ * 256:(hh_ + 1) * 256]
                    r1_ = S.add("dve", lambda h, sl_=sl_, hh_=hh_: h.tensor_reduce(
                        out=e.stm[:, 4 * hh_:4 * hh_ + 1], in_=sl_, axis=AX.X, op=ALU.add), after=sdeps + stm_guard)
                    r2_ = S.add("act", lambda h, sl_=sl_, hh_=hh_: h.activation(
                        out=e.sqs[:], in_=sl_, func=AF.Square, accum_out=e.stm[:, 4 * hh_ + 1:4 * hh_ + 2]), after=sdeps + sqs_guard + stm_guard)
                    sqs_guard = [r2_]
                    r3_ = S.add("act", lambda h, sl_=sl_, hh_=hh_: h.activation(
                        out=e.sqs[:], in_=sl_, func=AF.Abs, accum_out=e.stm[:, 4 * hh_ + 2:4 * hh_ + 3]), after=sdeps + [r2_])
                    sqs_guard = [r3_]
                    st_ts += [r1_, r2_, r3_]
                f1 = S.add("dve", lambda h: h.tensor_scalar_mul(
                    out=e.stm[:].rearrange("p (g f) -> p f g", f=4)[:, 0, :],
                    in0=e.stm[:].rearrange("p (g f) -> p f g", f=4)[:, 0, :], scalar1=1.0 / 256), after=st_ts)
                f2 = S.add("act", lambda h: h.activation(
                    out=e.stm[:].rearrange("p (g f) -> p f g", f=4)[:, 3, :],
                    in_=e.stm[:].rearrange("p (g f) -> p f g", f=4)[:, 1, :], func=AF.Sqrt), after=st_ts)
                f3 = S.add("act", lambda h: h.activation(
                    out=e.sqs[:, 0:4], in_=e.stm[:].rearrange("p (g f) -> p f g", f=4)[:, 0, :],
                    func=AF.Square), after=[f1] + sqs_guard)
                sqs_guard = [f3]
                f4 = S.add("dve", lambda h: h.scalar_tensor_tensor(
                    out=e.stm[:].rearrange("p (g f) -> p f g", f=4)[:, 1, :],
                    in0=e.stm[:].rearrange("p (g f) -> p f g", f=4)[:, 1, :], scalar=1.0 / 256,
                    in1=e.sqs[:, 0:4], op0=ALU.mult, op1=ALU.subtract), after=[f2, f3])
                f5 = S.add("dve", lambda h: h.tensor_scalar_mul(
                    out=e.stm[:].rearrange("p (g f) -> p f g", f=4)[:, 2, :],
                    in0=e.stm[:].rearrange("p (g f) -> p f g", f=4)[:, 2, :], scalar1=1.0 / 256), after=st_ts)
                f6 = S.add("dve", lambda h: h.tensor_copy(out=e.stm16[:], in_=e.stm[:]), after=[f1, f2, f4, f5])
                sqs_guard = [f3, f4]
                stm_guard = [f6]
                bankb = pbb[pth % 2]
                tt = S.add("pe", lambda h, bap=bankb.ap: h.transpose(
                    bap[0:16, 0:128], e.stm16[:], e.ident[:]), after=[f6] + bankb.war() + cl[:1])
                bankb.writer = tt
                wstat_ts.append(rd(S, "act", bankb, lambda h, pth=pth, bap=bankb.ap: h.copy(
                    out=e.statrow[pth][:], in_=bap[0:16, 0:128]), wstat_rd))

            wstat_rd = wstat_rd[-8:]
            g_ts = []
            for dc in range(8):
                bank = pb[dc % 2]
                _, dep = use_w(1)
                pairs = [(e.wslot[1][:, kc, dc * 128:(dc + 1) * 128], e.xhat[:, kc, c * 128:(c + 1) * 128], [l1w, xh_t[kc]]) for kc in range(8)]
                pairs += [(e.w1tail[g][:, dc * 128:(dc + 1) * 128], e.statrow[g][:], [wstat_ts[-4 + g]] + boot) for g in range(4)]
                mt = pe_acc_sl(S, bank, bank.ap[:, 0:128], pairs)
                dep(mt)
                wstat_rd.append(mt)
                g_ts.append(rd(S, "act", bank, lambda h, dc=dc, c=c, bap=bank.ap: h.activation(
                    out=e.raw[:, dc, c * 128:(c + 1) * 128], in_=bap[:, 0:128], func=AF.Gelu,
                    bias=e.b1sb[:, dc:dc + 1])))
            h1_ts += g_ts
            pairs = [(e.raw[:, kc, c * 128:(c + 1) * 128], w2g16[:, kc, :], [g_ts[kc]]) for kc in range(8)]
            mt = pe_acc_sl(S, pb[2], pb[2].ap[:, 0:16], pairs)
            lg1 = rd(S, "dve", pb[2], lambda h, bap=pb[2].ap: h.tensor_add(out=e.lgt[:], in0=bap[:, 0:16], in1=e.b2gr[:]), boot)
            lg2 = S.add("dve", lambda h: h.tensor_mul(out=e.lgt[:], in0=e.lgt[:], in1=e.invtau_rep[:]), after=[lg1])
            lg3 = S.add("dve", lambda h: h.tensor_reduce(out=e.ss4[:], in_=e.lgt[:].rearrange("p (g f) -> p g f", f=4), axis=AX.X, op=ALU.max), after=[lg2])
            lg4 = S.add("dve", lambda h: h.tensor_tensor(
                out=e.lgt[:].rearrange("p (g f) -> p g f", f=4), in0=e.lgt[:].rearrange("p (g f) -> p g f", f=4),
                in1=bc(e.ss4[:].unsqueeze(2), (128, 4, 4)), op=ALU.subtract), after=[lg3])
            lg5 = S.add("act", lambda h: h.activation(out=e.lgt[:], in_=e.lgt[:], func=AF.Exp), after=[lg4])
            lg6 = S.add("dve", lambda h: h.tensor_reduce(out=e.ss4[:], in_=e.lgt[:].rearrange("p (g f) -> p g f", f=4), axis=AX.X, op=ALU.add), after=[lg5])
            lg7 = S.add("dve", lambda h: h.reciprocal(out=e.ss4[:], in_=e.ss4[:]), after=[lg6])
            lg8 = S.add("dve", lambda h: h.tensor_tensor(
                out=e.lgt[:].rearrange("p (g f) -> p g f", f=4), in0=e.lgt[:].rearrange("p (g f) -> p g f", f=4),
                in1=bc(e.ss4[:].unsqueeze(2), (128, 4, 4)), op=ALU.mult), after=[lg7])
            lg9 = S.add("dve", lambda h: h.tensor_max(out=e.lgt[:], in0=e.lgt[:], in1=e.nf_rep[:]), after=[lg8])
            lg10 = S.add("dve", lambda h: h.tensor_reduce(out=e.ss4[:], in_=e.lgt[:].rearrange("p (g f) -> p g f", f=4), axis=AX.X, op=ALU.add), after=[lg9])
            lg11 = S.add("dve", lambda h: h.tensor_scalar_add(out=e.ss4[:], in0=e.ss4[:], scalar1=1e-8), after=[lg10])
            lg12 = S.add("dve", lambda h: h.reciprocal(out=e.ss4[:], in_=e.ss4[:]), after=[lg11])
            lg13 = S.add("dve", lambda h, c=c: h.tensor_tensor(
                out=e.wts[:, c, :].rearrange("p (g f) -> p g f", f=4), in0=e.lgt[:].rearrange("p (g f) -> p g f", f=4),
                in1=bc(e.ss4[:].unsqueeze(2), (128, 4, 4)), op=ALU.mult), after=[lg12])

            bl_ts = []
            for hh_ in range(4):
                hsl = slice(hh_ * 256, (hh_ + 1) * 256)
                t1_ = S.add("dve", lambda h, hh_=hh_, c=c: h.tensor_scalar_mul(
                    out=e.o16[:, hh_ * 256:(hh_ + 1) * 256], in0=e.pt16[:, 0, hh_ * 256:(hh_ + 1) * 256],
                    scalar1=e.wts[:, c, 4 * hh_:4 * hh_ + 1]), after=[lg13] + pt_ts[0:8])
                for pi, (srcap,) in enumerate(((e.pt16[:, 1, hsl],), (e.delta16[:, hsl],), (e.pt16[:, 2, hsl],))):
                    deps = [t1_] + (pt_ts[8:16] if pi == 0 else ([od] if pi == 1 else pt_ts[16:24]))
                    t1_ = S.add("dve", lambda h, srcap=srcap, hh_=hh_, c=c, pi=pi: h.scalar_tensor_tensor(
                        out=e.o16[:, hh_ * 256:(hh_ + 1) * 256], in0=srcap,
                        scalar=e.wts[:, c, 4 * hh_ + 1 + pi:4 * hh_ + 2 + pi],
                        in1=e.o16[:, hh_ * 256:(hh_ + 1) * 256], op0=ALU.mult, op1=ALU.add), after=deps)
                r_ = S.add("act", lambda h, hh_=hh_: h.activation(
                    out=e.sqs[:], in_=e.o16[:, hh_ * 256:(hh_ + 1) * 256], func=AF.Square,
                    accum_out=e.ss4[:, hh_:hh_ + 1]), after=[t1_] + sqs_guard)
                sqs_guard = [r_]
                bl_ts.append((t1_, r_))
            rm1 = S.add("act", lambda h: h.activation(out=e.rstd4[:], in_=e.ss4[:], func=AF.Sqrt, scale=1.0 / 256, bias=1e-5), after=[r for _, r in bl_ts] + [lg13])
            rm2 = S.add("dve", lambda h: h.reciprocal(out=e.rstd4[:], in_=e.rstd4[:]), after=[rm1])
            on_ts = []
            for hh_ in range(4):
                n1 = S.add("dve", lambda h, hh_=hh_: h.tensor_scalar_mul(
                    out=e.o16[:, hh_ * 256:(hh_ + 1) * 256], in0=e.o16[:, hh_ * 256:(hh_ + 1) * 256],
                    scalar1=e.rstd4[:, hh_:hh_ + 1]), after=[rm2, bl_ts[hh_][0]])
                on_ts.append(n1)
            onm = S.add("dve", lambda h: h.tensor_mul(out=e.o16[:], in0=e.o16[:], in1=e.onw16[:]), after=on_ts + boot)
            for cc in range(8):
                bankb = pbb[cc % 2]
                tt = S.add("pe", lambda h, cc=cc, bap=bankb.ap: h.transpose(
                    bap[:, 0:128], e.o16[:, cc * 128:(cc + 1) * 128], e.ident[:]),
                    after=[onm] + bankb.war() + cl[:1])
                bankb.writer = tt
                oevac_ts.append(rd(S, "act", bankb, lambda h, cc=cc, c=c, bap=bankb.ap: h.copy(
                    out=e.xo16[:, cc, c * 128:(c + 1) * 128], in_=bap[:, 0:128]), x16_rd))

            if dbg == "x1s":
                bar_extra = [S.dma("sp", lambda h, t0=t0, c=c: h.dma_start(
                    out=e.dbg_out["ddel"][t0 + c * 128:t0 + (c + 1) * 128, :],
                    in_=e.delta16[:]), after=od_list)]
                if s == 0 and c == 0:
                    bar_extra.append(S.dma("sp", lambda h: h.dma_start(
                        out=e.dbg_out["dS0"][:],
                        in_=e.S32[:].rearrange("p a b -> p (a b)")),
                        after=[S16_last[(hx, kx)] for hx in range(4) for kx in (0, 1)]))
                scan_guard = scan_guard + bar_extra

        lwo = load_w(0, e.wT["woT"])
        ao_ts = []
        for dc in range(8):
            bank = pb[dc % 4]
            _, dep = use_w(0)
            pairs = [(e.wslot[0][:, kc, dc * 128:(dc + 1) * 128], e.xo16[:, kc, :], [lwo] + oevac_ts) for kc in range(8)]
            mt = pe_acc(S, bank, pairs)
            dep(mt)
            ao_ts.append(rd(S, "act", bank, lambda h, dc=dc, bap=bank.ap: h.copy(
                out=e.attnout[:, dc, :], in_=bap[:, 0:512])))
        sqa = [S.add("act", lambda h, dc=dc: h.activation(out=e.gacc[:, dc, :], in_=e.attnout[:, dc, :], func=AF.Square), after=[ao_ts[dc]] + fir_ts["l"]) for dc in range(8)]
        sA1 = ones_mm_rows([e.attnout[:, dc, :] for dc in range(8)], pb[0], ao_ts)
        sA2 = ones_mm_rows([e.gacc[:, dc, :] for dc in range(8)], pb[1], sqa)
        la1, la2 = ln_rows(pb[0], pb[1], 1e-5, e.rstdA[:], e.mrsA[:])
        x1_ts = []
        for dc in range(8):
            xr = S.dma("sp", lambda h, dc=dc, t0=t0: h.dma_start(
                out=e.xres[:], in_=e.x_fm.rearrange("(c p) t -> p c t", p=128)[:, dc, t0:t0 + NT]),
                after=x1_ts[-2:])
            w1_ = S.add("dve", lambda h, dc=dc: h.tensor_mul(out=e.x1c[:], in0=e.attnout[:, dc, :], in1=e.rstdA[:]), after=[la1, ao_ts[dc]] + x1_ts[-2:])
            w2_ = S.add("dve", lambda h: h.tensor_add(out=e.x1c[:], in0=e.x1c[:], in1=e.mrsA[:]), after=[la2, w1_])
            w3_ = S.add("dve", lambda h: h.tensor_add(out=e.x1c[:], in0=e.x1c[:], in1=e.xres[:]), after=[w2_, xr])
            st_ = S.dma("sp", lambda h, dc=dc, t0=t0: h.dma_start(
                out=e.agin[dc * 128:(dc + 1) * 128, t0:t0 + NT], in_=e.x1c[:]), after=[w3_])
            if dbg in ("x1", "x1s"):
                st2_ = S.dma("sp", lambda h, dc=dc, t0=t0: h.dma_start(
                    out=e.dbg_out["dx1"][dc * 128:(dc + 1) * 128, t0:t0 + NT], in_=e.x1c[:]), after=[w3_])
                x1_ts.append(st2_)
            x1_ts.append(st_)

        if dbg == "qkv":
            for nm, buf, off in (("dq", e.q16, 0), ("dk", e.k16, 0), ("dv", e.v16, 64)):
                dbg_ts.append(S.dma("sp", lambda h, nm=nm, buf=buf, off=off, t0=t0: h.dma_start(
                    out=e.dbg_out[nm].rearrange("(c p) t -> p c t", p=128)[:, :, t0:t0 + NT],
                    in_=buf[:, :, off:off + 512]), after=qc_ts + kc_ts + vc_ts))
        if dbg == "x1s":
            dbg_ts.append(S.dma("sp", lambda h, t0=t0: h.dma_start(
                out=e.dbg_out["dps"].rearrange("(c p) t -> p c t", p=128)[:, :, t0:t0 + NT],
                in_=e.ps16[:]), after=fir_ts["s"]))
            dbg_ts.append(S.dma("sp", lambda h, t0=t0: h.dma_start(
                out=e.dbg_out["dpl"].rearrange("(c p) t -> p c t", p=128)[:, :, t0:t0 + NT],
                in_=e.pl16[:]), after=fir_ts["l"]))
            dbg_ts.append(S.dma("sp", lambda h, t0=t0: h.dma_start(
                out=e.dbg_out["dwt"][t0:t0 + NT, :].rearrange("(n p) f -> p n f", p=128),
                in_=e.wts[:]), after=[x1_ts[0]]))

        x1_all += [t for t in x1_ts]
        drain_boot(24)
        bar = S.barrier(extra=x1_ts[-1:])
        bar_t = bar[0]
    drain_boot(len(bootq))
    pbar = S.barrier()
    ag = S.add("pool", lambda h: h.collective_compute(
        "AllGather", ALU.bypass, replica_groups=[list(range(NCORE))],
        ins=[e.agin.opt()], outs=[e.agout.opt()]),
        after=x1_all + boot_state["done"] + pbar)
    e.ctx1.close()
    if dbg is None or dbg == "p2":
        _phase2(e, S, ctx, ag, boot_state["done"], dbg)
    return nc, S, ctx, e, dbg_ts


def finish(res):
    nc, S, ctx, e, dbg_ts = res
    with nc.Block() as block:
        S.emit(nc, block, ctx)
    return nc


def host_maps(inputs):
    f32 = np.float32
    x = np.asarray(inputs["L0_attn_hidden_states"], f32)
    g = lambda n: np.asarray(inputs[n], f32)
    shared = {
        "wqT": np.ascontiguousarray(g("L0_attn_Wq").T),
        "wkT": np.ascontiguousarray(g("L0_attn_Wk").T),
        "wvT": np.ascontiguousarray(g("L0_attn_Wv").T),
        "woT": np.ascontiguousarray(g("L0_attn_Wo").T),
        "w1T": np.ascontiguousarray(g("L0_attn_gate_W1").T),
        "wbT": np.ascontiguousarray(g("L0_attn_Wb").T),
        "w2gT": np.ascontiguousarray(g("L0_attn_gate_W2").T),
        "convq": np.ascontiguousarray(g("L0_attn_conv_q_w")),
        "convk": np.ascontiguousarray(g("L0_attn_conv_k_w")),
        "convv": np.ascontiguousarray(g("L0_attn_conv_v_w")),
        "firS": np.ascontiguousarray(g("L0_attn_fir_short_w").reshape(1024, 5)),
        "firL": np.ascontiguousarray(g("L0_attn_fir_long_w").reshape(1024, 63)),
        "b1": np.ascontiguousarray(g("L0_attn_gate_b1").reshape(8, 128).T),
        "b2g_rep": np.ascontiguousarray(np.broadcast_to(g("L0_attn_gate_b2"), (128, 16))),
        "lt_row": np.ascontiguousarray(g("L0_attn_log_temp").reshape(1, 16)),
        "fp_row": np.ascontiguousarray(g("L0_attn_floor_param").reshape(1, 16)),
        "onw": np.ascontiguousarray(np.broadcast_to(np.tile(g("L0_attn_o_norm_w"), 4), (128, 1024))),
        "identc": np.eye(128, dtype=NBF),
        "ones1c": np.ones((1, 128), NBF),
        "onescolc": np.ones((128, 1), NBF),
        "ind1c": np.repeat(np.eye(4, dtype=NBF), 256, axis=0),
        "mtriu1n": (-np.triu(np.ones((128, 128), f32), 1)).astype(NBF),
        "mtril1n": (-np.tril(np.ones((128, 128), f32), -1)).astype(NBF),
        "mtriu0": np.triu(np.ones((128, 128), f32), 0).astype(NBF),
        "fconsts": np.ascontiguousarray(np.broadcast_to(np.array(FCONST_VALS, f32), (128, 8))),
    }
    f = lambda n: np.asarray(inputs[n], f32)
    oh = np.zeros((8, 8, 128), NBF)
    for ee in range(8):
        oh[ee, ee, :] = 1
    gidx = np.arange(128) // 8
    shared.update({
        "inpT": np.ascontiguousarray(f("L0_ffn_in_proj_w").T),
        "inpB": np.ascontiguousarray(f("L0_ffn_in_proj_b").reshape(24, 128).T),
        "outwT": np.ascontiguousarray(f("L0_ffn_out_w").T),
        "outB": np.ascontiguousarray(f("L0_ffn_out_b").reshape(8, 128).T),
        "ln1g": np.ascontiguousarray(f("L0_ffn_ln1_g").reshape(8, 128).T),
        "ln1b": np.ascontiguousarray(f("L0_ffn_ln1_b").reshape(8, 128).T),
        "ln2g": np.ascontiguousarray(f("L0_ffn_ln2_g").reshape(8, 128).T),
        "ln2b": np.ascontiguousarray(f("L0_ffn_ln2_b").reshape(8, 128).T),
        "rwT": np.ascontiguousarray(f("L0_ffn_router_w").T),
        "rbrep": np.ascontiguousarray(np.broadcast_to(f("L0_ffn_router_b"), (128, 8))),
        "w1pT": np.ascontiguousarray(f("L0_ffn_w1").transpose(0, 2, 1)),
        "w2pT": np.ascontiguousarray(f("L0_ffn_w2").transpose(0, 2, 1)),
        "b1p2": np.ascontiguousarray(f("L0_ffn_b1").reshape(8, 8, 128).transpose(2, 0, 1)),
        "b2p2": np.ascontiguousarray(f("L0_ffn_b2")),
        "blockm": (gidx[:, None] == gidx[None, :]).astype(NBF),
        "repm": np.ascontiguousarray(np.eye(8, dtype=NBF)[:, np.tile(np.arange(8), 16)].reshape(8, 128)) if False else np.ascontiguousarray((np.arange(8)[:, None] == (np.arange(128) % 8)[None, :]).astype(NBF)),
        "onehots": oh,
        "idxc": np.ascontiguousarray(np.broadcast_to(np.arange(8, dtype=f32), (128, 8))),
    })
    maps = []
    for b in range(8):
        m = dict(shared)
        m["x_fm"] = np.ascontiguousarray(x[b].T)
        m["colbase"] = np.array([[b]], np.int32)
        maps.append(m)
    return maps


def kernel(**inputs):
    maps = host_maps(inputs)
    nc = finish(build(dbg=None))
    res = run_bass_kernel_spmd(nc, maps, core_ids=list(range(NCORE)))
    out = np.empty((8, 4096, 1024), np.float32)
    for c in range(NCORE):
        a = res.results[c]["out_fm"].reshape(1024, 512, 8)
        out[:, 512 * c:512 * (c + 1), :] = a.transpose(2, 1, 0)
    return out


def _phase2(e, S, ctx, ag, scr_done, dbg):
    nc = e.nc
    pb, pbb = e.pb, e.pbb
    sb = lambda n, sh, dt: ctx.enter_context(nc.sbuf_tensor(n, sh, dt))

    blockm = sb("blockm_s", [128, 128], BF16)
    repm = sb("repm_s", [8, 128], BF16)
    onehots = sb("onehots_s", [8, 8, 128], BF16)
    idxc = sb("idxc_s", [128, 8], F32)
    inpb_s = sb("inpb_s", [128, 24], F32)
    outb_s = sb("outb_s", [128, 8], F32)
    g1_s = sb("g1_s", [128, 8], F32)
    bb1_s = sb("bb1_s", [128, 8], F32)
    g2_s = sb("g2_s", [128, 8], F32)
    bb2_s = sb("bb2_s", [128, 8], F32)
    rw16 = sb("rw16", [128, 8, 8], BF16)
    rb_s = sb("rb_s", [128, 8], F32)
    b1p_s = sb("b1p_s", [128, 8, 8], F32)
    b2_16 = sb("b2_16", [8, 1024], BF16)

    ld2 = []

    def ld(dst, src, eng="sp"):
        t = S.dma(eng, lambda h, d=dst, s=src: h.dma_start(out=d, in_=s), after=[ag])
        ld2.append(t)
        return t
    ld(blockm[:], e.blockmD)
    ld(repm[:], e.repmD)
    ld(onehots[:], e.onehotsD)
    ld(idxc[:], e.idxcD)
    ld(inpb_s[:], e.inpB)
    ld(outb_s[:], e.outB)
    ld(g1_s[:], e.ln1g)
    ld(bb1_s[:], e.ln1b)
    ld(g2_s[:], e.ln2g)
    ld(bb2_s[:], e.ln2b)
    ld(rw16[:], e.rwT.rearrange("(c p) f -> p c f", p=128), eng="pool")
    ld(rb_s[:], e.rbrep)
    ld(b1p_s[:], e.b1p2)
    ld(b2_16[:], e.b2p2, eng="pool")
    colb_sb = e.colb
    xsl = S.dma("sp", lambda h: h.dma_start(
        out=e.xloc[:, :],
        in_=e.agout[:, bass.ts(h.partition_id(), 512)]), after=[ag])
    ld2.append(xsl)

    xg = sb("xg", [128, 8, 64], F32)
    x1t16 = sb("x1t16", [128, 8, 512], BF16)
    xf16 = sb("xf16", [128, 8, 512], BF16)
    qk16 = sb("qk16", [128, 16, 512], BF16)
    vtmp = sb("vtmp", [128, 512], BF16)
    big8 = sb("big8", [128, 4096], BF16)
    vtm16 = big8[:].rearrange("p (a b) -> p a b", a=4)
    ctx16 = sb("ctx16", [128, 8, 512], BF16)
    t16s = sb("t16s", [128, 512], BF16)

    hsum16 = x1t16
    hh16 = sb("hh16", [128, 8, 512], BF16)
    he16 = sb("he16", [128, 8, 4, 512], BF16)
    w1buf = [sb(f"w1b{i}", [128, 8, 512], BF16) for i in range(2)]
    w2buf = [sb(f"w2b{i}", [128, 8, 4, 128], BF16) for i in range(2)]
    r2acc = sb("r2acc", [128, 8, 512], F32)
    iobuf = [sb(f"iob{i}", [128, 8, 128], BF16) for i in range(2)]
    mrep16 = sb("mrep16", [128, 8, 512], BF16)
    maskrows = sb("maskrows", [8, 512], BF16)
    sqc = t16s
    scoret = sb("scoret", [128, 16, 8], F32)
    attn16 = sb("attn16", [128, 16, 8], BF16)
    msb2 = sb("msb2", [128, 128], BF16)
    atT = sb("atT", [8, 128], BF16)
    asb = sb("asb", [128, 128], BF16)
    logit2 = sb("logit2", [128, 8], F32)
    m2s = sb("m2s", [128, 1], F32)
    i1s = sb("i1s", [128, 1], F32)
    msk2 = sb("msk2", [128, 8], F32)
    mask216 = sb("mask216", [128, 8], BF16)
    r2_16 = big8[:].rearrange("p (c t) -> p c t", c=8)
    fo16 = qk16[:, 0:8, :]
    x2c = sb("x2c", [128, 512], F32)
    rows2 = sb("rows2", [1, 1024], F32)
    rb1_2 = sb("rb1_2", [1, 512], BF16)
    rb2_2 = sb("rb2_2", [1, 512], BF16)
    rstd_sb = sb("rstd_sb2", [128, 512], BF16)
    mrs_sb = sb("mrs_sb2", [128, 512], BF16)
    mx16 = sb("mx16", [128, 16], F32)
    tmp8 = sb("tmp8", [128, 8], F32)
    tf32 = sb("tf32", [128, 512], F32)

    r_t1 = rows2[0:1, 0:512]
    r_t2 = rows2[0:1, 512:1024]

    def ln_rows2(bank1, bank2, outb1, outb2, extra=()):
        t1 = rd(S, "dve", bank1, lambda h: h.tensor_scalar_mul(out=r_t1, in0=bank1.ap[0:1, :], scalar1=1.0 / 1024), list(extra))
        t2 = S.add("act", lambda h: h.activation(out=r_t2, in_=r_t1, func=AF.Square), after=[t1])
        t3 = rd(S, "dve", bank2, lambda h: h.scalar_tensor_tensor(out=r_t2, in0=bank2.ap[0:1, :], scalar=1.0 / 1024, in1=r_t2, op0=ALU.mult, op1=ALU.subtract), [t2])
        t4 = S.add("act", lambda h: h.activation(out=r_t2, in_=r_t2, func=AF.Sqrt, bias=1e-5), after=[t3])
        t5 = S.add("dve", lambda h: h.reciprocal(out=r_t2, in_=r_t2), after=[t4])
        t6 = S.add("dve", lambda h: h.tensor_mul(out=r_t1, in0=r_t1, in1=r_t2), after=[t5])
        t7 = S.add("dve", lambda h: h.tensor_scalar_mul(out=r_t1, in0=r_t1, scalar1=-1.0), after=[t6])
        t8 = S.add("dve", lambda h: h.tensor_copy(out=rb1_2[:], in_=r_t2), after=[t5])
        t9 = S.add("dve", lambda h: h.tensor_copy(out=rb2_2[:], in_=r_t1), after=[t7])
        tb1 = pe_acc(S, pb[4], [(e.ones1[:], rb1_2[:], [t8])])
        o1 = rd(S, "act", pb[4], lambda h, outb1=outb1: h.copy(out=outb1, in_=pb[4].ap[:]))
        tb2 = pe_acc(S, pb[5], [(e.ones1[:], rb2_2[:], [t9])])
        o2 = rd(S, "act", pb[5], lambda h, outb2=outb2: h.copy(out=outb2, in_=pb[5].ap[:]))
        return o1, o2

    sq_state = {"last": None}

    def ln_sums(chunks_ap, chunk_deps):
        s1t = pe_acc_sl(S, pb[0], pb[0].ap[0:1, :],
                        [(e.onescol[:], chunks_ap[dc], chunk_deps[dc]) for dc in range(8)])
        prevsq = sq_state["last"]
        war = pb[1].war()
        if pb[1].writer is not None:
            war.append(pb[1].writer)
        last = None
        for dc in range(8):
            dep = list(chunk_deps[dc]) + ([prevsq] if prevsq else [])
            sq = S.add("act", lambda h, src=chunks_ap[dc]: h.activation(out=sqc[:], in_=src, func=AF.Square), after=dep)
            mm_ = S.add("pe", (lambda dc=dc, st=(dc == 0), sp=(dc == 7): (lambda h: h.matmul(
                pb[1].ap[0:1, :], e.onescol[:], sqc[:], start=st, stop=sp)))(),
                after=[sq] + (war if dc == 0 else []))
            prevsq = mm_
            last = mm_
        pb[1].writer = last
        sq_state["last"] = last
        return s1t, last

    out_all = []
    io_cons = [[], []]
    w1_cons = [[], []]
    w2_cons = [[], []]
    he_rd = {}
    prev_ro = []
    hsum_guard = []
    fo_guard = []
    prev_out = []
    NPOS = 64

    for j in range(8):
        tiledeps = [ag] + ld2
        x1t_ts = []
        for dc in range(8):
            lx = S.dma("sp", lambda h, dc=dc, j=j: h.dma_start(
                out=xg[:],
                in_=e.xloc.rearrange("(u c p) t -> u c p t", c=8, p=128)[:, dc, :, j * 64:(j + 1) * 64]
                    .rearrange("u p t -> p u t")),
                after=tiledeps + prev_ro + hsum_guard)
            ro = S.add("dve", lambda h, dc=dc: h.tensor_copy(
                out=x1t16[:, dc, :].rearrange("p (q u) -> p q u", u=8),
                in_=xg[:].rearrange("p u q -> p q u")), after=[lx])
            prev_ro = [ro]
            x1t_ts.append(ro)
        s1t, s2t = ln_sums([x1t16[:, dc, :] for dc in range(8)], [[x1t_ts[dc]] for dc in range(8)])
        la1, la2 = ln_rows2(pb[0], pb[1], rstd_sb[:], mrs_sb[:])
        xf_ts = []
        for dc in range(8):
            u1 = S.add("dve", lambda h, dc=dc: h.tensor_mul(out=xf16[:, dc, :], in0=x1t16[:, dc, :], in1=rstd_sb[:]), after=[la1, x1t_ts[dc]])
            u2 = S.add("dve", lambda h, dc=dc: h.tensor_add(out=xf16[:, dc, :], in0=xf16[:, dc, :], in1=mrs_sb[:]), after=[la2, u1])
            xf_ts.append(u2)

        vt_ts = []
        qk_ts = [None] * 16
        vtmp_guard = []
        for dc in range(24):
            slot = dc % 2
            lw = S.dma("sp", lambda h, dc=dc, slot=slot: h.dma_start(
                out=iobuf[slot][:], in_=e.scrI[dc]), after=scr_done + io_cons[slot] + [ag])
            io_cons[slot] = []
            bank = pb[dc % 2]
            pairs = [(iobuf[slot][:, kc, :], xf16[:, kc, :], [lw, xf_ts[kc]]) for kc in range(8)]
            mt = pe_acc(S, bank, pairs)
            io_cons[slot].append(mt)
            if dc < 16:
                ev = rd(S, "act", bank, lambda h, dc=dc, bap=bank.ap: h.activation(
                    out=qk16[:, dc, :], in_=bap[:, 0:512], func=AF.Identity, bias=inpb_s[:, dc:dc + 1]),
                    fo_guard if dc < 8 else [])
                qk_ts[dc] = ev
            else:
                ev = rd(S, "act", bank, lambda h, dc=dc, bap=bank.ap: h.activation(
                    out=vtmp[:], in_=bap[:, 0:512], func=AF.Identity, bias=inpb_s[:, dc:dc + 1]),
                    vtmp_guard)
                tloc = []
                for sub in range(4):
                    bankb = pbb[sub % 2]
                    tt = S.add("pe", lambda h, dc=dc, sub=sub, bap=bankb.ap: h.transpose(
                        bap[:, 0:128], vtmp[:, sub * 128:(sub + 1) * 128], e.ident[:]),
                        after=[ev] + bankb.war())
                    bankb.writer = tt
                    ev2 = rd(S, "act", bankb, lambda h, dc=dc, sub=sub, bap=bankb.ap: h.copy(
                        out=vtm16[:, sub, (dc - 16) * 128:(dc - 15) * 128], in_=bap[:, 0:128]))
                    tloc.append(ev2)
                    vt_ts.append(ev2)
                vtmp_guard = tloc

        att_guard = []
        ctxev = [[None] * 4 for _ in range(8)]
        for sub in range(4):
            scols = slice(sub * 128, (sub + 1) * 128)
            red_ts = []
            for hh_ in range(16):
                dc = hh_ // 2
                po = 64 * (hh_ % 2)
                war = pb[2].war()
                if pb[2].writer is not None:
                    war.append(pb[2].writer)
                gm = S.add("pe", (lambda dc=dc, po=po, scols=scols: (lambda h: h.matmul(
                    pb[2].ap[:, 0:128], qk16[po:po + 64, 8 + dc, scols],
                    qk16[po:po + 64, dc, scols], start=True, stop=True)))(),
                    after=[qk_ts[8 + dc], qk_ts[dc]] + war)
                pb[2].writer = gm
                me = rd(S, "dve", pb[2], lambda h, bap=pb[2].ap: h.tensor_mul(
                    out=msb2[:], in0=bap[:, 0:128], in1=blockm[:]), ld2[:1] + att_guard + red_ts[-1:])
                bankb = pbb[0]
                tt = S.add("pe", lambda h, bap=bankb.ap: h.transpose(
                    bap[:, 0:128], msb2[:], e.ident[:]), after=[me] + bankb.war())
                bankb.writer = tt
                red = rd(S, "dve", bankb, lambda h, hh_=hh_, bap=bankb.ap: h.tensor_reduce(
                    out=scoret[:, hh_, :], in_=bap[:, 0:128].rearrange("p (g s) -> p s g", s=8),
                    axis=AX.X, op=ALU.add), att_guard)
                red_ts.append(red)
            f1 = S.add("dve", lambda h: h.tensor_reduce(out=mx16[:], in_=scoret[:], axis=AX.X, op=ALU.max), after=red_ts)
            f2 = S.add("dve", lambda h: h.tensor_tensor(
                out=scoret[:], in0=scoret[:], in1=bc(mx16[:].unsqueeze(2), (128, 16, 8)), op=ALU.subtract), after=[f1])
            f3 = S.add("act", lambda h: h.activation(out=scoret[:], in_=scoret[:], func=AF.Exp, scale=0.125), after=[f2])
            f4 = S.add("dve", lambda h: h.tensor_reduce(out=mx16[:], in_=scoret[:], axis=AX.X, op=ALU.add), after=[f3])
            f5 = S.add("dve", lambda h: h.reciprocal(out=mx16[:], in_=mx16[:]), after=[f4])
            f6 = S.add("dve", lambda h: h.tensor_tensor(
                out=attn16[:], in0=scoret[:], in1=bc(mx16[:].unsqueeze(2), (128, 16, 8)), op=ALU.mult), after=[f5])
            att_guard = [f6]
            for hh_ in range(16):
                dc = hh_ // 2
                po = 64 * (hh_ % 2)
                bankb = pbb[1]
                tt = S.add("pe", lambda h, hh_=hh_, bap=bankb.ap: h.transpose(
                    bap[0:8, 0:128], attn16[:, hh_, :], e.ident[:]), after=[f6] + bankb.war())
                bankb.writer = tt
                ea = rd(S, "act", bankb, lambda h, bap=bankb.ap: h.copy(out=atT[:], in_=bap[0:8, 0:128]),
                        att_guard if att_guard else [])
                war3 = pb[3].war()
                if pb[3].writer is not None:
                    war3.append(pb[3].writer)
                rp = S.add("pe", lambda h: h.matmul(pb[3].ap[:, 0:128], repm[:], atT[:],
                                                    start=True, stop=True), after=[ea] + war3)
                pb[3].writer = rp
                am = rd(S, "dve", pb[3], lambda h, bap=pb[3].ap: h.tensor_mul(
                    out=asb[:], in0=bap[:, 0:128], in1=blockm[:]), att_guard)
                bank = pb[4 + (hh_ % 2)]
                war4 = bank.war()
                if bank.writer is not None:
                    war4.append(bank.writer)
                cm = S.add("pe", (lambda hh_=hh_, sub=sub, po=po, bank=bank: (lambda h: h.matmul(
                    bank.ap[po:po + 64, 0:128], vtm16[:, sub, hh_ * 64:(hh_ + 1) * 64],
                    asb[:], start=True, stop=True)))(),
                    after=[am] + war4 + vt_ts)
                bank.writer = cm
                ce = rd(S, "act", bank, lambda h, hh_=hh_, dc=dc, po=po, sub=sub, bank=bank: h.copy(
                    out=ctx16[po:po + 64, dc, sub * 128:(sub + 1) * 128], in_=bank.ap[po:po + 64, 0:128]))
                ctxev[dc][sub] = ce
                att_guard = [ea, am, ce]

        hs_ts = []
        for dc in range(8):
            slot = dc % 2
            lw = S.dma("sp", lambda h, dc=dc, slot=slot: h.dma_start(
                out=iobuf[slot][:], in_=e.scrO[dc]), after=scr_done + io_cons[slot] + [ag])
            io_cons[slot] = []
            bank = pb[dc % 2]
            pairs = [(iobuf[slot][:, kc, :], ctx16[:, kc, :], [lw] + [ctxev[kc][su] for su in range(4)]) for kc in range(8)]
            mt = pe_acc(S, bank, pairs)
            io_cons[slot].append(mt)
            t1 = rd(S, "act", bank, lambda h, dc=dc, bap=bank.ap: h.activation(
                out=t16s[:], in_=bap[:, 0:512], func=AF.Identity, bias=outb_s[:, dc:dc + 1]),
                hs_ts[-1:])
            t2 = S.add("dve", lambda h, dc=dc: h.tensor_add(out=hsum16[:, dc, :], in0=t16s[:], in1=xf16[:, dc, :]), after=[t1, xf_ts[dc]])
            hs_ts.append(t2)
        s1t, s2t = ln_sums([hsum16[:, dc, :] for dc in range(8)], [hs_ts for dc in range(8)])
        lb1, lb2 = ln_rows2(pb[0], pb[1], rstd_sb[:], mrs_sb[:])
        hsum_guard = [lb1]
        hh_ts = []
        for dc in range(8):
            u1 = S.add("dve", lambda h, dc=dc: h.tensor_mul(out=hh16[:, dc, :], in0=hsum16[:, dc, :], in1=rstd_sb[:]), after=[lb1, hs_ts[dc]])
            u2 = S.add("dve", lambda h, dc=dc: h.tensor_add(out=hh16[:, dc, :], in0=hh16[:, dc, :], in1=mrs_sb[:]), after=[lb2, u1])
            u3 = S.add("dve", lambda h, dc=dc: h.tensor_scalar(out=hh16[:, dc, :], in0=hh16[:, dc, :],
                scalar1=g1_s[:, dc:dc + 1], scalar2=bb1_s[:, dc:dc + 1], op0=ALU.mult, op1=ALU.add), after=[u2])
            hh_ts.append(u3)

        mk_ts = []
        asg_guard = []
        for sub in range(4):
            scols = slice(sub * 128, (sub + 1) * 128)
            bank = pb[2]
            pairs = [(hh16[:, kc, scols], rw16[:, kc, :], [hh_ts[kc]]) for kc in range(8)]
            mt = pe_acc_sl(S, bank, bank.ap[:, 0:8], pairs)
            g1 = rd(S, "dve", bank, lambda h, bap=bank.ap: h.tensor_add(out=logit2[:], in0=bap[:, 0:8], in1=rb_s[:]), ld2[:1] + asg_guard)
            g2 = S.add("dve", lambda h: h.tensor_reduce(out=m2s[:], in_=logit2[:], axis=AX.X, op=ALU.max), after=[g1])
            g3 = S.add("dve", lambda h: h.tensor_tensor(out=msk2[:], in0=logit2[:], in1=bc(m2s[:], (128, 8)), op=ALU.is_ge), after=[g2])
            g4 = S.add("dve", lambda h: h.tensor_mul(out=tmp8[:], in0=msk2[:], in1=idxc[:]), after=[g3] + ld2[3:4])
            g5 = S.add("dve", lambda h: h.tensor_reduce(out=i1s[:], in_=tmp8[:], axis=AX.X, op=ALU.max), after=[g4])
            g6 = S.add("dve", lambda h: h.scalar_tensor_tensor(out=tmp8[:], in0=msk2[:], scalar=-1e30, in1=logit2[:], op0=ALU.mult, op1=ALU.add), after=[g3, g5])
            g7 = S.add("dve", lambda h: h.tensor_reduce(out=m2s[:], in_=tmp8[:], axis=AX.X, op=ALU.max), after=[g6])
            g8 = S.add("dve", lambda h: h.tensor_tensor(out=msk2[:], in0=tmp8[:], in1=bc(m2s[:], (128, 8)), op=ALU.is_ge), after=[g7])
            g9 = S.add("dve", lambda h: h.tensor_mul(out=tmp8[:], in0=msk2[:], in1=idxc[:]), after=[g8])
            g10 = S.add("dve", lambda h: h.tensor_reduce(out=m2s[:], in_=tmp8[:], axis=AX.X, op=ALU.max), after=[g9])
            g11 = S.add("dve", lambda h: h.tensor_max(out=i1s[:], in0=i1s[:], in1=m2s[:]), after=[g10, g5])
            g12 = S.add("dve", lambda h: h.tensor_tensor(out=msk2[:], in0=bc(i1s[:], (128, 8)), in1=idxc[:], op=ALU.is_equal), after=[g11])
            g13 = S.add("dve", lambda h: h.tensor_copy(out=mask216[:], in_=msk2[:]), after=[g12])
            bankb = pbb[1]
            tt = S.add("pe", lambda h, bap=bankb.ap: h.transpose(
                bap[0:8, 0:128], mask216[:], e.ident[:]), after=[g13] + bankb.war())
            bankb.writer = tt
            mv = rd(S, "act", bankb, lambda h, sub=sub, bap=bankb.ap: h.copy(
                out=maskrows[:, sub * 128:(sub + 1) * 128], in_=bap[0:8, 0:128]))
            mk_ts.append(mv)
            asg_guard = [g13, mv]
        mrep_ts = []
        for ee in range(8):
            bank = pb[3]
            war = bank.war()
            if bank.writer is not None:
                war.append(bank.writer)
            mm_ = S.add("pe", (lambda ee=ee: (lambda h: h.matmul(
                pb[3].ap[:, 0:512], onehots[:, ee, :], maskrows[:], start=True, stop=True)))(),
                after=mk_ts + war + ld2[2:3])
            bank.writer = mm_
            mrep_ts.append(rd(S, "act", bank, lambda h, ee=ee, bap=bank.ap: h.copy(
                out=mrep16[:, ee, :], in_=bap[:, 0:512])))

        he_ts = {}
        r2_ts = []
        r2a_ts = {}
        for df in range(2):
            for ee in range(8):
                slot = ee % 2
                lw1 = S.dma("sp", lambda h, ee=ee, df=df, slot=slot: h.dma_start(
                    out=w1buf[slot][:], in_=e.scr1[ee][:, :, df * 512:(df + 1) * 512]),
                    after=scr_done + w1_cons[slot] + [ag])
                w1_cons[slot] = []
                for dci in range(4):
                    dc = df * 4 + dci
                    bank = pb[dci % 2]
                    pairs = [(w1buf[slot][:, kc, dci * 128:(dci + 1) * 128], hh16[:, kc, :], [lw1, hh_ts[kc]]) for kc in range(8)]
                    mt = pe_acc_sl(S, bank, bank.ap[:, 0:512], pairs)
                    w1_cons[slot].append(mt)
                    ev = rd(S, "act", bank, lambda h, ee=ee, dci=dci, dc=dc, bap=bank.ap: h.activation(
                        out=he16[:, ee, dci, :], in_=bap[:, 0:512], func=AF.Gelu,
                        bias=b1p_s[:, ee, dc:dc + 1]), he_rd.pop((ee, dci), []))
                    mk = S.add("dve", lambda h, ee=ee, dci=dci: h.tensor_mul(
                        out=he16[:, ee, dci, :], in0=he16[:, ee, dci, :],
                        in1=mrep16[:, ee, :]),
                        after=[ev, mrep_ts[ee]])
                    he_ts[(ee, dci)] = [mk]
            for dc in range(8):
                slot2 = dc % 2
                lw2 = []
                for ee in range(8):
                    lw2.append(S.dma("sp", lambda h, ee=ee, dc=dc, df=df, slot2=slot2: h.dma_start(
                        out=w2buf[slot2][:, ee, :, :], in_=e.scr2[ee, dc][:, df * 4:(df + 1) * 4, :]),
                        after=scr_done + w2_cons[slot2] + [ag]))
                w2_cons[slot2] = []
                bank = pb[2 + dc % 2]
                war = bank.war()
                if bank.writer is not None:
                    war.append(bank.writer)
                last = None
                seq = [(ee, kci) for ee in range(8) for kci in range(4)]
                for idx, (ee, kci) in enumerate(seq):
                    first = (idx == 0)
                    closes = (df == 0 and idx == len(seq) - 1)
                    aft = [lw2[ee]] + he_ts[(ee, kci)] + (war if first else [])
                    last = S.add("pe", (lambda ee=ee, kci=kci, first=first, closes=closes, bank=bank, slot2=slot2: (lambda h: h.matmul(
                        bank.ap[:, 0:512], w2buf[slot2][:, ee, kci, :], he16[:, ee, kci, :],
                        start=first, stop=closes)))(), after=aft)
                    w2_cons[slot2].append(last)
                    he_rd.setdefault((ee, kci), []).append(last)
                if df == 0:
                    bank.writer = last
                    r2a_ts[dc] = rd(S, "dve", bank, lambda h, dc=dc, bank=bank: h.tensor_add(
                        out=r2acc[:, dc, :], in0=bank.ap[:, 0:512],
                        in1=hh16[:, dc, :]), [hh_ts[dc]])
                else:
                    last = S.add("pe", (lambda dc=dc, bank=bank: (lambda h: h.matmul(
                        bank.ap[:, 0:512], b2_16[:, dc * 128:(dc + 1) * 128],
                        maskrows[:], start=False, stop=True)))(),
                        after=[last] + mk_ts + ld2[-1:])
                    bank.writer = last
                    r2_ts.append(rd(S, "dve", bank, lambda h, dc=dc, bank=bank: h.tensor_add(
                        out=r2_16[:, dc, :], in0=bank.ap[:, 0:512],
                        in1=r2acc[:, dc, :]), [r2a_ts[dc]]))

        s1t, s2t = ln_sums([r2_16[:, dc, :] for dc in range(8)], [r2_ts for dc in range(8)])
        lc1, lc2 = ln_rows2(pb[0], pb[1], rstd_sb[:], mrs_sb[:])
        fo_ts = []
        for dc in range(8):
            u1 = S.add("dve", lambda h, dc=dc: h.tensor_mul(out=fo16[:, dc, :], in0=r2_16[:, dc, :], in1=rstd_sb[:]), after=[lc1] + r2_ts)
            u2 = S.add("dve", lambda h, dc=dc: h.tensor_add(out=fo16[:, dc, :], in0=fo16[:, dc, :], in1=mrs_sb[:]), after=[lc2, u1])
            u3 = S.add("dve", lambda h, dc=dc: h.tensor_scalar(out=fo16[:, dc, :], in0=fo16[:, dc, :],
                scalar1=g2_s[:, dc:dc + 1], scalar2=bb2_s[:, dc:dc + 1], op0=ALU.mult, op1=ALU.add), after=[u2])
            fo_ts.append(u3)
        s1t, s2t = ln_sums([fo16[:, dc, :] for dc in range(8)], [[fo_ts[dc]] for dc in range(8)])
        ld1, ld2_ = ln_rows2(pb[0], pb[1], rstd_sb[:], mrs_sb[:])
        out_ts = []
        for dc in range(8):
            chain = out_ts[-1:] if out_ts else prev_out
            lx = S.dma("sp", lambda h, dc=dc, j=j: h.dma_start(
                out=xg[:],
                in_=e.xloc.rearrange("(u c p) t -> u c p t", c=8, p=128)[:, dc, :, j * 64:(j + 1) * 64]
                    .rearrange("u p t -> p u t")),
                after=chain + x1t_ts + [fo_ts[dc]] + prev_ro)
            ro = S.add("dve", lambda h, dc=dc: h.tensor_copy(
                out=x2c[:].rearrange("p (q u) -> p q u", u=8),
                in_=xg[:].rearrange("p u q -> p q u")), after=[lx] + chain)
            w1_ = S.add("dve", lambda h, dc=dc: h.tensor_mul(out=tf32[:], in0=fo16[:, dc, :], in1=rstd_sb[:]), after=[ld1, fo_ts[dc]] + chain)
            w2_ = S.add("dve", lambda h: h.tensor_add(out=tf32[:], in0=tf32[:], in1=mrs_sb[:]), after=[ld2_, w1_])
            w3_ = S.add("dve", lambda h: h.tensor_add(out=x2c[:], in0=x2c[:], in1=tf32[:]), after=[w2_, ro])
            st_ = S.dma("sp", lambda h, dc=dc, j=j: h.dma_start(
                out=e.out_fm[dc * 128:(dc + 1) * 128, j * 512:(j + 1) * 512], in_=x2c[:]), after=[w3_])
            out_ts.append(st_)
            prev_ro = [ro]
        out_all += out_ts
        fo_guard = [w3_]
        prev_out = out_ts[-1:]
    S.barrier(extra=out_all)
